# version 18
# baseline (speedup 1.0000x reference)
import contextlib
import numpy as np
import concourse.bass as bass
import concourse.mybir as mybir
from concourse.bass_utils import run_bass_kernel_spmd

F32 = mybir.dt.float32
F32R = mybir.dt.float32r
BF16 = mybir.dt.bfloat16
I32 = mybir.dt.int32
AF = mybir.ActivationFunctionType
ALU = mybir.AluOpType
AX = mybir.AxisListType

L, D, DI, S, RK, HID = 4096, 1024, 2048, 16, 64, 4096
T = 512
NCH = L // T
NTT = T // 128
KD = D // 128
NEB = DI // 128
SG = 2
SH = 4
EPS = 1e-5
CMUL = 1.5 * (2.0 ** 20)
TINY = 1.17549435e-38

ENGS = ("tensor", "vector", "scalar", "gpsimd", "sync")


DMA_CH = ("cio", "cw0", "cw1", "cp")


class Sched:

    def __init__(self):
        self.q = {e: [] for e in ENGS}
        self.cnt = {e: 0 for e in ENGS}
        for ch in DMA_CH:
            self.cnt[ch] = 0

    def _res(self, waits):
        out = []
        for w in waits:
            if w is None:
                continue
            owner, c = w
            if owner in DMA_CH:
                c = self.cnt[owner]
            out.append((owner, c))
        return tuple(out)

    def emit(self, eng, fn, waits=(), inc=False):
        if inc:
            self.cnt[eng] += 1
        self.q[eng].append((self._res(waits), fn, bool(inc)))
        return (eng, self.cnt[eng]) if inc else None

    def dma(self, out, in_, waits=(), ch="cio"):
        rw = self._res(waits)
        self.cnt[ch] += 16

        def fn(e, _):
            return e.dma_start(out=out, in_=in_)

        self.q["sync"].append((rw, fn, ch))
        return (ch, self.cnt[ch])

    def tok(self, eng):
        return (eng, self.cnt[eng])

    def replay(self, eng, handle, sems, dma_sems):
        waited = {}
        for waits, fn, inc in self.q[eng]:
            for owner, c in waits:
                if c <= 0 or waited.get(owner, 0) >= c:
                    continue
                waited[owner] = c
                handle.wait_ge(dma_sems[owner] if owner in DMA_CH
                               else sems[owner], c)
            inst = fn(handle, inc)
            if inc in DMA_CH:
                inst.then_inc(dma_sems[inc], 16)
            elif inc:
                inst.then_inc(sems[eng], 1)


def build_program(debug=None, nch=NCH, pstop=8, psel=None):
    if psel is None:
        psel = set(range(1, pstop + 1))
    nc = bass.Bass("TRN2", target_bir_lowering=False)
    sc = Sched()

    x_h = nc.dram_tensor("x", [L, D], F32, kind="ExternalInput").ap()
    out_h = nc.dram_tensor("out", [L, D], F32, kind="ExternalOutput").ap()
    gammaC_h = nc.dram_tensor("gammaC", [128, KD], F32, kind="ExternalInput").ap()
    betaC_h = nc.dram_tensor("betaC", [128, KD], F32, kind="ExternalInput").ap()
    winT_h = nc.dram_tensor("winT", [D, 2 * DI], F32, kind="ExternalInput").ap()
    cwC_h = nc.dram_tensor("cwC", [128, NEB * 4], F32, kind="ExternalInput").ap()
    cbC_h = nc.dram_tensor("cbC", [128, NEB], F32, kind="ExternalInput").ap()
    xpwT_h = nc.dram_tensor("xpwT", [DI, 96], F32, kind="ExternalInput").ap()
    dtwT_h = nc.dram_tensor("dtwT", [RK, DI], F32, kind="ExternalInput").ap()
    dtbC_h = nc.dram_tensor("dtbC", [128, NEB], F32, kind="ExternalInput").ap()
    alogC_h = nc.dram_tensor("alogC", [128, NEB * S], F32, kind="ExternalInput").ap()
    dC_h = nc.dram_tensor("dC", [128, NEB], F32, kind="ExternalInput").ap()
    woutT_h = nc.dram_tensor("woutT", [DI, D], F32, kind="ExternalInput").ap()
    wfc_h = nc.dram_tensor("wfc", [HID, D], F32, kind="ExternalInput").ap()
    bfc_h = nc.dram_tensor("bfc", [1, HID], F32, kind="ExternalInput").ap()
    wproj_h = nc.dram_tensor("wproj", [D, HID], F32, kind="ExternalInput").ap()
    bproj_h = nc.dram_tensor("bproj", [1, D], F32, kind="ExternalInput").ap()
    idf32_h = nc.dram_tensor("idf32", [128, 128], F32, kind="ExternalInput").ap()
    idbf_h = nc.dram_tensor("idbf", [128, 128], BF16, kind="ExternalInput").ap()
    onesf_h = nc.dram_tensor("onesf", [1, 128], F32, kind="ExternalInput").ap()

    dbg_h = None
    if debug is not None:
        dbg_h = nc.dram_tensor("dbg", [128, 16384], F32,
                               kind="ExternalOutput").ap()

    s_inh = nc.dram_tensor("s_inh", [D, 2 * DI], F32R, kind="Internal").ap()
    s_inl = nc.dram_tensor("s_inl", [D, 2 * DI], F32R, kind="Internal").ap()
    s_outh = nc.dram_tensor("s_outh", [DI, D], F32R, kind="Internal").ap()
    s_outl = nc.dram_tensor("s_outl", [DI, D], F32R, kind="Internal").ap()
    s_fcq = nc.dram_tensor("s_fcq", [D, HID], BF16, kind="Internal").ap()
    s_pjq = nc.dram_tensor("s_pjq", [HID, D], BF16, kind="Internal").ap()
    s_xpwh = nc.dram_tensor("s_xpwh", [128, NEB * 96], F32R, kind="Internal").ap()
    s_xpwl = nc.dram_tensor("s_xpwl", [128, NEB * 96], F32R, kind="Internal").ap()
    s_bfcB = nc.dram_tensor("s_bfcB", [128, HID], F32, kind="Internal").ap()
    s_bpjB = nc.dram_tensor("s_bpjB", [128, D], F32, kind="Internal").ap()
    s_bm = nc.dram_tensor("s_bm", [32, T], F32, kind="Internal").ap()
    s_sz = nc.dram_tensor("s_sz", [NEB * 128, T], F32, kind="Internal").ap()

    uTh = nc.alloc_sbuf_tensor("uTh", [128, KD * T], F32R).ap()
    uTl = nc.alloc_sbuf_tensor("uTl", [128, KD * T], F32R).ap()
    x_sb = nc.alloc_sbuf_tensor("x_sb", [128, NTT * D], F32).ap()
    wshb = [nc.alloc_sbuf_tensor(f"wsh{i}", [128, 1024], F32R).ap()
            for i in range(2)]
    wslb = [nc.alloc_sbuf_tensor(f"wsl{i}", [128, 1024], F32R).ap()
            for i in range(2)]
    dtw_hi = nc.alloc_sbuf_tensor("dtw_hi", [64, DI], F32R).ap()
    dtw_lo = nc.alloc_sbuf_tensor("dtw_lo", [64, DI], F32R).ap()
    id_f32 = nc.alloc_sbuf_tensor("id_f32", [128, 128], F32).ap()
    id_bf = nc.alloc_sbuf_tensor("id_bf", [128, 128], BF16).ap()
    onesf = nc.alloc_sbuf_tensor("onesf_sb", [1, 128], F32).ap()
    acol = nc.alloc_sbuf_tensor("acol", [128, NEB * S], F32).ap()
    dcol = nc.alloc_sbuf_tensor("dcol", [128, NEB], F32).ap()
    cwcol = nc.alloc_sbuf_tensor("cwcol", [128, NEB * 4], F32).ap()
    cbcol = nc.alloc_sbuf_tensor("cbcol", [128, NEB], F32).ap()
    dtbcol = nc.alloc_sbuf_tensor("dtbcol", [128, NEB], F32).ap()
    dtb2col = nc.alloc_sbuf_tensor("dtb2col", [128, NEB], F32).ap()
    halfc = nc.alloc_sbuf_tensor("halfc", [128, 2], F32).ap()
    gcol = nc.alloc_sbuf_tensor("gcol", [128, KD], F32).ap()
    bcol = nc.alloc_sbuf_tensor("bcol", [128, KD], F32).ap()
    wbeta = nc.alloc_sbuf_tensor("wbeta", [128, 32], F32).ap()
    carry = nc.alloc_sbuf_tensor("carry", [128, 256], F32).ap()
    tails = nc.alloc_sbuf_tensor("tails", [128, NEB * 3], F32).ap()
    cols = nc.alloc_sbuf_tensor("cols", [128, 64], F32).ap()
    xch_sb = nc.alloc_sbuf_tensor("xch_sb", [128, T], F32R).ap()
    xcl_sb = nc.alloc_sbuf_tensor("xcl_sb", [128, T], F32R).ap()
    dtrh_sb = nc.alloc_sbuf_tensor("dtrh_sb", [64, T], F32R).ap()
    dtrl_sb = nc.alloc_sbuf_tensor("dtrl_sb", [64, T], F32R).ap()
    yph_sb, ypl_sb = xch_sb, xcl_sb
    xpwh_sb = nc.alloc_sbuf_tensor("xpwh_sb", [128, 768], F32R).ap()
    xpwl_sb = nc.alloc_sbuf_tensor("xpwl_sb", [128, 768], F32R).ap()

    AR_N = 26624
    arena = nc.alloc_sbuf_tensor("arena", [128, AR_N], F32).ap()

    def ar(off, n, dt=F32):
        a = arena[:, off:off + n]
        if dt is not F32:
            a = a.bitcast(dt)
        return a

    A_U32, A_U32T, A_TRASH = 0, 4096, 8192
    A_XC, A_Y = 0, 8192
    A_NBM, A_NCM = 16384, 18432
    A_DBUH = 20480
    A_DT, A_DTU, A_DA = 22528, 23040, 23552
    A_SP1, A_SP2 = 24064, 24576
    A_DA2 = 26112
    A_DTR32 = 16384
    A_XIN = 16896
    A_AB = 8192
    A_T2 = 24064
    A_SZ = 25088
    A_YPH, A_YPL = 20480, 20992
    E_A, E_F = 0, 4096
    E_X2Q, E_X2QT = 8192, 10240
    E_HQ = 10752
    E_HQT = 12800
    E_GB, E_GB2 = 14848, 15360
    E_BFC, E_BPJ = 16384, 20480
    E_QA, E_QC = 21504, 21760
    E_WFC = 22016
    E_WPJ = 22016
    W0, W1, W2 = 0, 4096, 8192
    P_QA, P_QC = 12288, 12544

    bmcm = ar(A_Y, T)

    Pb = [nc.alloc_psum_tensor(f"pb{i}", [128, 512], F32).ap() for i in range(8)]

    def act(eng_ignore, out, in_, func, waits, inc=True, **kw):
        if func is AF.Copy and any(
                not isinstance(kw.get(k, 0.0), (int, float))
                for k in ("bias", "scale")):
            func = AF.Identity
        return sc.emit("scalar", lambda e, _: e.activation(out, in_, func, **kw),
                       waits=waits, inc=inc)

    def emit_ln_cols(sum_tok, mud, rd, gd=None):
        S0, S1, S2, S3 = 20, 24, 28, 32
        t = sc.emit("vector", lambda e, _: e.tensor_scalar_mul(
            cols[:, mud:mud + 4], cols[:, 0:4], 1.0 / D), [sum_tok], inc=True)
        t = sc.emit("vector", lambda e, _: e.tensor_scalar_mul(
            cols[:, S0:S0 + 4], cols[:, 4:8], 1.0 / D), [t], inc=True)
        t = sc.emit("vector", lambda e, _: e.tensor_tensor(
            cols[:, S1:S1 + 4], cols[:, mud:mud + 4], cols[:, mud:mud + 4],
            ALU.mult), [t], inc=True)
        t = sc.emit("vector", lambda e, _: e.tensor_tensor(
            cols[:, S0:S0 + 4], cols[:, S0:S0 + 4], cols[:, S1:S1 + 4],
            ALU.subtract), [t], inc=True)
        tva = sc.emit("vector", lambda e, _: e.tensor_scalar_add(
            cols[:, S1:S1 + 4], cols[:, S0:S0 + 4], EPS), [t], inc=True)
        t = act(0, cols[:, S2:S2 + 4], cols[:, S1:S1 + 4], AF.Ln, [tva])
        tr = act(0, cols[:, rd:rd + 4], cols[:, S2:S2 + 4], AF.Exp, [t],
                 scale=-0.5)
        if gd is None:
            return tr
        t = sc.emit("vector", lambda e, _: e.reciprocal(
            cols[:, S2:S2 + 4], cols[:, S1:S1 + 4]), [tva, tr], inc=True)
        t = sc.emit("vector", lambda e, _: e.tensor_tensor(
            cols[:, S2:S2 + 4], cols[:, S0:S0 + 4], cols[:, S2:S2 + 4],
            ALU.mult), [t], inc=True)
        t = sc.emit("vector", lambda e, _: e.tensor_scalar_add(
            cols[:, S2:S2 + 4], cols[:, S2:S2 + 4], EPS), [t], inc=True)
        t = act(0, cols[:, S3:S3 + 4], cols[:, S2:S2 + 4], AF.Ln, [t])
        t = act(0, cols[:, S3:S3 + 4], cols[:, S3:S3 + 4], AF.Exp, [t],
                scale=-0.5)
        tg = sc.emit("vector", lambda e, _: e.tensor_tensor(
            cols[:, gd:gd + 4], cols[:, rd:rd + 4], cols[:, S3:S3 + 4],
            ALU.mult), [t, tr], inc=True)
        return tg

    def emit_ln_sums(src_fn, waits_fn, trash=None):
        toks = []
        to = A_TRASH if trash is None else trash
        for tt in range(NTT):
            t1 = act(0, ar(to, D), src_fn(tt), AF.Copy,
                     waits_fn(tt), accum_out=cols[:, tt:tt + 1])
            t2 = act(0, ar(to, D), src_fn(tt), AF.Square, [t1],
                     accum_out=cols[:, 4 + tt:5 + tt])
            toks.append(t2)
        return toks[-1]

    def emit_quant(src_ap, nblk, out_ap, waits, qa, qc):
        t1 = sc.emit("vector", lambda e, _: e.tensor_reduce(
            ar(qa, nblk)[:, :, None],
            src_ap.rearrange("p (n b) -> p n b", b=32),
            AX.X, ALU.max, apply_absolute_value=True), waits=waits, inc=True)
        t2 = sc.emit("vector", lambda e, _: e.tensor_scalar_max(
            ar(qa, nblk), ar(qa, nblk), TINY), waits=[t1], inc=True)
        t3 = sc.emit("vector", lambda e, _: e.tensor_scalar(
            ar(qc, nblk).bitcast(I32), ar(qa, nblk).bitcast(I32),
            int(0x7F800000), None, ALU.bitwise_and), waits=[t2], inc=True)
        t4 = sc.emit("vector", lambda e, _: e.tensor_scalar_mul(
            ar(qc, nblk), ar(qc, nblk), CMUL), waits=[t3], inc=True)
        cb = ar(qc, nblk)[:, :, None].broadcast_to([128, nblk, 32])
        t5 = sc.emit("vector", lambda e, _: e.tensor_tensor(
            src_ap.rearrange("p (n b) -> p n b", b=32),
            src_ap.rearrange("p (n b) -> p n b", b=32), cb, ALU.add),
            waits=[t4], inc=True)
        t6 = sc.emit("vector", lambda e, _: e.tensor_tensor(
            out_ap.rearrange("p (n b) -> p n b", b=32),
            src_ap.rearrange("p (n b) -> p n b", b=32), cb, ALU.subtract),
            waits=[t5], inc=True)
        return t6

    tk_c = tk_alog = tk_acol = tk_dtw = tk_xpw = None
    tk_bfcB = tk_bpjB = tk_wbeta = None
    tk_pdma = tk_ppe = tk_pact = tk_ev = None
    if psel is None or 11 in psel or len(psel & {2,3,4,5,6,7,8}) > 0 and 11 not in psel and 12 not in psel:
        for dst, src in ((gcol, gammaC_h), (bcol, betaC_h), (cwcol, cwC_h),
                         (cbcol, cbC_h), (dtbcol, dtbC_h), (dcol, dC_h),
                         (id_f32, idf32_h), (id_bf, idbf_h), (onesf, onesf_h)):
            tk_c = sc.dma(dst, src)
    if psel is None or 12 in psel or len(psel & {2,3,4,5,6,7,8}) > 0 and 11 not in psel and 12 not in psel:
        tk_alog = sc.dma(ar(A_TRASH, NEB * S), alogC_h)
        t = act(0, acol, ar(A_TRASH, NEB * S), AF.Exp, [tk_alog])
        tk_acol = sc.emit("vector", lambda e, _: e.tensor_scalar_mul(
            dtb2col, dtbcol, -0.5), waits=[t, tk_c], inc=True)
        th_ = sc.emit("vector", lambda e, _: e.memset(
            halfc[:, 0:1], 0.5), waits=[tk_acol], inc=True)
        tk_half = sc.emit("vector", lambda e, _: e.memset(
            halfc[:, 1:2], -0.5), waits=[th_], inc=True)

    if 2 in psel:
        tkw = sc.dma(ar(A_U32, DI)[0:64, :], dtwT_h)
        t = act(0, dtw_hi, ar(A_U32, DI)[0:64, :], AF.Copy, [tkw])
        tk_dtw = sc.emit("vector", lambda e, _: e.tensor_tensor(
            dtw_lo, ar(A_U32, DI)[0:64, :], dtw_hi, ALU.subtract),
            waits=[t], inc=True)

    if 3 in psel or 31 in psel:
        tkw = sc.dma(ar(A_U32, NEB * 96),
                     xpwT_h.rearrange("(e p) j -> p e j", p=128), waits=[tk_dtw])
        tk_xpw = tkw
    if 3 in psel or 32 in psel:
        t = act(0, ar(A_U32T, NEB * 96, F32R), ar(A_U32, NEB * 96), AF.Copy, [tkw])
        tk_xpw = t
    if 3 in psel or 33 in psel:
        t2 = sc.emit("vector", lambda e, _: e.tensor_tensor(
            ar(A_TRASH, NEB * 96, F32R), ar(A_U32, NEB * 96),
            ar(A_U32T, NEB * 96, F32R), ALU.subtract), waits=[t], inc=True)
        tk_xpw = t2
    if 3 in psel or 34 in psel:
        sc.dma(s_xpwh, ar(A_U32T, NEB * 96, F32R), waits=[t])
        tk_xpw = sc.dma(s_xpwl, ar(A_TRASH, NEB * 96, F32R), waits=[t2])

    if 4 in psel:
        tk_bfc = sc.dma(ar(A_U32, HID)[0:1, :], bfc_h, waits=[tk_xpw])
        tk_bpj = sc.dma(ar(A_U32T, D)[0:1, :], bproj_h)
        te = None
        for j in range(8):
            tm = sc.emit("tensor", lambda e, _, j=j: e.matmul(
                Pb[0][:], onesf, ar(A_U32, HID)[0:1, j * 512:(j + 1) * 512],
                start=True, stop=True), waits=[tk_bfc, tk_c, te], inc=True)
            te = act(0, ar(A_TRASH, HID)[:, j * 512:(j + 1) * 512], Pb[0][:],
                     AF.Copy, [tm])
        tk_bfcB = sc.dma(s_bfcB, ar(A_TRASH, HID), waits=[te])
        te = None
        for j in range(2):
            tm = sc.emit("tensor", lambda e, _, j=j: e.matmul(
                Pb[1][:], onesf, ar(A_U32T, D)[0:1, j * 512:(j + 1) * 512],
                start=True, stop=True), waits=[tk_bpj, te], inc=True)
            te = act(0, ar(A_TRASH + HID, D)[:, j * 512:(j + 1) * 512], Pb[1][:],
                     AF.Copy, [tm])
        tk_bpjB = sc.dma(s_bpjB, ar(A_TRASH + HID, D), waits=[te])

    if 5 in psel:
        tk_pdma = tk_bpjB
        tk_ppe = None
        tk_pact = None
        for kd in range(KD):
            tkw = sc.dma(ar(W0, 2 * DI), winT_h[kd * 128:(kd + 1) * 128, :],
                         waits=[tk_pdma, tk_ppe, tk_pact, tk_bfcB])
            for eb in range(32):
                tk_ppe = sc.emit("tensor", lambda e, _, kd=kd, eb=eb: e.matmul(
                    Pb[2][:, eb:eb + 1], ar(W0, 2 * DI)[:, eb * 128:(eb + 1) * 128],
                    bcol[:, kd:kd + 1], start=(kd == 0 and eb == 0),
                    stop=(kd == KD - 1)),
                    waits=[tkw], inc=(eb == 31))
            tg = act(0, ar(W1, 2 * DI), ar(W0, 2 * DI), AF.Copy, [tkw],
                     scale=gcol[:, kd:kd + 1])
            th = act(0, ar(W2, 2 * DI, F32R), ar(W1, 2 * DI), AF.Copy, [tg])
            tk_pact = th
            tl = sc.emit("vector", lambda e, _: e.tensor_tensor(
                ar(W1, 2 * DI, F32R), ar(W1, 2 * DI), ar(W2, 2 * DI, F32R),
                ALU.subtract), waits=[th], inc=True)
            sc.dma(s_inh[kd * 128:(kd + 1) * 128, :], ar(W2, 2 * DI, F32R),
                   waits=[th])
            tk_pdma = sc.dma(s_inl[kd * 128:(kd + 1) * 128, :],
                             ar(W1, 2 * DI, F32R), waits=[tl])
        tk_wbeta = act(0, wbeta, Pb[2][:, 0:32], AF.Copy, [tk_ppe])

    if 6 in psel:
        for ed in range(NEB):
            tkw = sc.dma(ar(W0, D), woutT_h[ed * 128:(ed + 1) * 128, :],
                         waits=[tk_pdma, tk_pact])
            th = act(0, ar(W1, D, F32R), ar(W0, D), AF.Copy, [tkw])
            tk_pact = th
            tl = sc.emit("vector", lambda e, _: e.tensor_tensor(
                ar(W2, D, F32R), ar(W0, D), ar(W1, D, F32R), ALU.subtract),
                waits=[th], inc=True)
            sc.dma(s_outh[ed * 128:(ed + 1) * 128, :], ar(W1, D, F32R), waits=[th])
            tk_pdma = sc.dma(s_outl[ed * 128:(ed + 1) * 128, :],
                             ar(W2, D, F32R), waits=[tl])

    if 7 in psel:
        tk_ev = None
        for nb in range(32):
            tkw = sc.dma(ar(W0, D), wfc_h[nb * 128:(nb + 1) * 128, :],
                         waits=[tk_pdma, tk_pact])
            tk_pact = None
            tq = emit_quant(ar(W0, D), 32, ar(W1, 512, BF16), [tkw], P_QA, P_QC)
            tp = None
            for kb in range(8):
                tp = sc.emit("tensor", lambda e, _, kb=kb: e.transpose(
                    Pb[3].bitcast(BF16)[:, kb * 128:(kb + 1) * 128],
                    ar(W1, 512, BF16)[:, kb * 128:(kb + 1) * 128], id_bf),
                    waits=[tq, tk_ev] if kb == 0 else [], inc=(kb == 7))
            tk_ev = sc.emit("vector", lambda e, _: e.tensor_copy(
                ar(W2, 512, BF16), Pb[3].bitcast(BF16)[:, 0:1024]),
                waits=[tp], inc=True)
            tk_pdma = sc.dma(
                s_fcq.rearrange("(kb p) n -> p kb n", p=128)[
                    :, :, nb * 128:(nb + 1) * 128],
                ar(W2, 512, BF16).rearrange("p (kb n) -> p kb n", n=128),
                waits=[tk_ev])

    if 8 in psel:
        for eb in range(8):
            tkw = sc.dma(ar(W0, HID), wproj_h[eb * 128:(eb + 1) * 128, :],
                         waits=[tk_pdma])
            tq = emit_quant(ar(W0, HID), 128, ar(W1, 2048, BF16), [tkw], P_QA, P_QC)
            tlast = tq
            for rnd in range(4):
                tp = None
                for k in range(8):
                    hb = rnd * 8 + k
                    tp = sc.emit("tensor", lambda e, _, hb=hb, k=k: e.transpose(
                        Pb[3].bitcast(BF16)[:, k * 128:(k + 1) * 128],
                        ar(W1, 2048, BF16)[:, hb * 128:(hb + 1) * 128], id_bf),
                        waits=[tlast, tk_ev] if k == 0 else [], inc=(k == 7))
                tk_ev = sc.emit("vector", lambda e, _, rnd=rnd: e.tensor_copy(
                    ar(W2, 2048, BF16)[:, rnd * 1024:(rnd + 1) * 1024],
                    Pb[3].bitcast(BF16)[:, 0:1024]), waits=[tp], inc=True)
                tlast = tk_ev
            tk_pdma = sc.dma(
                s_pjq.rearrange("(hb p) e -> p hb e", p=128)[
                    :, :, eb * 128:(eb + 1) * 128],
                ar(W2, 2048, BF16).rearrange("p (hb e) -> p hb e", e=128),
                waits=[tlast])

    if 5 not in psel:
        tk_pdma = None
        tk_ev = None
    tk_prologue = tk_pdma
    tk_prologue_ev = tk_ev if 7 in psel else None

    tk_out_dma = tk_prologue

    for c in range(nch):
        r0 = c * T

        tk_x = sc.dma(
            x_sb.rearrange("p (tt d) -> p tt d", d=D),
            x_h[r0:r0 + T, :].rearrange("(tt p) d -> p tt d", p=128),
            waits=[tk_out_dma])

        tsum = emit_ln_sums(
            lambda tt: x_sb[:, tt * D:(tt + 1) * D],
            lambda tt: [tk_x, tk_wbeta, tk_acol] if tt == 0 else [])
        tg1 = emit_ln_cols(tsum, 8, 12, gd=16)

        tk_pe_prev = tk_wbeta if c == 0 else None
        tk_u = None
        for tt in range(NTT):
            tu = sc.emit("vector", lambda e, _, tt=tt: e.tensor_scalar(
                ar(A_U32, NTT * D)[:, tt * D:(tt + 1) * D],
                x_sb[:, tt * D:(tt + 1) * D],
                cols[:, 8 + tt:9 + tt], cols[:, 16 + tt:17 + tt],
                ALU.subtract, ALU.mult), waits=[tg1], inc=True)
            for grp in range(2):
                tp = None
                for k in range(4):
                    kd = grp * 4 + k
                    tp = sc.emit("tensor", lambda e, _, tt=tt, kd=kd, k=k:
                        e.transpose(
                            Pb[2][:, k * 128:(k + 1) * 128],
                            ar(A_U32, NTT * D)[
                                :, tt * D + kd * 128:tt * D + (kd + 1) * 128],
                            id_f32),
                        waits=[tu, tk_pe_prev] if k == 0 else [], inc=(k == 3))
                uh_dst = uTh.rearrange("p (kd t) -> p kd t", t=T)[
                    :, grp * 4:(grp + 1) * 4, tt * 128:(tt + 1) * 128]
                u32_dst = ar(A_U32T, NTT * D).rearrange(
                    "p (kd t) -> p kd t", t=T)[
                    :, grp * 4:(grp + 1) * 4, tt * 128:(tt + 1) * 128]
                psrc = Pb[2].rearrange("p (k t) -> p k t", t=128)
                th = act(0, uh_dst, psrc, AF.Copy, [tp])
                t32 = act(0, u32_dst, psrc, AF.Copy, [th])
                tk_pe_prev = t32
                tk_u = t32
        tk_ulo = sc.emit("vector", lambda e, _: e.tensor_tensor(
            uTl, ar(A_U32T, KD * T), uTh, ALU.subtract), waits=[tk_u], inc=True)

        if debug == "u" and c == 0:
            sc.dma(dbg_h[:, 0:NTT * D], ar(A_U32, NTT * D), waits=[tk_ulo])
            sc.dma(dbg_h[:, NTT * D:2 * NTT * D], ar(A_U32T, NTT * D),
                   waits=[tk_ulo])
            sc.dma(dbg_h[:, 2 * NTT * D:2 * NTT * D + 64], cols,
                   waits=[tk_ulo])

        tk_xpwl = None

        if c == 0:
            tk_tails = sc.emit("vector", lambda e, _: e.memset(tails, 0.0),
                               waits=[], inc=True)
        tk_wfree = {0: None, 1: None}
        tk_evac = {0: None, 1: None}
        tk_xp_last = None
        tk_xp_prev = None
        tk_xc = []
        xin = ar(A_XIN, 530)
        for eb in range(NEB):
            bank = eb % 2
            if eb % 8 == 0:
                hw0 = (eb // 8) * 768
                tk_xpwl = sc.dma(xpwh_sb, s_xpwh[:, hw0:hw0 + 768],
                                 waits=[tk_xpw, tk_xp_prev])
                tk_xpwl = sc.dma(xpwl_sb, s_xpwl[:, hw0:hw0 + 768],
                                 waits=[tk_xpwl])
            wsh, wsl = wshb[bank], wslb[bank]
            tw1 = sc.dma(
                wsh.rearrange("p (kd e) -> p kd e", e=128),
                s_inh.rearrange("(kd p) e -> p kd e", p=128)[
                    :, :, eb * 128:(eb + 1) * 128],
                waits=[tk_wfree[bank],
                       tk_prologue if (c == 0 and eb == 0) else None],
                ch="cw0")
            tw2 = sc.dma(
                wsl.rearrange("p (kd e) -> p kd e", e=128),
                s_inl.rearrange("(kd p) e -> p kd e", p=128)[
                    :, :, eb * 128:(eb + 1) * 128],
                waits=[tk_wfree[bank]], ch="cw1")
            tmm = None
            for p, (wa, ua) in enumerate(((wsh, uTh), (wsh, uTl), (wsl, uTh))):
                for kd in range(KD):
                    tmm = sc.emit("tensor", lambda e, _, wa=wa, ua=ua, kd=kd,
                                  p=p, bank=bank: e.matmul(
                        Pb[bank][:], wa[:, kd * 128:(kd + 1) * 128],
                        ua[:, kd * T:(kd + 1) * T],
                        start=(p == 0 and kd == 0),
                        stop=(p == 2 and kd == KD - 1)),
                        waits=([tw1, tw2, tk_ulo, tk_evac[bank]]
                               if (p == 0 and kd == 0) else []),
                        inc=(p == 2 and kd == KD - 1))
            tk_wfree[bank] = tmm
            if debug == "xin" and c == 0 and eb == 0:
                trec = sc.emit("vector", lambda e, _: e.tensor_tensor(
                    ar(A_Y, 1024), wshb[0], wslb[0], ALU.add), waits=[tmm], inc=True)
                sc.dma(dbg_h[:, 2048:3072], ar(A_Y, 1024), waits=[trec])
            thal = sc.emit("vector", lambda e, _, eb=eb: e.tensor_copy(
                xin[:, 0:3], tails[:, eb * 3:(eb + 1) * 3]),
                waits=[tk_tails if c == 0 else None], inc=True)
            tev = act(0, xin[:, 3:3 + T], Pb[bank][:], AF.Copy, [tmm, thal],
                      bias=wbeta[:, eb:eb + 1])
            tk_evac[bank] = tev
            sc.emit("vector", lambda e, _, eb=eb: e.tensor_copy(
                tails[:, eb * 3:(eb + 1) * 3], xin[:, T:T + 3]),
                waits=[tev], inc=True)
            tcv = sc.emit("vector", lambda e, _, eb=eb: e.tensor_scalar_mul(
                ar(A_SP1, T), xin[:, 0:T], cwcol[:, eb * 4:eb * 4 + 1]),
                waits=[tev], inc=True)
            for k in range(1, 4):
                tcv = sc.emit("vector", lambda e, _, eb=eb, k=k:
                    e.scalar_tensor_tensor(
                        ar(A_SP1, T), xin[:, k:k + T],
                        cwcol[:, eb * 4 + k:eb * 4 + k + 1], ar(A_SP1, T),
                        ALU.mult, ALU.add), waits=[], inc=(k == 3))
            if debug == "xin" and c == 0 and eb == 0:
                sc.dma(dbg_h[:, 0:530], xin, waits=[tcv])
                sc.dma(dbg_h[:, 530:530 + 512], ar(A_SP1, T), waits=[tcv])
            xcs = ar(A_XC, NEB * T)[:, eb * T:(eb + 1) * T]
            txc = act(0, xcs, ar(A_SP1, T), AF.Silu, [tcv],
                      bias=cbcol[:, eb:eb + 1])
            tk_xc.append(txc)
            tph = act(0, xch_sb, xcs, AF.Copy, [txc, tk_xp_prev])
            tpl = sc.emit("vector", lambda e, _, xcs=xcs: e.tensor_tensor(
                xcl_sb, xcs, xch_sb, ALU.subtract),
                waits=[tph], inc=True)
            for p, wa, ua in ((0, xpwh_sb, xch_sb),
                              (1, xpwh_sb, xcl_sb),
                              (2, xpwl_sb, xch_sb)):
                tk_xp_last = sc.emit("tensor",
                    lambda e, _, eb=eb, p=p, wa=wa, ua=ua: e.matmul(
                        Pb[6][0:96, :],
                        wa[:, (eb % 8) * 96:(eb % 8 + 1) * 96], ua,
                        start=(eb == 0 and p == 0),
                        stop=(eb == NEB - 1 and p == 2)),
                    waits=([tpl, tk_xpwl] if p == 0 else []),
                    inc=(p == 2))
            tk_xp_prev = tk_xp_last

        tdr = act(0, ar(A_DTR32, T)[0:64, :], Pb[6][0:64, :], AF.Copy,
                  [tk_xp_last])
        tbc = act(0, bmcm[64:96, :], Pb[6][64:96, :], AF.Copy, [tk_xp_last])
        tdrh = act(0, dtrh_sb, ar(A_DTR32, T)[0:64, :], AF.Copy, [tdr])
        tdrl = sc.emit("vector", lambda e, _: e.tensor_tensor(
            dtrl_sb, ar(A_DTR32, T)[0:64, :], dtrh_sb,
            ALU.subtract), waits=[tdrh], inc=True)
        tk_bm_dma = sc.dma(s_bm, bmcm[64:96, :], waits=[tbc])

        if debug == "xc" and c == 0:
            sc.dma(dbg_h[:, 0:NEB * T], ar(A_XC, NEB * T), waits=[tk_xc[-1]])
            sc.dma(dbg_h[:, NEB * T:NEB * T + T], bmcm, waits=[tbc])
            sc.dma(dbg_h[:, NEB * T + T:NEB * T + 2 * T], ar(A_DTR32, T),
                   waits=[tdr])

        DT, DTU = ar(A_DT, T), ar(A_DTU, T)
        tk_scan_dv = None
        tk_dt_prev = None
        tk_dtu_prev = None
        tk_dtp_evac = {0: None, 1: None}
        tk_slotfree = {0: None, 1: None}
        tk_scan2 = {0: None, 1: None}
        tk_zevac = dict(tk_evac)
        tk_zout = {0: None, 1: None}
        tk_dbu_last = None
        gi = 0
        gi2 = 0

        def emit_z(eb):
            bank = eb % 2
            tw1 = sc.dma(
                wshb[bank].rearrange("p (kd e) -> p kd e", e=128),
                s_inh.rearrange("(kd p) e -> p kd e", p=128)[
                    :, :, (16 + eb) * 128:(17 + eb) * 128],
                waits=[tk_wfree[bank]], ch="cw0")
            tw2 = sc.dma(
                wslb[bank].rearrange("p (kd e) -> p kd e", e=128),
                s_inl.rearrange("(kd p) e -> p kd e", p=128)[
                    :, :, (16 + eb) * 128:(17 + eb) * 128],
                waits=[tk_wfree[bank]], ch="cw1")
            tmm = None
            for p, (wa, ua) in enumerate(((wshb[bank], uTh), (wshb[bank], uTl),
                                          (wslb[bank], uTh))):
                for kd in range(KD):
                    tmm = sc.emit("tensor", lambda e, _, wa=wa, ua=ua, kd=kd,
                                  p=p, bank=bank: e.matmul(
                        Pb[bank][:], wa[:, kd * 128:(kd + 1) * 128],
                        ua[:, kd * T:(kd + 1) * T],
                        start=(p == 0 and kd == 0),
                        stop=(p == 2 and kd == KD - 1)),
                        waits=([tw1, tw2, tk_zevac[bank]]
                               if (p == 0 and kd == 0) else []),
                        inc=(p == 2 and kd == KD - 1))
            tk_wfree[bank] = tmm
            szb = ar(A_SZ + bank * 512, T)
            tsz = act(0, szb, Pb[bank][:], AF.Silu, [tmm, tk_zout[bank]],
                      bias=wbeta[:, 16 + eb:17 + eb])
            tk_zevac[bank] = tsz
            tk_zout[bank] = sc.dma(s_sz[eb * 128:(eb + 1) * 128, :], szb,
                                   waits=[tsz], ch="cp")
        cst = {"pend": None, "y": None, "pool": None, "burst": None}

        def flush_pend():
            p = cst["pend"]
            if p is None:
                return
            cst["pend"] = None
            gs, slot, dbuh, tcar, hc_pool, db_p, first, tkncm = p
            eng = "gpsimd" if hc_pool else "vector"
            thc = sc.emit(eng, lambda e, _, gs=gs, dbuh=dbuh:
                e.tensor_tensor(
                    dbuh, dbuh,
                    ar(A_NCM, SH * T)[:, gs * T:(gs + SG) * T], ALU.mult),
                waits=[tcar, tkncm], inc=True)
            if hc_pool:
                cst["pool"] = thc
            ysl = ar(A_Y, NEB * T)[:, db_p * T:(db_p + 1) * T]
            ty0 = sc.emit("vector", lambda e, _, ysl=ysl, dbuh=dbuh,
                          first=first: (e.tensor_copy(
                              ysl, dbuh[:, 0:T]) if first else
                              e.tensor_tensor(ysl, ysl, dbuh[:, 0:T],
                                              ALU.add)),
                waits=[thc], inc=True)
            ty1 = sc.emit("gpsimd", lambda e, _, ysl=ysl,
                          dbuh=dbuh: e.tensor_tensor(
                    ysl, ysl, dbuh[:, T:2 * T], ALU.add),
                waits=[ty0], inc=True)
            tk_slotfree[slot] = ty1
            cst["y"] = ty1
            cst["pool"] = ty1

        for half in range(S // SH):
            flush_pend()
            nbm = ar(A_NBM, SH * T)
            ncm = ar(A_NCM, SH * T)
            tk_nbm = sc.dma(
                nbm.rearrange("p (s t) -> p s t", t=T),
                s_bm[half * SH:(half + 1) * SH, :].partition_broadcast(128),
                waits=[tk_bm_dma, cst["burst"], tdrl])
            tk_ncm = sc.dma(
                ncm.rearrange("p (s t) -> p s t", t=T),
                s_bm[16 + half * SH:16 + (half + 1) * SH, :]
                .partition_broadcast(128),
                waits=[tk_bm_dma, cst["y"]])
            for db in range(NEB):
                bank = db % 2
                sp1 = ar(A_SP1, T)
                sp2 = ar(A_SP2, T)
                tmm = None
                for p, (wa, ua) in enumerate((
                        (dtw_hi, dtrh_sb), (dtw_hi, dtrl_sb),
                        (dtw_lo, dtrh_sb))):
                    tmm = sc.emit("tensor", lambda e, _, wa=wa, ua=ua, db=db,
                                  p=p, bank=bank: e.matmul(
                        Pb[4 + bank][:], wa[:, db * 128:(db + 1) * 128], ua,
                        start=(p == 0), stop=(p == 2)),
                        waits=([tdrl, tk_dtp_evac[bank]] if p == 0 else []),
                        inc=(p == 2))
                t1 = act(0, sp1, Pb[4 + bank][:], AF.Tanh,
                         [tmm, tk_dt_prev,
                          tk_half if (c == 0 and half == 0 and db == 0)
                          else None],
                         scale=halfc[:, 1:2], bias=dtb2col[:, db:db + 1])
                tk_dtp_evac[bank] = t1
                tdt = act(0, DT, sp1, AF.Ln, [t1, tk_dtu_prev],
                          scale=halfc[:, 0:1], bias=halfc[:, 0:1])
                tk_dt_prev = tdt
                tdtu = sc.emit("vector", lambda e, _, db=db:
                    e.scalar_tensor_tensor(
                        DTU, DT, -1.0,
                        ar(A_XC, NEB * T)[:, db * T:(db + 1) * T],
                        ALU.mult, ALU.mult),
                    waits=[tdt, tk_xc[db]], inc=True)
                tk_dtu_prev = tdtu
                for g in range(SH // SG):
                    s0 = half * SH + g * SG
                    gs = g * SG
                    slot = gi % 2
                    dbuh = ar(A_DBUH + slot * (SG * T), SG * T)
                    flush_pend()
                    tdbu = sc.emit("vector", lambda e, _, gs=gs, dbuh=dbuh:
                        e.tensor_tensor(
                            dbuh.rearrange("p (s t) -> p s t", t=T),
                            DTU.unsqueeze(1).broadcast_to([128, SG, T]),
                            ar(A_NBM, SH * T)[:, gs * T:(gs + SG) * T]
                            .rearrange("p (s t) -> p s t", t=T),
                            ALU.mult),
                        waits=[tdtu, tk_nbm, tk_slotfree[slot],
                               cst["burst"]], inc=True)
                    cst["pool"] = tdbu
                    tk_dbu_last = tdbu
                    tsc_ = None
                    for si in range(SG):
                        s = s0 + si
                        da_s = gi2 % 2
                        gi2 += 1
                        da = ar(A_DA if da_s == 0 else A_DA2, T)
                        tda = act(0, da, DT, AF.Exp,
                                  [tdt, tk_scan2[da_s]],
                                  scale=acol[:, db * S + s:db * S + s + 1])
                        hseg = dbuh[:, si * T:(si + 1) * T]
                        init = (0.0 if c == 0 else
                                carry[:, db * 16 + s:db * 16 + s + 1])
                        tsc_ = sc.emit("vector", lambda e, _, hseg=hseg,
                                       init=init, da=da: e.tensor_tensor_scan(
                            hseg, da, hseg, init, ALU.mult, ALU.add),
                            waits=[tda, cst["pool"] if si == 0 else None],
                            inc=True)
                        tk_scan2[da_s] = tsc_
                    cst["burst"] = tsc_
                    tcar = sc.emit("gpsimd", lambda e, _, db=db, s0=s0,
                                   dbuh=dbuh: e.tensor_copy(
                            carry[:, db * 16 + s0:db * 16 + s0 + SG],
                            dbuh.rearrange(
                                "p (s t) -> p s t", t=T)[:, :, T - 1]),
                        waits=[tsc_], inc=True)
                    cst["pend"] = (gs, slot, dbuh, tcar, False, db,
                                   half == 0 and g == 0, tk_ncm)
                    gi += 1
                if half < 2 and db % 2 == 0:
                    emit_z(half * 8 + db // 2)
        flush_pend()
        tk_scan_dv = cst["y"]

        if debug == "y" and c == 0:
            sc.dma(dbg_h[:, 0:NEB * T], ar(A_Y, NEB * T), waits=[tk_scan_dv])

        tk_evacD = {0: None, 1: None}
        tk_gate = []
        for eb in range(NEB):
            bank = eb % 2
            szb = ar(A_SZ + bank * 512, T)
            tsz = sc.dma(szb, s_sz[eb * 128:(eb + 1) * 128, :],
                         waits=[tk_evacD[bank]], ch="cp")
            ysl = ar(A_Y, NEB * T)[:, eb * T:(eb + 1) * T]
            xcs = ar(A_XC, NEB * T)[:, eb * T:(eb + 1) * T]
            tyt = sc.emit("vector", lambda e, _, ysl=ysl, xcs=xcs, eb=eb:
                e.scalar_tensor_tensor(
                    ysl, xcs, dcol[:, eb:eb + 1], ysl, ALU.mult, ALU.add),
                waits=[tsz, tk_scan_dv], inc=True)
            tyg = sc.emit("vector", lambda e, _, ysl=ysl, szb=szb:
                e.tensor_tensor(ysl, ysl, szb, ALU.mult),
                waits=[tyt], inc=True)
            tk_evacD[bank] = tyg
            tk_gate.append(tyg)

        if debug == "yg" and c == 0:
            sc.dma(dbg_h[:, 0:NEB * T], ar(A_Y, NEB * T), waits=[tk_gate[-1]])

        tk_op_mm = None
        for db in range(NEB):
            ysl = ar(A_Y, NEB * T)[:, db * T:(db + 1) * T]
            typh = act(0, yph_sb, ysl, AF.Copy,
                       [tk_gate[db], tk_gate[-1], tk_op_mm])
            typl = sc.emit("vector", lambda e, _, ysl=ysl: e.tensor_tensor(
                ypl_sb, ysl, yph_sb, ALU.subtract),
                waits=[typh], inc=True)
            bank = db % 2
            wsh, wsl = wshb[bank], wslb[bank]
            tw1 = sc.dma(wsh[:, 0:1024], s_outh[db * 128:(db + 1) * 128, :],
                         waits=[tk_wfree[bank]], ch="cw0")
            tw2 = sc.dma(wsl[:, 0:1024], s_outl[db * 128:(db + 1) * 128, :],
                         waits=[tk_wfree[bank]], ch="cw1")
            for p, (ya, wa) in enumerate((
                    (yph_sb, wshb[bank]), (ypl_sb, wshb[bank]),
                    (yph_sb, wslb[bank]))):
                for tt in range(NTT):
                    for eh in range(2):
                        tk_op_mm = sc.emit("tensor",
                            lambda e, _, ya=ya, wa=wa, tt=tt, eh=eh, p=p,
                            db=db: e.matmul(
                                Pb[tt * 2 + eh][:],
                                ya[:, tt * 128:(tt + 1) * 128],
                                wa[:, eh * 512:(eh + 1) * 512],
                                start=(db == 0 and p == 0),
                                stop=(db == NEB - 1 and p == 2)),
                            waits=([typl, tw1, tw2, tk_evacD[0], tk_evacD[1]]
                                   if (p == 0 and tt == 0 and eh == 0)
                                   else []),
                            inc=(p == 2 and tt == NTT - 1 and eh == 1))
            tk_wfree[bank] = tk_op_mm

        if debug == "m" and c == 0:
            for i in range(8):
                tcp = sc.emit("vector", lambda e, _, i=i: e.tensor_copy(
                    ar(A_NBM, 4096)[:, i * 512:(i + 1) * 512], Pb[i][:]),
                    waits=[tk_op_mm], inc=True)
            sc.dma(dbg_h[:, 0:4096], ar(A_NBM, 4096), waits=[tcp])

        tk_a = None
        for tt in range(NTT):
            tt2 = sc.emit("vector", lambda e, _, tt=tt: e.tensor_scalar(
                ar(A_T2, 1024), x_sb[:, tt * D:(tt + 1) * D],
                cols[:, 8 + tt:9 + tt], cols[:, 12 + tt:13 + tt],
                ALU.subtract, ALU.mult), waits=[tk_op_mm], inc=True)
            for eh in range(2):
                tk_a = sc.emit("vector", lambda e, _, tt=tt, eh=eh:
                    e.tensor_tensor(
                        ar(A_AB, NTT * D)[
                            :, tt * D + eh * 512:tt * D + (eh + 1) * 512],
                        ar(A_T2, 1024)[:, eh * 512:(eh + 1) * 512],
                        Pb[tt * 2 + eh][:], ALU.add), waits=[tt2], inc=True)
        if debug == "a" and c == 0:
            sc.dma(dbg_h[:, 0:NTT * D], ar(A_AB, NTT * D), waits=[tk_a])
        tsum = emit_ln_sums(
            lambda tt: ar(A_AB, NTT * D)[:, tt * D:(tt + 1) * D],
            lambda tt: [tk_a] if tt == 0 else [], trash=A_NBM)
        tRA = emit_ln_cols(tsum, 40, 44)
        tk_x1 = None
        for tt in range(NTT):
            tq = sc.emit("vector", lambda e, _, tt=tt: e.tensor_scalar(
                ar(A_NBM, D), ar(A_AB, NTT * D)[:, tt * D:(tt + 1) * D],
                cols[:, 40 + tt:41 + tt], cols[:, 44 + tt:45 + tt],
                ALU.subtract, ALU.mult), waits=[tRA], inc=True)
            tk_x1 = sc.emit("vector", lambda e, _, tt=tt: e.tensor_tensor(
                x_sb[:, tt * D:(tt + 1) * D], x_sb[:, tt * D:(tt + 1) * D],
                ar(A_NBM, D), ALU.add), waits=[tq], inc=True)

        if debug == "x1" and c == 0:
            tdmp = sc.dma(dbg_h[:, 0:NTT * D], x_sb, waits=[tk_x1])
            tk_x1 = sc.emit("vector", lambda e, _: e.engine_nop(),
                            waits=[tdmp], inc=True)

        tk_bfcL = sc.dma(ar(E_BFC, HID), s_bfcB, waits=[tk_bfcB, tk_x1])
        tk_bpjL = sc.dma(ar(E_BPJ, D), s_bpjB, waits=[tk_bpjB])
        tsum = emit_ln_sums(
            lambda tt: x_sb[:, tt * D:(tt + 1) * D],
            lambda tt: [tk_x1] if tt == 0 else [])
        tR3 = emit_ln_cols(tsum, 48, 52)

        tk_tpfree = None
        tk_fc_mm = {0: None, 1: None}
        tk_pj_mm = None
        tk_pjw = {0: None, 1: None}
        tk_f = None
        for tt in range(NTT):
            tq = sc.emit("vector", lambda e, _, tt=tt: e.tensor_scalar(
                ar(E_A, D), x_sb[:, tt * D:(tt + 1) * D],
                cols[:, 48 + tt:49 + tt], cols[:, 52 + tt:53 + tt],
                ALU.subtract, ALU.mult), waits=[tR3], inc=True)
            tq = emit_quant(
                ar(E_A, D), 32,
                ar(E_X2Q, NTT * 512, BF16)[:, tt * D:(tt + 1) * D],
                [tq], E_QA, E_QC)
            if debug == "ffn" and c == 0 and tt == 0:
                tq2_ = sc.dma(dbg_h.bitcast(BF16)[:, 16384:17408],
                              ar(E_X2Q, NTT * 512, BF16)[:, 0:D], waits=[tq])
                tq = sc.emit("vector", lambda e, _: e.engine_nop(),
                             waits=[tq2_], inc=True)
            tp = None
            for kd in range(KD):
                tp = sc.emit("tensor", lambda e, _, tt=tt, kd=kd: e.transpose(
                    Pb[3].bitcast(BF16)[:, kd * 128:(kd + 1) * 128],
                    ar(E_X2Q, NTT * 512, BF16)[
                        :, tt * D + kd * 128:tt * D + (kd + 1) * 128],
                    id_bf),
                    waits=[tq, tk_tpfree] if kd == 0 else [], inc=(kd == 7))
            tk_x2qT = sc.emit("vector", lambda e, _: e.tensor_copy(
                ar(E_X2QT, 512, BF16), Pb[3].bitcast(BF16)[:, 0:1024]),
                waits=[tp], inc=True)
            tk_tpfree = tk_x2qT

            tk_fcev = {0: None, 1: None}
            for nb in range(8):
                par = nb % 2
                bank = 4 + par
                wfcs = ar(E_WFC + par * 2048, 2048, BF16)
                twf = sc.dma(
                    wfcs.rearrange("p (kd n) -> p kd n", n=512),
                    s_fcq.rearrange("(kd p) n -> p kd n", p=128)[
                        :, :, nb * 512:(nb + 1) * 512],
                    waits=[tk_fc_mm[par], tk_pj_mm,
                           tk_x1 if (tt == 0 and nb <= 1) else None,
                           tk_prologue if (c == 0 and tt == 0 and nb == 0)
                           else None], ch="cw0")
                tmm = None
                for kd in range(KD):
                    tmm = sc.emit("tensor", lambda e, _, kd=kd, bank=bank,
                                  wfcs=wfcs: e.matmul(
                            Pb[bank][:],
                            ar(E_X2QT, 512, BF16)[:, kd * 128:(kd + 1) * 128],
                            wfcs[:, kd * 512:(kd + 1) * 512],
                            start=(kd == 0), stop=(kd == KD - 1)),
                        waits=([twf, tk_x2qT, tk_fcev[par]]
                               if kd == 0 else []),
                        inc=(kd == KD - 1))
                tk_fc_mm[par] = tmm
                te1 = sc.emit("vector", lambda e, _, nb=nb, bank=bank:
                    e.tensor_tensor(
                        ar(E_GB, T), Pb[bank][:],
                        ar(E_BFC, HID)[:, nb * 512:(nb + 1) * 512], ALU.add),
                    waits=[tmm, tk_bfcL], inc=True)
                if debug == "h0" and c == 0 and tt == 0 and nb == 0:
                    td_ = sc.dma(dbg_h[:, 0:T], ar(E_GB, T), waits=[te1])
                    td_ = sc.dma(dbg_h[:, T:T + HID], ar(E_BFC, HID),
                                 waits=[td_])
                    te1 = sc.emit("vector", lambda e, _: e.engine_nop(),
                                  waits=[td_], inc=True)
                te2 = act(0, ar(E_GB2, T), ar(E_GB, T), AF.Gelu, [te1])
                tq2 = emit_quant(
                    ar(E_GB2, T), 16,
                    ar(E_HQ, 2048, BF16)[:, nb * 512:(nb + 1) * 512],
                    [te2], E_QA, E_QC)
                tk_fcev[nb % 2] = tq2

            if debug == "hq0" and c == 0 and tt == 0:
                td_ = sc.dma(dbg_h.bitcast(BF16)[:, 0:HID],
                             ar(E_HQ, 2048, BF16),
                             waits=[tk_fcev[0], tk_fcev[1]])
                tk_fcev[1] = sc.emit("vector", lambda e, _: e.engine_nop(),
                                     waits=[td_], inc=True)
            tlast = tk_fcev[1]
            for rnd in range(4):
                tp = None
                for k in range(8):
                    hb = rnd * 8 + k
                    tp = sc.emit("tensor", lambda e, _, hb=hb, k=k:
                        e.transpose(
                            Pb[3].bitcast(BF16)[:, k * 128:(k + 1) * 128],
                            ar(E_HQ, 2048, BF16)[:, hb * 128:(hb + 1) * 128],
                            id_bf),
                        waits=[tk_fcev[0], tk_fcev[1], tlast]
                              if k == 0 else [], inc=(k == 7))
                tlast = sc.emit("vector", lambda e, _, rnd=rnd: e.tensor_copy(
                    ar(E_HQT, 2048, BF16)[:, rnd * 1024:(rnd + 1) * 1024],
                    Pb[3].bitcast(BF16)[:, 0:1024]), waits=[tp], inc=True)
            tk_tpfree = tlast

            for hbp in range(16):
                par = hbp % 2
                pslab = ar(E_WPJ + par * 1024, 1024, BF16)
                twp = sc.dma(
                    pslab.rearrange("p (two e) -> p two e", e=1024),
                    s_pjq[hbp * 256:(hbp + 1) * 256, :].rearrange(
                        "(two p) e -> p two e", p=128),
                    waits=[tk_pjw[par],
                           tk_fc_mm[0] if hbp == 0 else None,
                           tk_fc_mm[1] if hbp == 0 else None], ch="cw1")
                for two in range(2):
                    hb = hbp * 2 + two
                    for eh in range(2):
                        tk_pj_mm = sc.emit("tensor",
                            lambda e, _, hb=hb, eh=eh, two=two, pslab=pslab,
                            tlast=tlast: e.matmul(
                                Pb[6 + eh][:],
                                ar(E_HQT, 2048, BF16)[
                                    :, hb * 128:(hb + 1) * 128],
                                pslab[:, two * 1024 + eh * 512:
                                      two * 1024 + (eh + 1) * 512],
                                start=(hb == 0), stop=(hb == 31)),
                            waits=([twp, tlast] if (two == 0 and eh == 0)
                                   else []),
                            inc=(two == 1 and eh == 1))
                tk_pjw[par] = tk_pj_mm
            for eh in range(2):
                tk_f = sc.emit("vector", lambda e, _, tt=tt, eh=eh:
                    e.tensor_tensor(
                        ar(E_F, NTT * D)[
                            :, tt * D + eh * 512:tt * D + (eh + 1) * 512],
                        Pb[6 + eh][:],
                        ar(E_BPJ, D)[:, eh * 512:(eh + 1) * 512], ALU.add),
                    waits=[tk_pj_mm, tk_bpjL], inc=True)

        if debug == "wq" and c == 0:
            sc.dma(dbg_h.bitcast(BF16)[:, 0:HID], s_fcq[0:128, :],
                   waits=[tk_prologue])
            sc.dma(dbg_h.bitcast(BF16)[:, HID:HID + D], s_pjq[0:128, :],
                   waits=[tk_prologue])
        if debug == "ffn" and c == 0:
            sc.dma(dbg_h[:, 0:NTT * D], ar(E_F, NTT * D), waits=[tk_f])
            sc.dma(dbg_h.bitcast(BF16)[:, 2 * NTT * D:3 * NTT * D],
                   ar(E_X2Q, NTT * 512, BF16), waits=[tk_f])
            sc.dma(dbg_h.bitcast(BF16)[:, 3 * NTT * D:3 * NTT * D + 4096],
                   ar(E_HQ, 2048, BF16), waits=[tk_f])
        tsum = emit_ln_sums(
            lambda tt: ar(E_F, NTT * D)[:, tt * D:(tt + 1) * D],
            lambda tt: [tk_f] if tt == 0 else [])
        tR4 = emit_ln_cols(tsum, 56, 60)
        tk_x2 = None
        for tt in range(NTT):
            tq = sc.emit("vector", lambda e, _, tt=tt: e.tensor_scalar(
                ar(E_A, D), ar(E_F, NTT * D)[:, tt * D:(tt + 1) * D],
                cols[:, 56 + tt:57 + tt], cols[:, 60 + tt:61 + tt],
                ALU.subtract, ALU.mult), waits=[tR4], inc=True)
            tk_x2 = sc.emit("vector", lambda e, _, tt=tt: e.tensor_tensor(
                x_sb[:, tt * D:(tt + 1) * D], x_sb[:, tt * D:(tt + 1) * D],
                ar(E_A, D), ALU.add), waits=[tq], inc=True)

        tk_out_dma = sc.dma(
            out_h[r0:r0 + T, :].rearrange("(tt p) d -> p tt d", p=128),
            x_sb.rearrange("p (tt d) -> p tt d", d=D),
            waits=[tk_x2])

    finals = {ch: sc.cnt[ch] for ch in DMA_CH}

    with contextlib.ExitStack() as es:
        e = es.enter_context
        sems = {eng: e(nc.semaphore(f"sem_{eng}")) for eng in ENGS}
        dma_sems = {ch: e(nc.semaphore(f"sem_{ch}")) for ch in DMA_CH}
        block = e(nc.Block())

        @block.tensor
        def _(h):
            sc.replay("tensor", h, sems, dma_sems)

        @block.vector
        def _(h):
            sc.replay("vector", h, sems, dma_sems)

        @block.scalar
        def _(h):
            sc.replay("scalar", h, sems, dma_sems)

        @block.gpsimd
        def _(h):
            sc.replay("gpsimd", h, sems, dma_sems)

        @block.sync
        def _(h):
            sc.replay("sync", h, sems, dma_sems)
            for ch in DMA_CH:
                if finals[ch]:
                    h.wait_ge(dma_sems[ch], finals[ch])

    return nc


_CACHE = {}


def _get_nc():
    if "nc" not in _CACHE:
        _CACHE["nc"] = build_program()
    return _CACHE["nc"]


def make_inputs(L0_attn_x, L0_attn_ln_gamma, L0_attn_ln_beta, L0_attn_in_proj_w,
                L0_attn_conv_w, L0_attn_conv_b, L0_attn_x_proj_w,
                L0_attn_dt_proj_w, L0_attn_dt_proj_b, L0_attn_A_log, L0_attn_D,
                L0_attn_out_proj_w, L0_ffn_w_fc, L0_ffn_b_fc, L0_ffn_w_proj,
                L0_ffn_b_proj):
    import ml_dtypes

    def np32(a):
        return np.ascontiguousarray(np.asarray(a), dtype=np.float32)

    x = np32(L0_attn_x)

    def colsify(v, nb):
        return np.ascontiguousarray(np32(v).reshape(nb, 128).T)

    common = {
        "gammaC": colsify(L0_attn_ln_gamma, KD),
        "betaC": colsify(L0_attn_ln_beta, KD),
        "winT": np.ascontiguousarray(np32(L0_attn_in_proj_w).T),
        "cwC": np.ascontiguousarray(
            np32(L0_attn_conv_w).reshape(NEB, 128, 4).transpose(1, 0, 2)
            .reshape(128, NEB * 4)),
        "cbC": colsify(L0_attn_conv_b, NEB),
        "xpwT": np.ascontiguousarray(np32(L0_attn_x_proj_w).T),
        "dtwT": np.ascontiguousarray(np32(L0_attn_dt_proj_w).T),
        "dtbC": colsify(L0_attn_dt_proj_b, NEB),
        "alogC": np.ascontiguousarray(
            np32(L0_attn_A_log).reshape(NEB, 128, S).transpose(1, 0, 2)
            .reshape(128, NEB * S)),
        "dC": colsify(L0_attn_D, NEB),
        "woutT": np.ascontiguousarray(np32(L0_attn_out_proj_w).T),
        "wfc": np32(L0_ffn_w_fc),
        "bfc": np.ascontiguousarray(np32(L0_ffn_b_fc).reshape(1, HID)),
        "wproj": np32(L0_ffn_w_proj),
        "bproj": np.ascontiguousarray(np32(L0_ffn_b_proj).reshape(1, D)),
        "idf32": np.eye(128, dtype=np.float32),
        "idbf": np.eye(128, dtype=ml_dtypes.bfloat16),
        "onesf": np.ones((1, 128), dtype=np.float32),
    }
    return [dict(common, x=np.ascontiguousarray(x[b])) for b in range(8)]


def kernel(**inputs):
    in_maps = make_inputs(**inputs)
    nc = _get_nc()
    res = run_bass_kernel_spmd(nc, in_maps, core_ids=list(range(8)))
    return np.stack([np.asarray(r["out"], dtype=np.float32)
                     for r in res.results], axis=0)


# version 19
# speedup vs baseline: 1.1276x; 1.1276x over previous
import contextlib
import numpy as np
import concourse.bass as bass
import concourse.mybir as mybir
from concourse.bass_utils import run_bass_kernel_spmd

F32 = mybir.dt.float32
F32R = mybir.dt.float32r
BF16 = mybir.dt.bfloat16
I32 = mybir.dt.int32
AF = mybir.ActivationFunctionType
ALU = mybir.AluOpType
AX = mybir.AxisListType

L, D, DI, S, RK, HID = 4096, 1024, 2048, 16, 64, 4096
T = 512
NCH = L // T
NTT = T // 128
KD = D // 128
NEB = DI // 128
SG = 2
SH = 4
EPS = 1e-5
CMUL = 1.5 * (2.0 ** 20)
TINY = 1.17549435e-38

ENGS = ("tensor", "vector", "scalar", "gpsimd", "sync")


DMA_CH = ("cio", "cw0", "cw1", "cp")


class Sched:

    def __init__(self):
        self.q = {e: [] for e in ENGS}
        self.cnt = {e: 0 for e in ENGS}
        for ch in DMA_CH:
            self.cnt[ch] = 0

    def _res(self, waits):
        out = []
        for w in waits:
            if w is None:
                continue
            owner, c = w
            if owner in DMA_CH:
                c = self.cnt[owner]
            out.append((owner, c))
        return tuple(out)

    def emit(self, eng, fn, waits=(), inc=False):
        if inc:
            self.cnt[eng] += 1
        self.q[eng].append((self._res(waits), fn, bool(inc)))
        return (eng, self.cnt[eng]) if inc else None

    def dma(self, out, in_, waits=(), ch="cio"):
        rw = self._res(waits)
        self.cnt[ch] += 16

        def fn(e, _):
            return e.dma_start(out=out, in_=in_)

        self.q["sync"].append((rw, fn, ch))
        return (ch, self.cnt[ch])

    def tok(self, eng):
        return (eng, self.cnt[eng])

    def replay(self, eng, handle, sems, dma_sems):
        waited = {}
        for waits, fn, inc in self.q[eng]:
            for owner, c in waits:
                if c <= 0 or waited.get(owner, 0) >= c:
                    continue
                waited[owner] = c
                handle.wait_ge(dma_sems[owner] if owner in DMA_CH
                               else sems[owner], c)
            inst = fn(handle, inc)
            if inc in DMA_CH:
                inst.then_inc(dma_sems[inc], 16)
            elif inc:
                inst.then_inc(sems[eng], 1)


def build_program(debug=None, nch=NCH, pstop=8, psel=None):
    if psel is None:
        psel = set(range(1, pstop + 1))
    nc = bass.Bass("TRN2", target_bir_lowering=False)
    sc = Sched()

    x_h = nc.dram_tensor("x", [L, D], F32, kind="ExternalInput").ap()
    out_h = nc.dram_tensor("out", [L, D], F32, kind="ExternalOutput").ap()
    gammaC_h = nc.dram_tensor("gammaC", [128, KD], F32, kind="ExternalInput").ap()
    betaC_h = nc.dram_tensor("betaC", [128, KD], F32, kind="ExternalInput").ap()
    winT_h = nc.dram_tensor("winT", [D, 2 * DI], F32, kind="ExternalInput").ap()
    cwC_h = nc.dram_tensor("cwC", [128, NEB * 4], F32, kind="ExternalInput").ap()
    cbC_h = nc.dram_tensor("cbC", [128, NEB], F32, kind="ExternalInput").ap()
    xpwT_h = nc.dram_tensor("xpwT", [DI, 96], F32, kind="ExternalInput").ap()
    dtwT_h = nc.dram_tensor("dtwT", [RK, DI], F32, kind="ExternalInput").ap()
    dtbC_h = nc.dram_tensor("dtbC", [128, NEB], F32, kind="ExternalInput").ap()
    alogC_h = nc.dram_tensor("alogC", [128, NEB * S], F32, kind="ExternalInput").ap()
    dC_h = nc.dram_tensor("dC", [128, NEB], F32, kind="ExternalInput").ap()
    woutT_h = nc.dram_tensor("woutT", [DI, D], F32, kind="ExternalInput").ap()
    wfc_h = nc.dram_tensor("wfc", [HID, D], F32, kind="ExternalInput").ap()
    bfc_h = nc.dram_tensor("bfc", [1, HID], F32, kind="ExternalInput").ap()
    wproj_h = nc.dram_tensor("wproj", [D, HID], F32, kind="ExternalInput").ap()
    bproj_h = nc.dram_tensor("bproj", [1, D], F32, kind="ExternalInput").ap()
    idf32_h = nc.dram_tensor("idf32", [128, 128], F32, kind="ExternalInput").ap()
    idbf_h = nc.dram_tensor("idbf", [128, 128], BF16, kind="ExternalInput").ap()
    onesf_h = nc.dram_tensor("onesf", [1, 128], F32, kind="ExternalInput").ap()

    dbg_h = None
    if debug is not None:
        dbg_h = nc.dram_tensor("dbg", [128, 16384], F32,
                               kind="ExternalOutput").ap()

    s_inh = nc.dram_tensor("s_inh", [D, 2 * DI], F32R, kind="Internal").ap()
    s_inl = nc.dram_tensor("s_inl", [D, 2 * DI], F32R, kind="Internal").ap()
    s_outh = nc.dram_tensor("s_outh", [DI, D], F32R, kind="Internal").ap()
    s_outl = nc.dram_tensor("s_outl", [DI, D], F32R, kind="Internal").ap()
    s_fcq = nc.dram_tensor("s_fcq", [D, HID], BF16, kind="Internal").ap()
    s_pjq = nc.dram_tensor("s_pjq", [HID, D], BF16, kind="Internal").ap()
    s_xpwh = nc.dram_tensor("s_xpwh", [128, NEB * 96], F32R, kind="Internal").ap()
    s_xpwl = nc.dram_tensor("s_xpwl", [128, NEB * 96], F32R, kind="Internal").ap()
    s_bfcB = nc.dram_tensor("s_bfcB", [128, HID], F32, kind="Internal").ap()
    s_bpjB = nc.dram_tensor("s_bpjB", [128, D], F32, kind="Internal").ap()
    s_bm = nc.dram_tensor("s_bm", [32, T], F32, kind="Internal").ap()
    s_sz = nc.dram_tensor("s_sz", [NEB * 128, T], F32, kind="Internal").ap()

    uTh = nc.alloc_sbuf_tensor("uTh", [128, KD * T], F32R).ap()
    uTl = nc.alloc_sbuf_tensor("uTl", [128, KD * T], F32R).ap()
    x_sb = nc.alloc_sbuf_tensor("x_sb", [128, NTT * D], F32).ap()
    wshb = [nc.alloc_sbuf_tensor(f"wsh{i}", [128, 1024], F32R).ap()
            for i in range(2)]
    wslb = [nc.alloc_sbuf_tensor(f"wsl{i}", [128, 1024], F32R).ap()
            for i in range(2)]
    dtw_hi = nc.alloc_sbuf_tensor("dtw_hi", [64, DI], F32R).ap()
    dtw_lo = nc.alloc_sbuf_tensor("dtw_lo", [64, DI], F32R).ap()
    id_f32 = nc.alloc_sbuf_tensor("id_f32", [128, 128], F32).ap()
    id_bf = nc.alloc_sbuf_tensor("id_bf", [128, 128], BF16).ap()
    onesf = nc.alloc_sbuf_tensor("onesf_sb", [1, 128], F32).ap()
    acol = nc.alloc_sbuf_tensor("acol", [128, NEB * S], F32).ap()
    dcol = nc.alloc_sbuf_tensor("dcol", [128, NEB], F32).ap()
    cwcol = nc.alloc_sbuf_tensor("cwcol", [128, NEB * 4], F32).ap()
    cbcol = nc.alloc_sbuf_tensor("cbcol", [128, NEB], F32).ap()
    dtbcol = nc.alloc_sbuf_tensor("dtbcol", [128, NEB], F32).ap()
    dtb2col = nc.alloc_sbuf_tensor("dtb2col", [128, NEB], F32).ap()
    halfc = nc.alloc_sbuf_tensor("halfc", [128, 2], F32).ap()
    gcol = nc.alloc_sbuf_tensor("gcol", [128, KD], F32).ap()
    bcol = nc.alloc_sbuf_tensor("bcol", [128, KD], F32).ap()
    wbeta = nc.alloc_sbuf_tensor("wbeta", [128, 32], F32).ap()
    carry = nc.alloc_sbuf_tensor("carry", [128, 256], F32).ap()
    tails = nc.alloc_sbuf_tensor("tails", [128, NEB * 3], F32).ap()
    cols = nc.alloc_sbuf_tensor("cols", [128, 64], F32).ap()
    xch_sb = nc.alloc_sbuf_tensor("xch_sb", [128, T], F32R).ap()
    xcl_sb = nc.alloc_sbuf_tensor("xcl_sb", [128, T], F32R).ap()
    dtrh_sb = nc.alloc_sbuf_tensor("dtrh_sb", [64, T], F32R).ap()
    dtrl_sb = nc.alloc_sbuf_tensor("dtrl_sb", [64, T], F32R).ap()
    yph_sb, ypl_sb = xch_sb, xcl_sb
    xpwh_sb = nc.alloc_sbuf_tensor("xpwh_sb", [128, 768], F32R).ap()
    xpwl_sb = nc.alloc_sbuf_tensor("xpwl_sb", [128, 768], F32R).ap()

    AR_N = 26624
    arena = nc.alloc_sbuf_tensor("arena", [128, AR_N], F32).ap()

    def ar(off, n, dt=F32):
        a = arena[:, off:off + n]
        if dt is not F32:
            a = a.bitcast(dt)
        return a

    A_U32, A_U32T, A_TRASH = 0, 4096, 8192
    A_XC, A_Y = 0, 8192
    A_NBM, A_NCM = 16384, 18432
    A_DBUH = 20480
    A_DT, A_DTU, A_DA = 22528, 23040, 23552
    A_SP1, A_SP2 = 24064, 24576
    A_DA2 = 26112
    A_DTR32 = 16384
    A_XIN = 16896
    A_AB = 8192
    A_T2 = 24064
    A_SZ = 25088
    A_YPH, A_YPL = 20480, 20992
    E_A, E_F = 0, 4096
    E_X2Q, E_X2QT = 8192, 10240
    E_HQ = 10752
    E_HQT = 12800
    E_GB, E_GB2 = 14848, 15360
    E_BFC, E_BPJ = 16384, 20480
    E_QA, E_QC = 21504, 21760
    E_WFC = 22016
    E_WPJ = 22016
    W0, W1, W2 = 0, 4096, 8192
    P_QA, P_QC = 12288, 12544

    bmcm = ar(A_Y, T)

    Pb = [nc.alloc_psum_tensor(f"pb{i}", [128, 512], F32).ap() for i in range(8)]

    def act(eng_ignore, out, in_, func, waits, inc=True, **kw):
        if func is AF.Copy and any(
                not isinstance(kw.get(k, 0.0), (int, float))
                for k in ("bias", "scale")):
            func = AF.Identity
        return sc.emit("scalar", lambda e, _: e.activation(out, in_, func, **kw),
                       waits=waits, inc=inc)

    def emit_ln_cols(sum_tok, mud, rd, gd=None):
        S0, S1, S2, S3 = 20, 24, 28, 32
        t = sc.emit("vector", lambda e, _: e.tensor_scalar_mul(
            cols[:, mud:mud + 4], cols[:, 0:4], 1.0 / D), [sum_tok], inc=True)
        t = sc.emit("vector", lambda e, _: e.tensor_scalar_mul(
            cols[:, S0:S0 + 4], cols[:, 4:8], 1.0 / D), [t], inc=True)
        t = sc.emit("vector", lambda e, _: e.tensor_tensor(
            cols[:, S1:S1 + 4], cols[:, mud:mud + 4], cols[:, mud:mud + 4],
            ALU.mult), [t], inc=True)
        t = sc.emit("vector", lambda e, _: e.tensor_tensor(
            cols[:, S0:S0 + 4], cols[:, S0:S0 + 4], cols[:, S1:S1 + 4],
            ALU.subtract), [t], inc=True)
        tva = sc.emit("vector", lambda e, _: e.tensor_scalar_add(
            cols[:, S1:S1 + 4], cols[:, S0:S0 + 4], EPS), [t], inc=True)
        t = act(0, cols[:, S2:S2 + 4], cols[:, S1:S1 + 4], AF.Ln, [tva])
        tr = act(0, cols[:, rd:rd + 4], cols[:, S2:S2 + 4], AF.Exp, [t],
                 scale=-0.5)
        if gd is None:
            return tr
        t = sc.emit("vector", lambda e, _: e.reciprocal(
            cols[:, S2:S2 + 4], cols[:, S1:S1 + 4]), [tva, tr], inc=True)
        t = sc.emit("vector", lambda e, _: e.tensor_tensor(
            cols[:, S2:S2 + 4], cols[:, S0:S0 + 4], cols[:, S2:S2 + 4],
            ALU.mult), [t], inc=True)
        t = sc.emit("vector", lambda e, _: e.tensor_scalar_add(
            cols[:, S2:S2 + 4], cols[:, S2:S2 + 4], EPS), [t], inc=True)
        t = act(0, cols[:, S3:S3 + 4], cols[:, S2:S2 + 4], AF.Ln, [t])
        t = act(0, cols[:, S3:S3 + 4], cols[:, S3:S3 + 4], AF.Exp, [t],
                scale=-0.5)
        tg = sc.emit("vector", lambda e, _: e.tensor_tensor(
            cols[:, gd:gd + 4], cols[:, rd:rd + 4], cols[:, S3:S3 + 4],
            ALU.mult), [t, tr], inc=True)
        return tg

    def emit_ln_sums(src_fn, waits_fn, trash=None):
        toks = []
        to = A_TRASH if trash is None else trash
        for tt in range(NTT):
            t1 = act(0, ar(to, D), src_fn(tt), AF.Copy,
                     waits_fn(tt), accum_out=cols[:, tt:tt + 1])
            t2 = act(0, ar(to, D), src_fn(tt), AF.Square, [t1],
                     accum_out=cols[:, 4 + tt:5 + tt])
            toks.append(t2)
        return toks[-1]

    def emit_quant(src_ap, nblk, out_ap, waits, qa, qc):
        t1 = sc.emit("vector", lambda e, _: e.tensor_reduce(
            ar(qa, nblk)[:, :, None],
            src_ap.rearrange("p (n b) -> p n b", b=32),
            AX.X, ALU.max, apply_absolute_value=True), waits=waits, inc=True)
        t2 = sc.emit("vector", lambda e, _: e.tensor_scalar_max(
            ar(qa, nblk), ar(qa, nblk), TINY), waits=[t1], inc=True)
        t3 = sc.emit("vector", lambda e, _: e.tensor_scalar(
            ar(qc, nblk).bitcast(I32), ar(qa, nblk).bitcast(I32),
            int(0x7F800000), None, ALU.bitwise_and), waits=[t2], inc=True)
        t4 = sc.emit("vector", lambda e, _: e.tensor_scalar_mul(
            ar(qc, nblk), ar(qc, nblk), CMUL), waits=[t3], inc=True)
        cb = ar(qc, nblk)[:, :, None].broadcast_to([128, nblk, 32])
        t5 = sc.emit("vector", lambda e, _: e.tensor_tensor(
            src_ap.rearrange("p (n b) -> p n b", b=32),
            src_ap.rearrange("p (n b) -> p n b", b=32), cb, ALU.add),
            waits=[t4], inc=True)
        t6 = sc.emit("vector", lambda e, _: e.tensor_tensor(
            out_ap.rearrange("p (n b) -> p n b", b=32),
            src_ap.rearrange("p (n b) -> p n b", b=32), cb, ALU.subtract),
            waits=[t5], inc=True)
        return t6

    tk_c = tk_alog = tk_acol = tk_dtw = tk_xpw = None
    tk_bfcB = tk_bpjB = tk_wbeta = None
    tk_pdma = tk_ppe = tk_pact = tk_ev = None
    if psel is None or 11 in psel or len(psel & {2,3,4,5,6,7,8}) > 0 and 11 not in psel and 12 not in psel:
        for dst, src in ((gcol, gammaC_h), (bcol, betaC_h), (cwcol, cwC_h),
                         (cbcol, cbC_h), (dtbcol, dtbC_h), (dcol, dC_h),
                         (id_f32, idf32_h), (id_bf, idbf_h), (onesf, onesf_h)):
            tk_c = sc.dma(dst, src)
    if psel is None or 12 in psel or len(psel & {2,3,4,5,6,7,8}) > 0 and 11 not in psel and 12 not in psel:
        tk_alog = sc.dma(ar(A_TRASH, NEB * S), alogC_h)
        t = act(0, acol, ar(A_TRASH, NEB * S), AF.Exp, [tk_alog])
        tk_acol = sc.emit("vector", lambda e, _: e.tensor_scalar_mul(
            dtb2col, dtbcol, -0.5), waits=[t, tk_c], inc=True)
        th_ = sc.emit("vector", lambda e, _: e.memset(
            halfc[:, 0:1], 0.5), waits=[tk_acol], inc=True)
        tk_half = sc.emit("vector", lambda e, _: e.memset(
            halfc[:, 1:2], -0.5), waits=[th_], inc=True)

    if 2 in psel:
        tkw = sc.dma(ar(A_U32, DI)[0:64, :], dtwT_h)
        t = act(0, dtw_hi, ar(A_U32, DI)[0:64, :], AF.Copy, [tkw])
        tk_dtw = sc.emit("vector", lambda e, _: e.tensor_tensor(
            dtw_lo, ar(A_U32, DI)[0:64, :], dtw_hi, ALU.subtract),
            waits=[t], inc=True)

    if 3 in psel or 31 in psel:
        tkw = sc.dma(ar(A_U32, NEB * 96),
                     xpwT_h.rearrange("(e p) j -> p e j", p=128), waits=[tk_dtw])
        tk_xpw = tkw
    if 3 in psel or 32 in psel:
        t = act(0, ar(A_U32T, NEB * 96, F32R), ar(A_U32, NEB * 96), AF.Copy, [tkw])
        tk_xpw = t
    if 3 in psel or 33 in psel:
        t2 = sc.emit("vector", lambda e, _: e.tensor_tensor(
            ar(A_TRASH, NEB * 96, F32R), ar(A_U32, NEB * 96),
            ar(A_U32T, NEB * 96, F32R), ALU.subtract), waits=[t], inc=True)
        tk_xpw = t2
    if 3 in psel or 34 in psel:
        sc.dma(s_xpwh, ar(A_U32T, NEB * 96, F32R), waits=[t])
        tk_xpw = sc.dma(s_xpwl, ar(A_TRASH, NEB * 96, F32R), waits=[t2])

    if 4 in psel:
        tk_bfc = sc.dma(ar(A_U32, HID)[0:1, :], bfc_h, waits=[tk_xpw])
        tk_bpj = sc.dma(ar(A_U32T, D)[0:1, :], bproj_h)
        te = None
        for j in range(8):
            tm = sc.emit("tensor", lambda e, _, j=j: e.matmul(
                Pb[0][:], onesf, ar(A_U32, HID)[0:1, j * 512:(j + 1) * 512],
                start=True, stop=True), waits=[tk_bfc, tk_c, te], inc=True)
            te = act(0, ar(A_TRASH, HID)[:, j * 512:(j + 1) * 512], Pb[0][:],
                     AF.Copy, [tm])
        tk_bfcB = sc.dma(s_bfcB, ar(A_TRASH, HID), waits=[te])
        te = None
        for j in range(2):
            tm = sc.emit("tensor", lambda e, _, j=j: e.matmul(
                Pb[1][:], onesf, ar(A_U32T, D)[0:1, j * 512:(j + 1) * 512],
                start=True, stop=True), waits=[tk_bpj, te], inc=True)
            te = act(0, ar(A_TRASH + HID, D)[:, j * 512:(j + 1) * 512], Pb[1][:],
                     AF.Copy, [tm])
        tk_bpjB = sc.dma(s_bpjB, ar(A_TRASH + HID, D), waits=[te])

    if 5 in psel:
        tk_pdma = tk_bpjB
        tk_ppe = None
        tk_pact = None
        for kd in range(KD):
            tkw = sc.dma(ar(W0, 2 * DI), winT_h[kd * 128:(kd + 1) * 128, :],
                         waits=[tk_pdma, tk_ppe, tk_pact, tk_bfcB])
            for eb in range(32):
                tk_ppe = sc.emit("tensor", lambda e, _, kd=kd, eb=eb: e.matmul(
                    Pb[2][:, eb:eb + 1], ar(W0, 2 * DI)[:, eb * 128:(eb + 1) * 128],
                    bcol[:, kd:kd + 1], start=(kd == 0 and eb == 0),
                    stop=(kd == KD - 1)),
                    waits=[tkw], inc=(eb == 31))
            tg = act(0, ar(W1, 2 * DI), ar(W0, 2 * DI), AF.Copy, [tkw],
                     scale=gcol[:, kd:kd + 1])
            th = act(0, ar(W2, 2 * DI, F32R), ar(W1, 2 * DI), AF.Copy, [tg])
            tk_pact = th
            tl = sc.emit("vector", lambda e, _: e.tensor_tensor(
                ar(W1, 2 * DI, F32R), ar(W1, 2 * DI), ar(W2, 2 * DI, F32R),
                ALU.subtract), waits=[th], inc=True)
            sc.dma(s_inh[kd * 128:(kd + 1) * 128, :], ar(W2, 2 * DI, F32R),
                   waits=[th])
            tk_pdma = sc.dma(s_inl[kd * 128:(kd + 1) * 128, :],
                             ar(W1, 2 * DI, F32R), waits=[tl])
        tk_wbeta = act(0, wbeta, Pb[2][:, 0:32], AF.Copy, [tk_ppe])

    if 6 in psel:
        for ed in range(NEB):
            tkw = sc.dma(ar(W0, D), woutT_h[ed * 128:(ed + 1) * 128, :],
                         waits=[tk_pdma, tk_pact])
            th = act(0, ar(W1, D, F32R), ar(W0, D), AF.Copy, [tkw])
            tk_pact = th
            tl = sc.emit("vector", lambda e, _: e.tensor_tensor(
                ar(W2, D, F32R), ar(W0, D), ar(W1, D, F32R), ALU.subtract),
                waits=[th], inc=True)
            sc.dma(s_outh[ed * 128:(ed + 1) * 128, :], ar(W1, D, F32R), waits=[th])
            tk_pdma = sc.dma(s_outl[ed * 128:(ed + 1) * 128, :],
                             ar(W2, D, F32R), waits=[tl])

    if 7 in psel:
        tk_ev = None
        for nb in range(32):
            tkw = sc.dma(ar(W0, D), wfc_h[nb * 128:(nb + 1) * 128, :],
                         waits=[tk_pdma, tk_pact])
            tk_pact = None
            tq = emit_quant(ar(W0, D), 32, ar(W1, 512, BF16), [tkw], P_QA, P_QC)
            tp = None
            for kb in range(8):
                tp = sc.emit("tensor", lambda e, _, kb=kb: e.transpose(
                    Pb[3].bitcast(BF16)[:, kb * 128:(kb + 1) * 128],
                    ar(W1, 512, BF16)[:, kb * 128:(kb + 1) * 128], id_bf),
                    waits=[tq, tk_ev] if kb == 0 else [], inc=(kb == 7))
            tk_ev = sc.emit("vector", lambda e, _: e.tensor_copy(
                ar(W2, 512, BF16), Pb[3].bitcast(BF16)[:, 0:1024]),
                waits=[tp], inc=True)
            tk_pdma = sc.dma(
                s_fcq.rearrange("(kb p) n -> p kb n", p=128)[
                    :, :, nb * 128:(nb + 1) * 128],
                ar(W2, 512, BF16).rearrange("p (kb n) -> p kb n", n=128),
                waits=[tk_ev])

    if 8 in psel:
        for eb in range(8):
            tkw = sc.dma(ar(W0, HID), wproj_h[eb * 128:(eb + 1) * 128, :],
                         waits=[tk_pdma])
            tq = emit_quant(ar(W0, HID), 128, ar(W1, 2048, BF16), [tkw], P_QA, P_QC)
            tlast = tq
            for rnd in range(4):
                tp = None
                for k in range(8):
                    hb = rnd * 8 + k
                    tp = sc.emit("tensor", lambda e, _, hb=hb, k=k: e.transpose(
                        Pb[3].bitcast(BF16)[:, k * 128:(k + 1) * 128],
                        ar(W1, 2048, BF16)[:, hb * 128:(hb + 1) * 128], id_bf),
                        waits=[tlast, tk_ev] if k == 0 else [], inc=(k == 7))
                tk_ev = sc.emit("vector", lambda e, _, rnd=rnd: e.tensor_copy(
                    ar(W2, 2048, BF16)[:, rnd * 1024:(rnd + 1) * 1024],
                    Pb[3].bitcast(BF16)[:, 0:1024]), waits=[tp], inc=True)
                tlast = tk_ev
            tk_pdma = sc.dma(
                s_pjq.rearrange("(hb p) e -> p hb e", p=128)[
                    :, :, eb * 128:(eb + 1) * 128],
                ar(W2, 2048, BF16).rearrange("p (hb e) -> p hb e", e=128),
                waits=[tlast])

    if 5 not in psel:
        tk_pdma = None
        tk_ev = None
    tk_prologue = tk_pdma
    tk_prologue_ev = tk_ev if 7 in psel else None

    tk_out_dma = tk_prologue

    for c in range(nch):
        r0 = c * T

        tk_x = sc.dma(
            x_sb.rearrange("p (tt d) -> p tt d", d=D),
            x_h[r0:r0 + T, :].rearrange("(tt p) d -> p tt d", p=128),
            waits=[tk_out_dma])

        tsum = emit_ln_sums(
            lambda tt: x_sb[:, tt * D:(tt + 1) * D],
            lambda tt: [tk_x, tk_wbeta, tk_acol] if tt == 0 else [])
        tg1 = emit_ln_cols(tsum, 8, 12, gd=16)

        tk_pe_prev = tk_wbeta if c == 0 else None
        tk_u = None
        for tt in range(NTT):
            tu = sc.emit("vector", lambda e, _, tt=tt: e.tensor_scalar(
                ar(A_U32, NTT * D)[:, tt * D:(tt + 1) * D],
                x_sb[:, tt * D:(tt + 1) * D],
                cols[:, 8 + tt:9 + tt], cols[:, 16 + tt:17 + tt],
                ALU.subtract, ALU.mult), waits=[tg1], inc=True)
            for grp in range(2):
                tp = None
                for k in range(4):
                    kd = grp * 4 + k
                    tp = sc.emit("tensor", lambda e, _, tt=tt, kd=kd, k=k:
                        e.transpose(
                            Pb[2][:, k * 128:(k + 1) * 128],
                            ar(A_U32, NTT * D)[
                                :, tt * D + kd * 128:tt * D + (kd + 1) * 128],
                            id_f32),
                        waits=[tu, tk_pe_prev] if k == 0 else [], inc=(k == 3))
                uh_dst = uTh.rearrange("p (kd t) -> p kd t", t=T)[
                    :, grp * 4:(grp + 1) * 4, tt * 128:(tt + 1) * 128]
                u32_dst = ar(A_U32T, NTT * D).rearrange(
                    "p (kd t) -> p kd t", t=T)[
                    :, grp * 4:(grp + 1) * 4, tt * 128:(tt + 1) * 128]
                psrc = Pb[2].rearrange("p (k t) -> p k t", t=128)
                th = act(0, uh_dst, psrc, AF.Copy, [tp])
                t32 = act(0, u32_dst, psrc, AF.Copy, [th])
                tk_pe_prev = t32
                tk_u = t32
        tk_ulo = sc.emit("vector", lambda e, _: e.tensor_tensor(
            uTl, ar(A_U32T, KD * T), uTh, ALU.subtract), waits=[tk_u], inc=True)

        if debug == "u" and c == 0:
            sc.dma(dbg_h[:, 0:NTT * D], ar(A_U32, NTT * D), waits=[tk_ulo])
            sc.dma(dbg_h[:, NTT * D:2 * NTT * D], ar(A_U32T, NTT * D),
                   waits=[tk_ulo])
            sc.dma(dbg_h[:, 2 * NTT * D:2 * NTT * D + 64], cols,
                   waits=[tk_ulo])

        tk_xpwl = None

        if c == 0:
            tk_tails = sc.emit("vector", lambda e, _: e.memset(tails, 0.0),
                               waits=[], inc=True)
        tk_wfree = {0: None, 1: None}
        tk_evac = {0: None, 1: None}
        tk_xp_last = None
        tk_xp_prev = None
        tk_xc = []
        xin = ar(A_XIN, 530)
        for eb in range(NEB):
            bank = eb % 2
            if eb % 8 == 0:
                hw0 = (eb // 8) * 768
                tk_xpwl = sc.dma(xpwh_sb, s_xpwh[:, hw0:hw0 + 768],
                                 waits=[tk_xpw, tk_xp_prev])
                tk_xpwl = sc.dma(xpwl_sb, s_xpwl[:, hw0:hw0 + 768],
                                 waits=[tk_xpwl])
            wsh, wsl = wshb[bank], wslb[bank]
            tw1 = sc.dma(
                wsh.rearrange("p (kd e) -> p kd e", e=128),
                s_inh.rearrange("(kd p) e -> p kd e", p=128)[
                    :, :, eb * 128:(eb + 1) * 128],
                waits=[tk_wfree[bank],
                       tk_prologue if (c == 0 and eb == 0) else None],
                ch="cw0")
            tw2 = sc.dma(
                wsl.rearrange("p (kd e) -> p kd e", e=128),
                s_inl.rearrange("(kd p) e -> p kd e", p=128)[
                    :, :, eb * 128:(eb + 1) * 128],
                waits=[tk_wfree[bank]], ch="cw1")
            tmm = None
            for p, (wa, ua) in enumerate(((wsh, uTh), (wsh, uTl), (wsl, uTh))):
                for kd in range(KD):
                    tmm = sc.emit("tensor", lambda e, _, wa=wa, ua=ua, kd=kd,
                                  p=p, bank=bank: e.matmul(
                        Pb[bank][:], wa[:, kd * 128:(kd + 1) * 128],
                        ua[:, kd * T:(kd + 1) * T],
                        start=(p == 0 and kd == 0),
                        stop=(p == 2 and kd == KD - 1)),
                        waits=([tw1, tw2, tk_ulo, tk_evac[bank]]
                               if (p == 0 and kd == 0) else []),
                        inc=(p == 2 and kd == KD - 1))
            tk_wfree[bank] = tmm
            if debug == "xin" and c == 0 and eb == 0:
                trec = sc.emit("vector", lambda e, _: e.tensor_tensor(
                    ar(A_Y, 1024), wshb[0], wslb[0], ALU.add), waits=[tmm], inc=True)
                sc.dma(dbg_h[:, 2048:3072], ar(A_Y, 1024), waits=[trec])
            thal = sc.emit("vector", lambda e, _, eb=eb: e.tensor_copy(
                xin[:, 0:3], tails[:, eb * 3:(eb + 1) * 3]),
                waits=[tk_tails if c == 0 else None], inc=True)
            tev = act(0, xin[:, 3:3 + T], Pb[bank][:], AF.Copy, [tmm, thal],
                      bias=wbeta[:, eb:eb + 1])
            tk_evac[bank] = tev
            sc.emit("vector", lambda e, _, eb=eb: e.tensor_copy(
                tails[:, eb * 3:(eb + 1) * 3], xin[:, T:T + 3]),
                waits=[tev], inc=True)
            tcv = sc.emit("vector", lambda e, _, eb=eb: e.tensor_scalar_mul(
                ar(A_SP1, T), xin[:, 0:T], cwcol[:, eb * 4:eb * 4 + 1]),
                waits=[tev], inc=True)
            for k in range(1, 4):
                tcv = sc.emit("vector", lambda e, _, eb=eb, k=k:
                    e.scalar_tensor_tensor(
                        ar(A_SP1, T), xin[:, k:k + T],
                        cwcol[:, eb * 4 + k:eb * 4 + k + 1], ar(A_SP1, T),
                        ALU.mult, ALU.add), waits=[], inc=(k == 3))
            if debug == "xin" and c == 0 and eb == 0:
                sc.dma(dbg_h[:, 0:530], xin, waits=[tcv])
                sc.dma(dbg_h[:, 530:530 + 512], ar(A_SP1, T), waits=[tcv])
            xcs = ar(A_XC, NEB * T)[:, eb * T:(eb + 1) * T]
            txc = act(0, xcs, ar(A_SP1, T), AF.Silu, [tcv],
                      bias=cbcol[:, eb:eb + 1])
            tk_xc.append(txc)
            tph = act(0, xch_sb, xcs, AF.Copy, [txc, tk_xp_prev])
            tpl = sc.emit("vector", lambda e, _, xcs=xcs: e.tensor_tensor(
                xcl_sb, xcs, xch_sb, ALU.subtract),
                waits=[tph], inc=True)
            for p, wa, ua in ((0, xpwh_sb, xch_sb),
                              (1, xpwh_sb, xcl_sb),
                              (2, xpwl_sb, xch_sb)):
                tk_xp_last = sc.emit("tensor",
                    lambda e, _, eb=eb, p=p, wa=wa, ua=ua: e.matmul(
                        Pb[6][0:96, :],
                        wa[:, (eb % 8) * 96:(eb % 8 + 1) * 96], ua,
                        start=(eb == 0 and p == 0),
                        stop=(eb == NEB - 1 and p == 2)),
                    waits=([tpl, tk_xpwl] if p == 0 else []),
                    inc=(p == 2))
            tk_xp_prev = tk_xp_last

        tdr = act(0, ar(A_DTR32, T)[0:64, :], Pb[6][0:64, :], AF.Copy,
                  [tk_xp_last])
        tbc = act(0, bmcm[64:96, :], Pb[6][64:96, :], AF.Copy, [tk_xp_last])
        tdrh = act(0, dtrh_sb, ar(A_DTR32, T)[0:64, :], AF.Copy, [tdr])
        tdrl = sc.emit("vector", lambda e, _: e.tensor_tensor(
            dtrl_sb, ar(A_DTR32, T)[0:64, :], dtrh_sb,
            ALU.subtract), waits=[tdrh], inc=True)
        tk_bm_dma = sc.dma(s_bm, bmcm[64:96, :], waits=[tbc])

        if debug == "xc" and c == 0:
            sc.dma(dbg_h[:, 0:NEB * T], ar(A_XC, NEB * T), waits=[tk_xc[-1]])
            sc.dma(dbg_h[:, NEB * T:NEB * T + T], bmcm, waits=[tbc])
            sc.dma(dbg_h[:, NEB * T + T:NEB * T + 2 * T], ar(A_DTR32, T),
                   waits=[tdr])

        DT, DTU = ar(A_DT, T), ar(A_DTU, T)
        tk_scan_dv = None
        tk_dt_prev = None
        tk_dtu_prev = None
        tk_dtp_evac = {0: None, 1: None}
        tk_slotfree = {0: None, 1: None}
        tk_scan2 = {0: None, 1: None}
        tk_zevac = dict(tk_evac)
        tk_zout = {0: None, 1: None}
        tk_dbu_last = None
        gi = 0
        gi2 = 0

        def emit_z(eb):
            bank = eb % 2
            tw1 = sc.dma(
                wshb[bank].rearrange("p (kd e) -> p kd e", e=128),
                s_inh.rearrange("(kd p) e -> p kd e", p=128)[
                    :, :, (16 + eb) * 128:(17 + eb) * 128],
                waits=[tk_wfree[bank]], ch="cw0")
            tw2 = sc.dma(
                wslb[bank].rearrange("p (kd e) -> p kd e", e=128),
                s_inl.rearrange("(kd p) e -> p kd e", p=128)[
                    :, :, (16 + eb) * 128:(17 + eb) * 128],
                waits=[tk_wfree[bank]], ch="cw1")
            tmm = None
            for p, (wa, ua) in enumerate(((wshb[bank], uTh), (wshb[bank], uTl),
                                          (wslb[bank], uTh))):
                for kd in range(KD):
                    tmm = sc.emit("tensor", lambda e, _, wa=wa, ua=ua, kd=kd,
                                  p=p, bank=bank: e.matmul(
                        Pb[bank][:], wa[:, kd * 128:(kd + 1) * 128],
                        ua[:, kd * T:(kd + 1) * T],
                        start=(p == 0 and kd == 0),
                        stop=(p == 2 and kd == KD - 1)),
                        waits=([tw1, tw2, tk_zevac[bank]]
                               if (p == 0 and kd == 0) else []),
                        inc=(p == 2 and kd == KD - 1))
            tk_wfree[bank] = tmm
            szb = ar(A_SZ + bank * 512, T)
            tsz = act(0, szb, Pb[bank][:], AF.Silu, [tmm, tk_zout[bank]],
                      bias=wbeta[:, 16 + eb:17 + eb])
            tk_zevac[bank] = tsz
            tk_zout[bank] = sc.dma(s_sz[eb * 128:(eb + 1) * 128, :], szb,
                                   waits=[tsz], ch="cp")
        cst = {"pend": None, "y": None, "pool": None, "burst": None}

        def flush_pend():
            p = cst["pend"]
            if p is None:
                return
            cst["pend"] = None
            gs, slot, dbuh, tcar, hc_pool, db_p, first, tkncm = p
            eng = "gpsimd" if hc_pool else "vector"
            thc = sc.emit(eng, lambda e, _, gs=gs, dbuh=dbuh:
                e.tensor_tensor(
                    dbuh, dbuh,
                    ar(A_NCM, SH * T)[:, gs * T:(gs + SG) * T], ALU.mult),
                waits=[tcar, tkncm], inc=True)
            if hc_pool:
                cst["pool"] = thc
            ysl = ar(A_Y, NEB * T)[:, db_p * T:(db_p + 1) * T]
            ty0 = sc.emit("vector", lambda e, _, ysl=ysl, dbuh=dbuh,
                          first=first: (e.tensor_copy(
                              ysl, dbuh[:, 0:T]) if first else
                              e.tensor_tensor(ysl, ysl, dbuh[:, 0:T],
                                              ALU.add)),
                waits=[thc], inc=True)
            ty1 = sc.emit("vector", lambda e, _, ysl=ysl,
                          dbuh=dbuh: e.tensor_tensor(
                    ysl, ysl, dbuh[:, T:2 * T], ALU.add),
                waits=[ty0], inc=True)
            tk_slotfree[slot] = ty1
            cst["y"] = ty1

        for half in range(S // SH):
            flush_pend()
            nbm = ar(A_NBM, SH * T)
            ncm = ar(A_NCM, SH * T)
            tk_nbm = sc.dma(
                nbm.rearrange("p (s t) -> p s t", t=T),
                s_bm[half * SH:(half + 1) * SH, :].partition_broadcast(128),
                waits=[tk_bm_dma, cst["burst"], tdrl])
            tk_ncm = sc.dma(
                ncm.rearrange("p (s t) -> p s t", t=T),
                s_bm[16 + half * SH:16 + (half + 1) * SH, :]
                .partition_broadcast(128),
                waits=[tk_bm_dma, cst["y"]])
            for db in range(NEB):
                bank = db % 2
                sp1 = ar(A_SP1, T)
                sp2 = ar(A_SP2, T)
                tmm = None
                for p, (wa, ua) in enumerate((
                        (dtw_hi, dtrh_sb), (dtw_hi, dtrl_sb),
                        (dtw_lo, dtrh_sb))):
                    tmm = sc.emit("tensor", lambda e, _, wa=wa, ua=ua, db=db,
                                  p=p, bank=bank: e.matmul(
                        Pb[4 + bank][:], wa[:, db * 128:(db + 1) * 128], ua,
                        start=(p == 0), stop=(p == 2)),
                        waits=([tdrl, tk_dtp_evac[bank]] if p == 0 else []),
                        inc=(p == 2))
                t1 = act(0, sp1, Pb[4 + bank][:], AF.Tanh,
                         [tmm, tk_dt_prev,
                          tk_half if (c == 0 and half == 0 and db == 0)
                          else None],
                         scale=halfc[:, 1:2], bias=dtb2col[:, db:db + 1])
                tk_dtp_evac[bank] = t1
                tdt = act(0, DT, sp1, AF.Ln, [t1, tk_dtu_prev],
                          scale=halfc[:, 0:1], bias=halfc[:, 0:1])
                tk_dt_prev = tdt
                tdtu = sc.emit("vector", lambda e, _, db=db:
                    e.scalar_tensor_tensor(
                        DTU, DT, -1.0,
                        ar(A_XC, NEB * T)[:, db * T:(db + 1) * T],
                        ALU.mult, ALU.mult),
                    waits=[tdt, tk_xc[db], tk_dbu_last], inc=True)
                tk_dtu_prev = tdtu
                for g in range(SH // SG):
                    s0 = half * SH + g * SG
                    gs = g * SG
                    slot = gi % 2
                    dbuh = ar(A_DBUH + slot * (SG * T), SG * T)
                    flush_pend()
                    tdbu = sc.emit("gpsimd", lambda e, _, gs=gs, dbuh=dbuh:
                        e.tensor_tensor(
                            dbuh.rearrange("p (s t) -> p s t", t=T),
                            DTU.unsqueeze(1).broadcast_to([128, SG, T]),
                            ar(A_NBM, SH * T)[:, gs * T:(gs + SG) * T]
                            .rearrange("p (s t) -> p s t", t=T),
                            ALU.mult),
                        waits=[tdtu, tk_nbm, tk_slotfree[slot],
                               cst["burst"]], inc=True)
                    cst["pool"] = tdbu
                    tk_dbu_last = tdbu
                    tsc_ = None
                    for si in range(SG):
                        s = s0 + si
                        da_s = gi2 % 2
                        gi2 += 1
                        da = ar(A_DA if da_s == 0 else A_DA2, T)
                        tda = act(0, da, DT, AF.Exp,
                                  [tdt, tk_scan2[da_s]],
                                  scale=acol[:, db * S + s:db * S + s + 1])
                        hseg = dbuh[:, si * T:(si + 1) * T]
                        init = (0.0 if c == 0 else
                                carry[:, db * 16 + s:db * 16 + s + 1])
                        tsc_ = sc.emit("vector", lambda e, _, hseg=hseg,
                                       init=init, da=da: e.tensor_tensor_scan(
                            hseg, da, hseg, init, ALU.mult, ALU.add),
                            waits=[tda, cst["pool"] if si == 0 else None],
                            inc=True)
                        tk_scan2[da_s] = tsc_
                    cst["burst"] = tsc_
                    tcar = sc.emit("gpsimd", lambda e, _, db=db, s0=s0,
                                   dbuh=dbuh: e.tensor_copy(
                            carry[:, db * 16 + s0:db * 16 + s0 + SG],
                            dbuh.rearrange(
                                "p (s t) -> p s t", t=T)[:, :, T - 1]),
                        waits=[tsc_], inc=True)
                    cst["pend"] = (gs, slot, dbuh, tcar, False, db,
                                   half == 0 and g == 0, tk_ncm)
                    gi += 1
                if half < 2 and db % 2 == 0:
                    emit_z(half * 8 + db // 2)
        flush_pend()
        tk_scan_dv = cst["y"]

        if debug == "y" and c == 0:
            sc.dma(dbg_h[:, 0:NEB * T], ar(A_Y, NEB * T), waits=[tk_scan_dv])

        tk_evacD = {0: None, 1: None}
        tk_gate = []
        for eb in range(NEB):
            bank = eb % 2
            szb = ar(A_SZ + bank * 512, T)
            tsz = sc.dma(szb, s_sz[eb * 128:(eb + 1) * 128, :],
                         waits=[tk_evacD[bank]], ch="cp")
            ysl = ar(A_Y, NEB * T)[:, eb * T:(eb + 1) * T]
            xcs = ar(A_XC, NEB * T)[:, eb * T:(eb + 1) * T]
            tyt = sc.emit("vector", lambda e, _, ysl=ysl, xcs=xcs, eb=eb:
                e.scalar_tensor_tensor(
                    ysl, xcs, dcol[:, eb:eb + 1], ysl, ALU.mult, ALU.add),
                waits=[tsz, tk_scan_dv], inc=True)
            tyg = sc.emit("vector", lambda e, _, ysl=ysl, szb=szb:
                e.tensor_tensor(ysl, ysl, szb, ALU.mult),
                waits=[tyt], inc=True)
            tk_evacD[bank] = tyg
            tk_gate.append(tyg)

        if debug == "yg" and c == 0:
            sc.dma(dbg_h[:, 0:NEB * T], ar(A_Y, NEB * T), waits=[tk_gate[-1]])

        tk_op_mm = None
        for db in range(NEB):
            ysl = ar(A_Y, NEB * T)[:, db * T:(db + 1) * T]
            typh = act(0, yph_sb, ysl, AF.Copy,
                       [tk_gate[db], tk_gate[-1], tk_op_mm])
            typl = sc.emit("vector", lambda e, _, ysl=ysl: e.tensor_tensor(
                ypl_sb, ysl, yph_sb, ALU.subtract),
                waits=[typh], inc=True)
            bank = db % 2
            wsh, wsl = wshb[bank], wslb[bank]
            tw1 = sc.dma(wsh[:, 0:1024], s_outh[db * 128:(db + 1) * 128, :],
                         waits=[tk_wfree[bank]], ch="cw0")
            tw2 = sc.dma(wsl[:, 0:1024], s_outl[db * 128:(db + 1) * 128, :],
                         waits=[tk_wfree[bank]], ch="cw1")
            for p, (ya, wa) in enumerate((
                    (yph_sb, wshb[bank]), (ypl_sb, wshb[bank]),
                    (yph_sb, wslb[bank]))):
                for tt in range(NTT):
                    for eh in range(2):
                        tk_op_mm = sc.emit("tensor",
                            lambda e, _, ya=ya, wa=wa, tt=tt, eh=eh, p=p,
                            db=db: e.matmul(
                                Pb[tt * 2 + eh][:],
                                ya[:, tt * 128:(tt + 1) * 128],
                                wa[:, eh * 512:(eh + 1) * 512],
                                start=(db == 0 and p == 0),
                                stop=(db == NEB - 1 and p == 2)),
                            waits=([typl, tw1, tw2, tk_evacD[0], tk_evacD[1]]
                                   if (p == 0 and tt == 0 and eh == 0)
                                   else []),
                            inc=(p == 2 and tt == NTT - 1 and eh == 1))
            tk_wfree[bank] = tk_op_mm

        if debug == "m" and c == 0:
            for i in range(8):
                tcp = sc.emit("vector", lambda e, _, i=i: e.tensor_copy(
                    ar(A_NBM, 4096)[:, i * 512:(i + 1) * 512], Pb[i][:]),
                    waits=[tk_op_mm], inc=True)
            sc.dma(dbg_h[:, 0:4096], ar(A_NBM, 4096), waits=[tcp])

        tk_a = None
        for tt in range(NTT):
            tt2 = sc.emit("vector", lambda e, _, tt=tt: e.tensor_scalar(
                ar(A_T2, 1024), x_sb[:, tt * D:(tt + 1) * D],
                cols[:, 8 + tt:9 + tt], cols[:, 12 + tt:13 + tt],
                ALU.subtract, ALU.mult), waits=[tk_op_mm], inc=True)
            for eh in range(2):
                tk_a = sc.emit("vector", lambda e, _, tt=tt, eh=eh:
                    e.tensor_tensor(
                        ar(A_AB, NTT * D)[
                            :, tt * D + eh * 512:tt * D + (eh + 1) * 512],
                        ar(A_T2, 1024)[:, eh * 512:(eh + 1) * 512],
                        Pb[tt * 2 + eh][:], ALU.add), waits=[tt2], inc=True)
        if debug == "a" and c == 0:
            sc.dma(dbg_h[:, 0:NTT * D], ar(A_AB, NTT * D), waits=[tk_a])
        tsum = emit_ln_sums(
            lambda tt: ar(A_AB, NTT * D)[:, tt * D:(tt + 1) * D],
            lambda tt: [tk_a] if tt == 0 else [], trash=A_NBM)
        tRA = emit_ln_cols(tsum, 40, 44)
        tk_x1 = None
        for tt in range(NTT):
            tq = sc.emit("vector", lambda e, _, tt=tt: e.tensor_scalar(
                ar(A_NBM, D), ar(A_AB, NTT * D)[:, tt * D:(tt + 1) * D],
                cols[:, 40 + tt:41 + tt], cols[:, 44 + tt:45 + tt],
                ALU.subtract, ALU.mult), waits=[tRA], inc=True)
            tk_x1 = sc.emit("vector", lambda e, _, tt=tt: e.tensor_tensor(
                x_sb[:, tt * D:(tt + 1) * D], x_sb[:, tt * D:(tt + 1) * D],
                ar(A_NBM, D), ALU.add), waits=[tq], inc=True)

        if debug == "x1" and c == 0:
            tdmp = sc.dma(dbg_h[:, 0:NTT * D], x_sb, waits=[tk_x1])
            tk_x1 = sc.emit("vector", lambda e, _: e.engine_nop(),
                            waits=[tdmp], inc=True)

        tk_bfcL = sc.dma(ar(E_BFC, HID), s_bfcB, waits=[tk_bfcB, tk_x1])
        tk_bpjL = sc.dma(ar(E_BPJ, D), s_bpjB, waits=[tk_bpjB])
        tsum = emit_ln_sums(
            lambda tt: x_sb[:, tt * D:(tt + 1) * D],
            lambda tt: [tk_x1] if tt == 0 else [])
        tR3 = emit_ln_cols(tsum, 48, 52)

        tk_tpfree = None
        tk_fc_mm = {0: None, 1: None}
        tk_pj_mm = None
        tk_pjw = {0: None, 1: None}
        tk_f = None
        for tt in range(NTT):
            tq = sc.emit("vector", lambda e, _, tt=tt: e.tensor_scalar(
                ar(E_A, D), x_sb[:, tt * D:(tt + 1) * D],
                cols[:, 48 + tt:49 + tt], cols[:, 52 + tt:53 + tt],
                ALU.subtract, ALU.mult), waits=[tR3], inc=True)
            tq = emit_quant(
                ar(E_A, D), 32,
                ar(E_X2Q, NTT * 512, BF16)[:, tt * D:(tt + 1) * D],
                [tq], E_QA, E_QC)
            if debug == "ffn" and c == 0 and tt == 0:
                tq2_ = sc.dma(dbg_h.bitcast(BF16)[:, 16384:17408],
                              ar(E_X2Q, NTT * 512, BF16)[:, 0:D], waits=[tq])
                tq = sc.emit("vector", lambda e, _: e.engine_nop(),
                             waits=[tq2_], inc=True)
            tp = None
            for kd in range(KD):
                tp = sc.emit("tensor", lambda e, _, tt=tt, kd=kd: e.transpose(
                    Pb[3].bitcast(BF16)[:, kd * 128:(kd + 1) * 128],
                    ar(E_X2Q, NTT * 512, BF16)[
                        :, tt * D + kd * 128:tt * D + (kd + 1) * 128],
                    id_bf),
                    waits=[tq, tk_tpfree] if kd == 0 else [], inc=(kd == 7))
            tk_x2qT = sc.emit("vector", lambda e, _: e.tensor_copy(
                ar(E_X2QT, 512, BF16), Pb[3].bitcast(BF16)[:, 0:1024]),
                waits=[tp], inc=True)
            tk_tpfree = tk_x2qT

            tk_fcev = {0: None, 1: None}
            for nb in range(8):
                par = nb % 2
                bank = 4 + par
                wfcs = ar(E_WFC + par * 2048, 2048, BF16)
                twf = sc.dma(
                    wfcs.rearrange("p (kd n) -> p kd n", n=512),
                    s_fcq.rearrange("(kd p) n -> p kd n", p=128)[
                        :, :, nb * 512:(nb + 1) * 512],
                    waits=[tk_fc_mm[par], tk_pj_mm,
                           tk_x1 if (tt == 0 and nb <= 1) else None,
                           tk_prologue if (c == 0 and tt == 0 and nb == 0)
                           else None], ch="cw0")
                tmm = None
                for kd in range(KD):
                    tmm = sc.emit("tensor", lambda e, _, kd=kd, bank=bank,
                                  wfcs=wfcs: e.matmul(
                            Pb[bank][:],
                            ar(E_X2QT, 512, BF16)[:, kd * 128:(kd + 1) * 128],
                            wfcs[:, kd * 512:(kd + 1) * 512],
                            start=(kd == 0), stop=(kd == KD - 1)),
                        waits=([twf, tk_x2qT, tk_fcev[par]]
                               if kd == 0 else []),
                        inc=(kd == KD - 1))
                tk_fc_mm[par] = tmm
                te1 = sc.emit("vector", lambda e, _, nb=nb, bank=bank:
                    e.tensor_tensor(
                        ar(E_GB, T), Pb[bank][:],
                        ar(E_BFC, HID)[:, nb * 512:(nb + 1) * 512], ALU.add),
                    waits=[tmm, tk_bfcL], inc=True)
                if debug == "h0" and c == 0 and tt == 0 and nb == 0:
                    td_ = sc.dma(dbg_h[:, 0:T], ar(E_GB, T), waits=[te1])
                    td_ = sc.dma(dbg_h[:, T:T + HID], ar(E_BFC, HID),
                                 waits=[td_])
                    te1 = sc.emit("vector", lambda e, _: e.engine_nop(),
                                  waits=[td_], inc=True)
                te2 = act(0, ar(E_GB2, T), ar(E_GB, T), AF.Gelu, [te1])
                tq2 = emit_quant(
                    ar(E_GB2, T), 16,
                    ar(E_HQ, 2048, BF16)[:, nb * 512:(nb + 1) * 512],
                    [te2], E_QA, E_QC)
                tk_fcev[nb % 2] = tq2

            if debug == "hq0" and c == 0 and tt == 0:
                td_ = sc.dma(dbg_h.bitcast(BF16)[:, 0:HID],
                             ar(E_HQ, 2048, BF16),
                             waits=[tk_fcev[0], tk_fcev[1]])
                tk_fcev[1] = sc.emit("vector", lambda e, _: e.engine_nop(),
                                     waits=[td_], inc=True)
            tlast = tk_fcev[1]
            for rnd in range(4):
                tp = None
                for k in range(8):
                    hb = rnd * 8 + k
                    tp = sc.emit("tensor", lambda e, _, hb=hb, k=k:
                        e.transpose(
                            Pb[3].bitcast(BF16)[:, k * 128:(k + 1) * 128],
                            ar(E_HQ, 2048, BF16)[:, hb * 128:(hb + 1) * 128],
                            id_bf),
                        waits=[tk_fcev[0], tk_fcev[1], tlast]
                              if k == 0 else [], inc=(k == 7))
                tlast = sc.emit("vector", lambda e, _, rnd=rnd: e.tensor_copy(
                    ar(E_HQT, 2048, BF16)[:, rnd * 1024:(rnd + 1) * 1024],
                    Pb[3].bitcast(BF16)[:, 0:1024]), waits=[tp], inc=True)
            tk_tpfree = tlast

            for hbp in range(16):
                par = hbp % 2
                pslab = ar(E_WPJ + par * 1024, 1024, BF16)
                twp = sc.dma(
                    pslab.rearrange("p (two e) -> p two e", e=1024),
                    s_pjq[hbp * 256:(hbp + 1) * 256, :].rearrange(
                        "(two p) e -> p two e", p=128),
                    waits=[tk_pjw[par],
                           tk_fc_mm[0] if hbp == 0 else None,
                           tk_fc_mm[1] if hbp == 0 else None], ch="cw1")
                for two in range(2):
                    hb = hbp * 2 + two
                    for eh in range(2):
                        tk_pj_mm = sc.emit("tensor",
                            lambda e, _, hb=hb, eh=eh, two=two, pslab=pslab,
                            tlast=tlast: e.matmul(
                                Pb[6 + eh][:],
                                ar(E_HQT, 2048, BF16)[
                                    :, hb * 128:(hb + 1) * 128],
                                pslab[:, two * 1024 + eh * 512:
                                      two * 1024 + (eh + 1) * 512],
                                start=(hb == 0), stop=(hb == 31)),
                            waits=([twp, tlast] if (two == 0 and eh == 0)
                                   else []),
                            inc=(two == 1 and eh == 1))
                tk_pjw[par] = tk_pj_mm
            for eh in range(2):
                tk_f = sc.emit("vector", lambda e, _, tt=tt, eh=eh:
                    e.tensor_tensor(
                        ar(E_F, NTT * D)[
                            :, tt * D + eh * 512:tt * D + (eh + 1) * 512],
                        Pb[6 + eh][:],
                        ar(E_BPJ, D)[:, eh * 512:(eh + 1) * 512], ALU.add),
                    waits=[tk_pj_mm, tk_bpjL], inc=True)

        if debug == "wq" and c == 0:
            sc.dma(dbg_h.bitcast(BF16)[:, 0:HID], s_fcq[0:128, :],
                   waits=[tk_prologue])
            sc.dma(dbg_h.bitcast(BF16)[:, HID:HID + D], s_pjq[0:128, :],
                   waits=[tk_prologue])
        if debug == "ffn" and c == 0:
            sc.dma(dbg_h[:, 0:NTT * D], ar(E_F, NTT * D), waits=[tk_f])
            sc.dma(dbg_h.bitcast(BF16)[:, 2 * NTT * D:3 * NTT * D],
                   ar(E_X2Q, NTT * 512, BF16), waits=[tk_f])
            sc.dma(dbg_h.bitcast(BF16)[:, 3 * NTT * D:3 * NTT * D + 4096],
                   ar(E_HQ, 2048, BF16), waits=[tk_f])
        tsum = emit_ln_sums(
            lambda tt: ar(E_F, NTT * D)[:, tt * D:(tt + 1) * D],
            lambda tt: [tk_f] if tt == 0 else [])
        tR4 = emit_ln_cols(tsum, 56, 60)
        tk_x2 = None
        for tt in range(NTT):
            tq = sc.emit("vector", lambda e, _, tt=tt: e.tensor_scalar(
                ar(E_A, D), ar(E_F, NTT * D)[:, tt * D:(tt + 1) * D],
                cols[:, 56 + tt:57 + tt], cols[:, 60 + tt:61 + tt],
                ALU.subtract, ALU.mult), waits=[tR4], inc=True)
            tk_x2 = sc.emit("vector", lambda e, _, tt=tt: e.tensor_tensor(
                x_sb[:, tt * D:(tt + 1) * D], x_sb[:, tt * D:(tt + 1) * D],
                ar(E_A, D), ALU.add), waits=[tq], inc=True)

        tk_out_dma = sc.dma(
            out_h[r0:r0 + T, :].rearrange("(tt p) d -> p tt d", p=128),
            x_sb.rearrange("p (tt d) -> p tt d", d=D),
            waits=[tk_x2])

    finals = {ch: sc.cnt[ch] for ch in DMA_CH}

    with contextlib.ExitStack() as es:
        e = es.enter_context
        sems = {eng: e(nc.semaphore(f"sem_{eng}")) for eng in ENGS}
        dma_sems = {ch: e(nc.semaphore(f"sem_{ch}")) for ch in DMA_CH}
        block = e(nc.Block())

        @block.tensor
        def _(h):
            sc.replay("tensor", h, sems, dma_sems)

        @block.vector
        def _(h):
            sc.replay("vector", h, sems, dma_sems)

        @block.scalar
        def _(h):
            sc.replay("scalar", h, sems, dma_sems)

        @block.gpsimd
        def _(h):
            sc.replay("gpsimd", h, sems, dma_sems)

        @block.sync
        def _(h):
            sc.replay("sync", h, sems, dma_sems)
            for ch in DMA_CH:
                if finals[ch]:
                    h.wait_ge(dma_sems[ch], finals[ch])

    return nc


_CACHE = {}


def _get_nc():
    if "nc" not in _CACHE:
        _CACHE["nc"] = build_program()
    return _CACHE["nc"]


def make_inputs(L0_attn_x, L0_attn_ln_gamma, L0_attn_ln_beta, L0_attn_in_proj_w,
                L0_attn_conv_w, L0_attn_conv_b, L0_attn_x_proj_w,
                L0_attn_dt_proj_w, L0_attn_dt_proj_b, L0_attn_A_log, L0_attn_D,
                L0_attn_out_proj_w, L0_ffn_w_fc, L0_ffn_b_fc, L0_ffn_w_proj,
                L0_ffn_b_proj):
    import ml_dtypes

    def np32(a):
        return np.ascontiguousarray(np.asarray(a), dtype=np.float32)

    x = np32(L0_attn_x)

    def colsify(v, nb):
        return np.ascontiguousarray(np32(v).reshape(nb, 128).T)

    common = {
        "gammaC": colsify(L0_attn_ln_gamma, KD),
        "betaC": colsify(L0_attn_ln_beta, KD),
        "winT": np.ascontiguousarray(np32(L0_attn_in_proj_w).T),
        "cwC": np.ascontiguousarray(
            np32(L0_attn_conv_w).reshape(NEB, 128, 4).transpose(1, 0, 2)
            .reshape(128, NEB * 4)),
        "cbC": colsify(L0_attn_conv_b, NEB),
        "xpwT": np.ascontiguousarray(np32(L0_attn_x_proj_w).T),
        "dtwT": np.ascontiguousarray(np32(L0_attn_dt_proj_w).T),
        "dtbC": colsify(L0_attn_dt_proj_b, NEB),
        "alogC": np.ascontiguousarray(
            np32(L0_attn_A_log).reshape(NEB, 128, S).transpose(1, 0, 2)
            .reshape(128, NEB * S)),
        "dC": colsify(L0_attn_D, NEB),
        "woutT": np.ascontiguousarray(np32(L0_attn_out_proj_w).T),
        "wfc": np32(L0_ffn_w_fc),
        "bfc": np.ascontiguousarray(np32(L0_ffn_b_fc).reshape(1, HID)),
        "wproj": np32(L0_ffn_w_proj),
        "bproj": np.ascontiguousarray(np32(L0_ffn_b_proj).reshape(1, D)),
        "idf32": np.eye(128, dtype=np.float32),
        "idbf": np.eye(128, dtype=ml_dtypes.bfloat16),
        "onesf": np.ones((1, 128), dtype=np.float32),
    }
    return [dict(common, x=np.ascontiguousarray(x[b])) for b in range(8)]


def kernel(**inputs):
    in_maps = make_inputs(**inputs)
    nc = _get_nc()
    res = run_bass_kernel_spmd(nc, in_maps, core_ids=list(range(8)))
    return np.stack([np.asarray(r["out"], dtype=np.float32)
                     for r in res.results], axis=0)


# version 20
# speedup vs baseline: 1.1680x; 1.0358x over previous
import contextlib
import numpy as np
import concourse.bass as bass
import concourse.mybir as mybir
from concourse.bass_utils import run_bass_kernel_spmd

F32 = mybir.dt.float32
F32R = mybir.dt.float32r
BF16 = mybir.dt.bfloat16
I32 = mybir.dt.int32
AF = mybir.ActivationFunctionType
ALU = mybir.AluOpType
AX = mybir.AxisListType

L, D, DI, S, RK, HID = 4096, 1024, 2048, 16, 64, 4096
T = 512
NCH = L // T
NTT = T // 128
KD = D // 128
NEB = DI // 128
SG = 2
SH = 4
EPS = 1e-5
CMUL = 1.5 * (2.0 ** 20)
TINY = 1.17549435e-38

ENGS = ("tensor", "vector", "scalar", "gpsimd", "sync")


DMA_CH = ("cio", "cw0", "cw1", "cp")


class Sched:

    def __init__(self):
        self.q = {e: [] for e in ENGS}
        self.cnt = {e: 0 for e in ENGS}
        for ch in DMA_CH:
            self.cnt[ch] = 0

    def _res(self, waits):
        out = []
        for w in waits:
            if w is None:
                continue
            owner, c = w
            if owner in DMA_CH:
                c = self.cnt[owner]
            out.append((owner, c))
        return tuple(out)

    def emit(self, eng, fn, waits=(), inc=False):
        if inc:
            self.cnt[eng] += 1
        self.q[eng].append((self._res(waits), fn, bool(inc)))
        return (eng, self.cnt[eng]) if inc else None

    def dma(self, out, in_, waits=(), ch="cio"):
        rw = self._res(waits)
        self.cnt[ch] += 16

        def fn(e, _):
            return e.dma_start(out=out, in_=in_)

        self.q["sync"].append((rw, fn, ch))
        return (ch, self.cnt[ch])

    def tok(self, eng):
        return (eng, self.cnt[eng])

    def replay(self, eng, handle, sems, dma_sems):
        waited = {}
        for waits, fn, inc in self.q[eng]:
            for owner, c in waits:
                if c <= 0 or waited.get(owner, 0) >= c:
                    continue
                waited[owner] = c
                handle.wait_ge(dma_sems[owner] if owner in DMA_CH
                               else sems[owner], c)
            inst = fn(handle, inc)
            if inc in DMA_CH:
                inst.then_inc(dma_sems[inc], 16)
            elif inc:
                inst.then_inc(sems[eng], 1)


def build_program(debug=None, nch=NCH, pstop=8, psel=None):
    if psel is None:
        psel = set(range(1, pstop + 1))
    nc = bass.Bass("TRN2", target_bir_lowering=False)
    sc = Sched()

    x_h = nc.dram_tensor("x", [L, D], F32, kind="ExternalInput").ap()
    out_h = nc.dram_tensor("out", [L, D], F32, kind="ExternalOutput").ap()
    gammaC_h = nc.dram_tensor("gammaC", [128, KD], F32, kind="ExternalInput").ap()
    betaC_h = nc.dram_tensor("betaC", [128, KD], F32, kind="ExternalInput").ap()
    winT_h = nc.dram_tensor("winT", [D, 2 * DI], F32, kind="ExternalInput").ap()
    cwC_h = nc.dram_tensor("cwC", [128, NEB * 4], F32, kind="ExternalInput").ap()
    cbC_h = nc.dram_tensor("cbC", [128, NEB], F32, kind="ExternalInput").ap()
    xpwT_h = nc.dram_tensor("xpwT", [DI, 96], F32, kind="ExternalInput").ap()
    dtwT_h = nc.dram_tensor("dtwT", [RK, DI], F32, kind="ExternalInput").ap()
    dtbC_h = nc.dram_tensor("dtbC", [128, NEB], F32, kind="ExternalInput").ap()
    alogC_h = nc.dram_tensor("alogC", [128, NEB * S], F32, kind="ExternalInput").ap()
    dC_h = nc.dram_tensor("dC", [128, NEB], F32, kind="ExternalInput").ap()
    woutT_h = nc.dram_tensor("woutT", [DI, D], F32, kind="ExternalInput").ap()
    wfc_h = nc.dram_tensor("wfc", [HID, D], F32, kind="ExternalInput").ap()
    bfc_h = nc.dram_tensor("bfc", [1, HID], F32, kind="ExternalInput").ap()
    wproj_h = nc.dram_tensor("wproj", [D, HID], F32, kind="ExternalInput").ap()
    bproj_h = nc.dram_tensor("bproj", [1, D], F32, kind="ExternalInput").ap()
    idf32_h = nc.dram_tensor("idf32", [128, 128], F32, kind="ExternalInput").ap()
    idbf_h = nc.dram_tensor("idbf", [128, 128], BF16, kind="ExternalInput").ap()
    onesf_h = nc.dram_tensor("onesf", [1, 128], F32, kind="ExternalInput").ap()

    dbg_h = None
    if debug is not None:
        dbg_h = nc.dram_tensor("dbg", [128, 16384], F32,
                               kind="ExternalOutput").ap()

    s_inh = nc.dram_tensor("s_inh", [D, 2 * DI], F32R, kind="Internal").ap()
    s_inl = nc.dram_tensor("s_inl", [D, 2 * DI], F32R, kind="Internal").ap()
    s_outh = nc.dram_tensor("s_outh", [DI, D], F32R, kind="Internal").ap()
    s_outl = nc.dram_tensor("s_outl", [DI, D], F32R, kind="Internal").ap()
    s_fcq = nc.dram_tensor("s_fcq", [D, HID], BF16, kind="Internal").ap()
    s_pjq = nc.dram_tensor("s_pjq", [HID, D], BF16, kind="Internal").ap()
    s_xpwh = nc.dram_tensor("s_xpwh", [128, NEB * 96], F32R, kind="Internal").ap()
    s_xpwl = nc.dram_tensor("s_xpwl", [128, NEB * 96], F32R, kind="Internal").ap()
    s_bfcB = nc.dram_tensor("s_bfcB", [128, HID], F32, kind="Internal").ap()
    s_bpjB = nc.dram_tensor("s_bpjB", [128, D], F32, kind="Internal").ap()
    s_bm = nc.dram_tensor("s_bm", [32, T], F32, kind="Internal").ap()
    s_sz = nc.dram_tensor("s_sz", [NEB * 128, T], F32, kind="Internal").ap()

    uTh = nc.alloc_sbuf_tensor("uTh", [128, KD * T], F32R).ap()
    uTl = nc.alloc_sbuf_tensor("uTl", [128, KD * T], F32R).ap()
    x_sb = nc.alloc_sbuf_tensor("x_sb", [128, NTT * D], F32).ap()
    wshb = [nc.alloc_sbuf_tensor(f"wsh{i}", [128, 1024], F32R).ap()
            for i in range(2)]
    wslb = [nc.alloc_sbuf_tensor(f"wsl{i}", [128, 1024], F32R).ap()
            for i in range(2)]
    dtw_hi = nc.alloc_sbuf_tensor("dtw_hi", [64, DI], F32R).ap()
    dtw_lo = nc.alloc_sbuf_tensor("dtw_lo", [64, DI], F32R).ap()
    id_f32 = nc.alloc_sbuf_tensor("id_f32", [128, 128], F32).ap()
    id_bf = nc.alloc_sbuf_tensor("id_bf", [128, 128], BF16).ap()
    onesf = nc.alloc_sbuf_tensor("onesf_sb", [1, 128], F32).ap()
    acol = nc.alloc_sbuf_tensor("acol", [128, NEB * S], F32).ap()
    dcol = nc.alloc_sbuf_tensor("dcol", [128, NEB], F32).ap()
    cwcol = nc.alloc_sbuf_tensor("cwcol", [128, NEB * 4], F32).ap()
    cbcol = nc.alloc_sbuf_tensor("cbcol", [128, NEB], F32).ap()
    dtbcol = nc.alloc_sbuf_tensor("dtbcol", [128, NEB], F32).ap()
    dtb2col = nc.alloc_sbuf_tensor("dtb2col", [128, NEB], F32).ap()
    halfc = nc.alloc_sbuf_tensor("halfc", [128, 2], F32).ap()
    gcol = nc.alloc_sbuf_tensor("gcol", [128, KD], F32).ap()
    bcol = nc.alloc_sbuf_tensor("bcol", [128, KD], F32).ap()
    wbeta = nc.alloc_sbuf_tensor("wbeta", [128, 32], F32).ap()
    carry = nc.alloc_sbuf_tensor("carry", [128, 256], F32).ap()
    tails = nc.alloc_sbuf_tensor("tails", [128, NEB * 3], F32).ap()
    cols = nc.alloc_sbuf_tensor("cols", [128, 64], F32).ap()
    xch_sb = nc.alloc_sbuf_tensor("xch_sb", [128, T], F32R).ap()
    xcl_sb = nc.alloc_sbuf_tensor("xcl_sb", [128, T], F32R).ap()
    dtrh_sb = nc.alloc_sbuf_tensor("dtrh_sb", [64, T], F32R).ap()
    dtrl_sb = nc.alloc_sbuf_tensor("dtrl_sb", [64, T], F32R).ap()
    yph_sb, ypl_sb = xch_sb, xcl_sb
    xpwh_sb = nc.alloc_sbuf_tensor("xpwh_sb", [128, 768], F32R).ap()
    xpwl_sb = nc.alloc_sbuf_tensor("xpwl_sb", [128, 768], F32R).ap()

    AR_N = 26624
    arena = nc.alloc_sbuf_tensor("arena", [128, AR_N], F32).ap()

    def ar(off, n, dt=F32):
        a = arena[:, off:off + n]
        if dt is not F32:
            a = a.bitcast(dt)
        return a

    A_U32, A_U32T, A_TRASH = 0, 4096, 8192
    A_XC, A_Y = 0, 8192
    A_NBM, A_NCM = 16384, 18432
    A_DBUH = 20480
    A_DT, A_DTU, A_DA = 22528, 23040, 23552
    A_SP1, A_SP2 = 24064, 24576
    A_DA2 = 26112
    A_DTR32 = 16384
    A_XIN = 16896
    A_AB = 8192
    A_T2 = 24064
    A_SZ = 25088
    A_YPH, A_YPL = 20480, 20992
    E_A, E_F = 0, 4096
    E_X2Q, E_X2QT = 8192, 10240
    E_HQ = 10752
    E_HQT = 12800
    E_GB, E_GB2 = 14848, 15360
    E_BFC, E_BPJ = 16384, 20480
    E_QA, E_QC = 21504, 21760
    E_WFC = 22016
    E_WPJ = 22016
    W0, W1, W2 = 0, 4096, 8192
    P_QA, P_QC = 12288, 12544

    bmcm = ar(A_Y, T)

    Pb = [nc.alloc_psum_tensor(f"pb{i}", [128, 512], F32).ap() for i in range(8)]

    def act(eng_ignore, out, in_, func, waits, inc=True, **kw):
        if func is AF.Copy and any(
                not isinstance(kw.get(k, 0.0), (int, float))
                for k in ("bias", "scale")):
            func = AF.Identity
        return sc.emit("scalar", lambda e, _: e.activation(out, in_, func, **kw),
                       waits=waits, inc=inc)

    def emit_ln_cols(sum_tok, mud, rd, gd=None):
        S0, S1, S2, S3 = 20, 24, 28, 32
        t = sc.emit("vector", lambda e, _: e.tensor_scalar_mul(
            cols[:, mud:mud + 4], cols[:, 0:4], 1.0 / D), [sum_tok], inc=True)
        t = sc.emit("vector", lambda e, _: e.tensor_scalar_mul(
            cols[:, S0:S0 + 4], cols[:, 4:8], 1.0 / D), [t], inc=True)
        t = sc.emit("vector", lambda e, _: e.tensor_tensor(
            cols[:, S1:S1 + 4], cols[:, mud:mud + 4], cols[:, mud:mud + 4],
            ALU.mult), [t], inc=True)
        t = sc.emit("vector", lambda e, _: e.tensor_tensor(
            cols[:, S0:S0 + 4], cols[:, S0:S0 + 4], cols[:, S1:S1 + 4],
            ALU.subtract), [t], inc=True)
        tva = sc.emit("vector", lambda e, _: e.tensor_scalar_add(
            cols[:, S1:S1 + 4], cols[:, S0:S0 + 4], EPS), [t], inc=True)
        t = act(0, cols[:, S2:S2 + 4], cols[:, S1:S1 + 4], AF.Ln, [tva])
        tr = act(0, cols[:, rd:rd + 4], cols[:, S2:S2 + 4], AF.Exp, [t],
                 scale=-0.5)
        if gd is None:
            return tr
        t = sc.emit("vector", lambda e, _: e.reciprocal(
            cols[:, S2:S2 + 4], cols[:, S1:S1 + 4]), [tva, tr], inc=True)
        t = sc.emit("vector", lambda e, _: e.tensor_tensor(
            cols[:, S2:S2 + 4], cols[:, S0:S0 + 4], cols[:, S2:S2 + 4],
            ALU.mult), [t], inc=True)
        t = sc.emit("vector", lambda e, _: e.tensor_scalar_add(
            cols[:, S2:S2 + 4], cols[:, S2:S2 + 4], EPS), [t], inc=True)
        t = act(0, cols[:, S3:S3 + 4], cols[:, S2:S2 + 4], AF.Ln, [t])
        t = act(0, cols[:, S3:S3 + 4], cols[:, S3:S3 + 4], AF.Exp, [t],
                scale=-0.5)
        tg = sc.emit("vector", lambda e, _: e.tensor_tensor(
            cols[:, gd:gd + 4], cols[:, rd:rd + 4], cols[:, S3:S3 + 4],
            ALU.mult), [t, tr], inc=True)
        return tg

    def emit_ln_sums(src_fn, waits_fn, trash=None):
        toks = []
        to = A_TRASH if trash is None else trash
        for tt in range(NTT):
            t1 = act(0, ar(to, D), src_fn(tt), AF.Copy,
                     waits_fn(tt), accum_out=cols[:, tt:tt + 1])
            t2 = act(0, ar(to, D), src_fn(tt), AF.Square, [t1],
                     accum_out=cols[:, 4 + tt:5 + tt])
            toks.append(t2)
        return toks[-1]

    def emit_quant(src_ap, nblk, out_ap, waits, qa, qc):
        t1 = sc.emit("vector", lambda e, _: e.tensor_reduce(
            ar(qa, nblk)[:, :, None],
            src_ap.rearrange("p (n b) -> p n b", b=32),
            AX.X, ALU.max, apply_absolute_value=True), waits=waits, inc=True)
        t2 = sc.emit("vector", lambda e, _: e.tensor_scalar_max(
            ar(qa, nblk), ar(qa, nblk), TINY), waits=[t1], inc=True)
        t3 = sc.emit("vector", lambda e, _: e.tensor_scalar(
            ar(qc, nblk).bitcast(I32), ar(qa, nblk).bitcast(I32),
            int(0x7F800000), None, ALU.bitwise_and), waits=[t2], inc=True)
        t4 = sc.emit("vector", lambda e, _: e.tensor_scalar_mul(
            ar(qc, nblk), ar(qc, nblk), CMUL), waits=[t3], inc=True)
        cb = ar(qc, nblk)[:, :, None].broadcast_to([128, nblk, 32])
        t5 = sc.emit("vector", lambda e, _: e.tensor_tensor(
            src_ap.rearrange("p (n b) -> p n b", b=32),
            src_ap.rearrange("p (n b) -> p n b", b=32), cb, ALU.add),
            waits=[t4], inc=True)
        t6 = sc.emit("vector", lambda e, _: e.tensor_tensor(
            out_ap.rearrange("p (n b) -> p n b", b=32),
            src_ap.rearrange("p (n b) -> p n b", b=32), cb, ALU.subtract),
            waits=[t5], inc=True)
        return t6

    tk_c = tk_alog = tk_acol = tk_dtw = tk_xpw = None
    tk_bfcB = tk_bpjB = tk_wbeta = None
    tk_pdma = tk_ppe = tk_pact = tk_ev = None
    if psel is None or 11 in psel or len(psel & {2,3,4,5,6,7,8}) > 0 and 11 not in psel and 12 not in psel:
        for dst, src in ((gcol, gammaC_h), (bcol, betaC_h), (cwcol, cwC_h),
                         (cbcol, cbC_h), (dtbcol, dtbC_h), (dcol, dC_h),
                         (id_f32, idf32_h), (id_bf, idbf_h), (onesf, onesf_h)):
            tk_c = sc.dma(dst, src)
    if psel is None or 12 in psel or len(psel & {2,3,4,5,6,7,8}) > 0 and 11 not in psel and 12 not in psel:
        tk_alog = sc.dma(ar(A_TRASH, NEB * S), alogC_h)
        t = act(0, acol, ar(A_TRASH, NEB * S), AF.Exp, [tk_alog])
        tk_acol = sc.emit("vector", lambda e, _: e.tensor_scalar_mul(
            dtb2col, dtbcol, -0.5), waits=[t, tk_c], inc=True)
        th_ = sc.emit("vector", lambda e, _: e.memset(
            halfc[:, 0:1], 0.5), waits=[tk_acol], inc=True)
        tk_half = sc.emit("vector", lambda e, _: e.memset(
            halfc[:, 1:2], -0.5), waits=[th_], inc=True)

    if 2 in psel:
        tkw = sc.dma(ar(A_U32, DI)[0:64, :], dtwT_h)
        t = act(0, dtw_hi, ar(A_U32, DI)[0:64, :], AF.Copy, [tkw])
        tk_dtw = sc.emit("vector", lambda e, _: e.tensor_tensor(
            dtw_lo, ar(A_U32, DI)[0:64, :], dtw_hi, ALU.subtract),
            waits=[t], inc=True)

    if 3 in psel or 31 in psel:
        tkw = sc.dma(ar(A_U32, NEB * 96),
                     xpwT_h.rearrange("(e p) j -> p e j", p=128), waits=[tk_dtw])
        tk_xpw = tkw
    if 3 in psel or 32 in psel:
        t = act(0, ar(A_U32T, NEB * 96, F32R), ar(A_U32, NEB * 96), AF.Copy, [tkw])
        tk_xpw = t
    if 3 in psel or 33 in psel:
        t2 = sc.emit("vector", lambda e, _: e.tensor_tensor(
            ar(A_TRASH, NEB * 96, F32R), ar(A_U32, NEB * 96),
            ar(A_U32T, NEB * 96, F32R), ALU.subtract), waits=[t], inc=True)
        tk_xpw = t2
    if 3 in psel or 34 in psel:
        sc.dma(s_xpwh, ar(A_U32T, NEB * 96, F32R), waits=[t])
        tk_xpw = sc.dma(s_xpwl, ar(A_TRASH, NEB * 96, F32R), waits=[t2])

    if 4 in psel:
        tk_bfc = sc.dma(ar(A_U32, HID)[0:1, :], bfc_h, waits=[tk_xpw])
        tk_bpj = sc.dma(ar(A_U32T, D)[0:1, :], bproj_h)
        te = None
        for j in range(8):
            tm = sc.emit("tensor", lambda e, _, j=j: e.matmul(
                Pb[0][:], onesf, ar(A_U32, HID)[0:1, j * 512:(j + 1) * 512],
                start=True, stop=True), waits=[tk_bfc, tk_c, te], inc=True)
            te = act(0, ar(A_TRASH, HID)[:, j * 512:(j + 1) * 512], Pb[0][:],
                     AF.Copy, [tm])
        tk_bfcB = sc.dma(s_bfcB, ar(A_TRASH, HID), waits=[te])
        te = None
        for j in range(2):
            tm = sc.emit("tensor", lambda e, _, j=j: e.matmul(
                Pb[1][:], onesf, ar(A_U32T, D)[0:1, j * 512:(j + 1) * 512],
                start=True, stop=True), waits=[tk_bpj, te], inc=True)
            te = act(0, ar(A_TRASH + HID, D)[:, j * 512:(j + 1) * 512], Pb[1][:],
                     AF.Copy, [tm])
        tk_bpjB = sc.dma(s_bpjB, ar(A_TRASH + HID, D), waits=[te])

    if 5 in psel:
        tk_pdma = tk_bpjB
        tk_ppe = None
        tk_pact = None
        for kd in range(KD):
            tkw = sc.dma(ar(W0, 2 * DI), winT_h[kd * 128:(kd + 1) * 128, :],
                         waits=[tk_pdma, tk_ppe, tk_pact, tk_bfcB])
            for eb in range(32):
                tk_ppe = sc.emit("tensor", lambda e, _, kd=kd, eb=eb: e.matmul(
                    Pb[2][:, eb:eb + 1], ar(W0, 2 * DI)[:, eb * 128:(eb + 1) * 128],
                    bcol[:, kd:kd + 1], start=(kd == 0 and eb == 0),
                    stop=(kd == KD - 1)),
                    waits=[tkw], inc=(eb == 31))
            tg = act(0, ar(W1, 2 * DI), ar(W0, 2 * DI), AF.Copy, [tkw],
                     scale=gcol[:, kd:kd + 1])
            th = act(0, ar(W2, 2 * DI, F32R), ar(W1, 2 * DI), AF.Copy, [tg])
            tk_pact = th
            tl = sc.emit("vector", lambda e, _: e.tensor_tensor(
                ar(W1, 2 * DI, F32R), ar(W1, 2 * DI), ar(W2, 2 * DI, F32R),
                ALU.subtract), waits=[th], inc=True)
            sc.dma(s_inh[kd * 128:(kd + 1) * 128, :], ar(W2, 2 * DI, F32R),
                   waits=[th])
            tk_pdma = sc.dma(s_inl[kd * 128:(kd + 1) * 128, :],
                             ar(W1, 2 * DI, F32R), waits=[tl])
        tk_wbeta = act(0, wbeta, Pb[2][:, 0:32], AF.Copy, [tk_ppe])

    if 6 in psel:
        for ed in range(NEB):
            tkw = sc.dma(ar(W0, D), woutT_h[ed * 128:(ed + 1) * 128, :],
                         waits=[tk_pdma, tk_pact])
            th = act(0, ar(W1, D, F32R), ar(W0, D), AF.Copy, [tkw])
            tk_pact = th
            tl = sc.emit("vector", lambda e, _: e.tensor_tensor(
                ar(W2, D, F32R), ar(W0, D), ar(W1, D, F32R), ALU.subtract),
                waits=[th], inc=True)
            sc.dma(s_outh[ed * 128:(ed + 1) * 128, :], ar(W1, D, F32R), waits=[th])
            tk_pdma = sc.dma(s_outl[ed * 128:(ed + 1) * 128, :],
                             ar(W2, D, F32R), waits=[tl])

    if 7 in psel:
        tk_ev = None
        for nb in range(32):
            tkw = sc.dma(ar(W0, D), wfc_h[nb * 128:(nb + 1) * 128, :],
                         waits=[tk_pdma, tk_pact])
            tk_pact = None
            tq = emit_quant(ar(W0, D), 32, ar(W1, 512, BF16), [tkw], P_QA, P_QC)
            tp = None
            for kb in range(8):
                tp = sc.emit("tensor", lambda e, _, kb=kb: e.transpose(
                    Pb[3].bitcast(BF16)[:, kb * 128:(kb + 1) * 128],
                    ar(W1, 512, BF16)[:, kb * 128:(kb + 1) * 128], id_bf),
                    waits=[tq, tk_ev] if kb == 0 else [], inc=(kb == 7))
            tk_ev = sc.emit("vector", lambda e, _: e.tensor_copy(
                ar(W2, 512, BF16), Pb[3].bitcast(BF16)[:, 0:1024]),
                waits=[tp], inc=True)
            tk_pdma = sc.dma(
                s_fcq.rearrange("(kb p) n -> p kb n", p=128)[
                    :, :, nb * 128:(nb + 1) * 128],
                ar(W2, 512, BF16).rearrange("p (kb n) -> p kb n", n=128),
                waits=[tk_ev])

    if 8 in psel:
        for eb in range(8):
            tkw = sc.dma(ar(W0, HID), wproj_h[eb * 128:(eb + 1) * 128, :],
                         waits=[tk_pdma])
            tq = emit_quant(ar(W0, HID), 128, ar(W1, 2048, BF16), [tkw], P_QA, P_QC)
            tlast = tq
            for rnd in range(4):
                tp = None
                for k in range(8):
                    hb = rnd * 8 + k
                    tp = sc.emit("tensor", lambda e, _, hb=hb, k=k: e.transpose(
                        Pb[3].bitcast(BF16)[:, k * 128:(k + 1) * 128],
                        ar(W1, 2048, BF16)[:, hb * 128:(hb + 1) * 128], id_bf),
                        waits=[tlast, tk_ev] if k == 0 else [], inc=(k == 7))
                tk_ev = sc.emit("vector", lambda e, _, rnd=rnd: e.tensor_copy(
                    ar(W2, 2048, BF16)[:, rnd * 1024:(rnd + 1) * 1024],
                    Pb[3].bitcast(BF16)[:, 0:1024]), waits=[tp], inc=True)
                tlast = tk_ev
            tk_pdma = sc.dma(
                s_pjq.rearrange("(hb p) e -> p hb e", p=128)[
                    :, :, eb * 128:(eb + 1) * 128],
                ar(W2, 2048, BF16).rearrange("p (hb e) -> p hb e", e=128),
                waits=[tlast])

    if 5 not in psel:
        tk_pdma = None
        tk_ev = None
    tk_prologue = tk_pdma
    tk_prologue_ev = tk_ev if 7 in psel else None

    tk_out_dma = tk_prologue

    for c in range(nch):
        r0 = c * T

        tk_x = sc.dma(
            x_sb.rearrange("p (tt d) -> p tt d", d=D),
            x_h[r0:r0 + T, :].rearrange("(tt p) d -> p tt d", p=128),
            waits=[tk_out_dma])

        tsum = emit_ln_sums(
            lambda tt: x_sb[:, tt * D:(tt + 1) * D],
            lambda tt: [tk_x, tk_wbeta, tk_acol] if tt == 0 else [])
        tg1 = emit_ln_cols(tsum, 8, 12, gd=16)

        tk_pe_prev = tk_wbeta if c == 0 else None
        tk_u = None
        for tt in range(NTT):
            tu = sc.emit("vector", lambda e, _, tt=tt: e.tensor_scalar(
                ar(A_U32, NTT * D)[:, tt * D:(tt + 1) * D],
                x_sb[:, tt * D:(tt + 1) * D],
                cols[:, 8 + tt:9 + tt], cols[:, 16 + tt:17 + tt],
                ALU.subtract, ALU.mult), waits=[tg1], inc=True)
            for grp in range(2):
                tp = None
                for k in range(4):
                    kd = grp * 4 + k
                    tp = sc.emit("tensor", lambda e, _, tt=tt, kd=kd, k=k:
                        e.transpose(
                            Pb[2][:, k * 128:(k + 1) * 128],
                            ar(A_U32, NTT * D)[
                                :, tt * D + kd * 128:tt * D + (kd + 1) * 128],
                            id_f32),
                        waits=[tu, tk_pe_prev] if k == 0 else [], inc=(k == 3))
                uh_dst = uTh.rearrange("p (kd t) -> p kd t", t=T)[
                    :, grp * 4:(grp + 1) * 4, tt * 128:(tt + 1) * 128]
                u32_dst = ar(A_U32T, NTT * D).rearrange(
                    "p (kd t) -> p kd t", t=T)[
                    :, grp * 4:(grp + 1) * 4, tt * 128:(tt + 1) * 128]
                psrc = Pb[2].rearrange("p (k t) -> p k t", t=128)
                th = act(0, uh_dst, psrc, AF.Copy, [tp])
                t32 = act(0, u32_dst, psrc, AF.Copy, [th])
                tk_pe_prev = t32
                tk_u = t32
        tk_ulo = sc.emit("vector", lambda e, _: e.tensor_tensor(
            uTl, ar(A_U32T, KD * T), uTh, ALU.subtract), waits=[tk_u], inc=True)

        if debug == "u" and c == 0:
            sc.dma(dbg_h[:, 0:NTT * D], ar(A_U32, NTT * D), waits=[tk_ulo])
            sc.dma(dbg_h[:, NTT * D:2 * NTT * D], ar(A_U32T, NTT * D),
                   waits=[tk_ulo])
            sc.dma(dbg_h[:, 2 * NTT * D:2 * NTT * D + 64], cols,
                   waits=[tk_ulo])

        tk_xpwl = None

        if c == 0:
            tk_tails = sc.emit("vector", lambda e, _: e.memset(tails, 0.0),
                               waits=[], inc=True)
        tk_wfree = {0: None, 1: None}
        tk_evac = {0: None, 1: None}
        tk_xp_last = None
        tk_xp_prev = None
        tk_xc = []
        xin = ar(A_XIN, 530)
        for eb in range(NEB):
            bank = eb % 2
            if eb % 8 == 0:
                hw0 = (eb // 8) * 768
                tk_xpwl = sc.dma(xpwh_sb, s_xpwh[:, hw0:hw0 + 768],
                                 waits=[tk_xpw, tk_xp_prev])
                tk_xpwl = sc.dma(xpwl_sb, s_xpwl[:, hw0:hw0 + 768],
                                 waits=[tk_xpwl])
            wsh, wsl = wshb[bank], wslb[bank]
            tw1 = sc.dma(
                wsh.rearrange("p (kd e) -> p kd e", e=128),
                s_inh.rearrange("(kd p) e -> p kd e", p=128)[
                    :, :, eb * 128:(eb + 1) * 128],
                waits=[tk_wfree[bank],
                       tk_prologue if (c == 0 and eb == 0) else None],
                ch="cw0")
            tw2 = sc.dma(
                wsl.rearrange("p (kd e) -> p kd e", e=128),
                s_inl.rearrange("(kd p) e -> p kd e", p=128)[
                    :, :, eb * 128:(eb + 1) * 128],
                waits=[tk_wfree[bank]], ch="cw1")
            tmm = None
            for p, (wa, ua) in enumerate(((wsh, uTh), (wsh, uTl), (wsl, uTh))):
                for kd in range(KD):
                    tmm = sc.emit("tensor", lambda e, _, wa=wa, ua=ua, kd=kd,
                                  p=p, bank=bank: e.matmul(
                        Pb[bank][:], wa[:, kd * 128:(kd + 1) * 128],
                        ua[:, kd * T:(kd + 1) * T],
                        start=(p == 0 and kd == 0),
                        stop=(p == 2 and kd == KD - 1)),
                        waits=([tw1, tw2, tk_ulo, tk_evac[bank]]
                               if (p == 0 and kd == 0) else []),
                        inc=(p == 2 and kd == KD - 1))
            tk_wfree[bank] = tmm
            if debug == "xin" and c == 0 and eb == 0:
                trec = sc.emit("vector", lambda e, _: e.tensor_tensor(
                    ar(A_Y, 1024), wshb[0], wslb[0], ALU.add), waits=[tmm], inc=True)
                sc.dma(dbg_h[:, 2048:3072], ar(A_Y, 1024), waits=[trec])
            thal = sc.emit("vector", lambda e, _, eb=eb: e.tensor_copy(
                xin[:, 0:3], tails[:, eb * 3:(eb + 1) * 3]),
                waits=[tk_tails if c == 0 else None], inc=True)
            tev = act(0, xin[:, 3:3 + T], Pb[bank][:], AF.Copy, [tmm, thal],
                      bias=wbeta[:, eb:eb + 1])
            tk_evac[bank] = tev
            sc.emit("vector", lambda e, _, eb=eb: e.tensor_copy(
                tails[:, eb * 3:(eb + 1) * 3], xin[:, T:T + 3]),
                waits=[tev], inc=True)
            tcv = sc.emit("vector", lambda e, _, eb=eb: e.tensor_scalar_mul(
                ar(A_SP1, T), xin[:, 0:T], cwcol[:, eb * 4:eb * 4 + 1]),
                waits=[tev], inc=True)
            for k in range(1, 4):
                tcv = sc.emit("vector", lambda e, _, eb=eb, k=k:
                    e.scalar_tensor_tensor(
                        ar(A_SP1, T), xin[:, k:k + T],
                        cwcol[:, eb * 4 + k:eb * 4 + k + 1], ar(A_SP1, T),
                        ALU.mult, ALU.add), waits=[], inc=(k == 3))
            if debug == "xin" and c == 0 and eb == 0:
                sc.dma(dbg_h[:, 0:530], xin, waits=[tcv])
                sc.dma(dbg_h[:, 530:530 + 512], ar(A_SP1, T), waits=[tcv])
            xcs = ar(A_XC, NEB * T)[:, eb * T:(eb + 1) * T]
            txc = act(0, xcs, ar(A_SP1, T), AF.Silu, [tcv],
                      bias=cbcol[:, eb:eb + 1])
            tk_xc.append(txc)
            tph = act(0, xch_sb, xcs, AF.Copy, [txc, tk_xp_prev])
            tpl = sc.emit("vector", lambda e, _, xcs=xcs: e.tensor_tensor(
                xcl_sb, xcs, xch_sb, ALU.subtract),
                waits=[tph], inc=True)
            for p, wa, ua in ((0, xpwh_sb, xch_sb),
                              (1, xpwh_sb, xcl_sb),
                              (2, xpwl_sb, xch_sb)):
                tk_xp_last = sc.emit("tensor",
                    lambda e, _, eb=eb, p=p, wa=wa, ua=ua: e.matmul(
                        Pb[6][0:96, :],
                        wa[:, (eb % 8) * 96:(eb % 8 + 1) * 96], ua,
                        start=(eb == 0 and p == 0),
                        stop=(eb == NEB - 1 and p == 2)),
                    waits=([tpl, tk_xpwl] if p == 0 else []),
                    inc=(p == 2))
            tk_xp_prev = tk_xp_last

        tdr = act(0, ar(A_DTR32, T)[0:64, :], Pb[6][0:64, :], AF.Copy,
                  [tk_xp_last])
        tbc = act(0, bmcm[64:96, :], Pb[6][64:96, :], AF.Copy, [tk_xp_last])
        tdrh = act(0, dtrh_sb, ar(A_DTR32, T)[0:64, :], AF.Copy, [tdr])
        tdrl = sc.emit("vector", lambda e, _: e.tensor_tensor(
            dtrl_sb, ar(A_DTR32, T)[0:64, :], dtrh_sb,
            ALU.subtract), waits=[tdrh], inc=True)
        tk_bm_dma = sc.dma(s_bm, bmcm[64:96, :], waits=[tbc])

        if debug == "xc" and c == 0:
            sc.dma(dbg_h[:, 0:NEB * T], ar(A_XC, NEB * T), waits=[tk_xc[-1]])
            sc.dma(dbg_h[:, NEB * T:NEB * T + T], bmcm, waits=[tbc])
            sc.dma(dbg_h[:, NEB * T + T:NEB * T + 2 * T], ar(A_DTR32, T),
                   waits=[tdr])

        DT, DTU = ar(A_DT, T), ar(A_DTU, T)
        tk_scan_dv = None
        tk_dt_prev = None
        tk_dtu_prev = None
        tk_dtp_evac = {0: None, 1: None}
        tk_slotfree = {0: None, 1: None}
        tk_scan2 = {0: None, 1: None}
        tk_zevac = dict(tk_evac)
        tk_zout = {0: None, 1: None}
        tk_dbu_last = None
        gi = 0
        gi2 = 0

        def emit_z(eb):
            bank = eb % 2
            tw1 = sc.dma(
                wshb[bank].rearrange("p (kd e) -> p kd e", e=128),
                s_inh.rearrange("(kd p) e -> p kd e", p=128)[
                    :, :, (16 + eb) * 128:(17 + eb) * 128],
                waits=[tk_wfree[bank]], ch="cw0")
            tw2 = sc.dma(
                wslb[bank].rearrange("p (kd e) -> p kd e", e=128),
                s_inl.rearrange("(kd p) e -> p kd e", p=128)[
                    :, :, (16 + eb) * 128:(17 + eb) * 128],
                waits=[tk_wfree[bank]], ch="cw1")
            tmm = None
            for p, (wa, ua) in enumerate(((wshb[bank], uTh), (wshb[bank], uTl),
                                          (wslb[bank], uTh))):
                for kd in range(KD):
                    tmm = sc.emit("tensor", lambda e, _, wa=wa, ua=ua, kd=kd,
                                  p=p, bank=bank: e.matmul(
                        Pb[bank][:], wa[:, kd * 128:(kd + 1) * 128],
                        ua[:, kd * T:(kd + 1) * T],
                        start=(p == 0 and kd == 0),
                        stop=(p == 2 and kd == KD - 1)),
                        waits=([tw1, tw2, tk_zevac[bank]]
                               if (p == 0 and kd == 0) else []),
                        inc=(p == 2 and kd == KD - 1))
            tk_wfree[bank] = tmm
            szb = ar(A_SZ + bank * 512, T)
            tsz = act(0, szb, Pb[bank][:], AF.Silu, [tmm, tk_zout[bank]],
                      bias=wbeta[:, 16 + eb:17 + eb])
            tk_zevac[bank] = tsz
            tk_zout[bank] = sc.dma(s_sz[eb * 128:(eb + 1) * 128, :], szb,
                                   waits=[tsz], ch="cp")
        cst = {"pend": None, "y": None, "pool": None, "burst": None}

        def flush_pend():
            p = cst["pend"]
            if p is None:
                return
            cst["pend"] = None
            gs, slot, dbuh, tcar, hc_pool, db_p, first, tkncm = p
            eng = "gpsimd" if hc_pool else "vector"
            thc = sc.emit(eng, lambda e, _, gs=gs, dbuh=dbuh:
                e.tensor_tensor(
                    dbuh, dbuh,
                    ar(A_NCM, SH * T)[:, gs * T:(gs + SG) * T], ALU.mult),
                waits=[tcar, tkncm], inc=True)
            if hc_pool:
                cst["pool"] = thc
            ysl = ar(A_Y, NEB * T)[:, db_p * T:(db_p + 1) * T]
            if first:
                ty1 = sc.emit("vector", lambda e, _, ysl=ysl, dbuh=dbuh:
                    e.tensor_tensor(ysl, dbuh[:, 0:T], dbuh[:, T:2 * T],
                                    ALU.add),
                    waits=[thc], inc=True)
            else:
                ty0 = sc.emit("vector", lambda e, _, ysl=ysl, dbuh=dbuh:
                    e.tensor_tensor(ysl, ysl, dbuh[:, 0:T], ALU.add),
                    waits=[thc], inc=True)
                ty1 = sc.emit("vector", lambda e, _, ysl=ysl,
                              dbuh=dbuh: e.tensor_tensor(
                        ysl, ysl, dbuh[:, T:2 * T], ALU.add),
                    waits=[ty0], inc=True)
            tk_slotfree[slot] = ty1
            cst["y"] = ty1

        for half in range(S // SH):
            flush_pend()
            nbm = ar(A_NBM, SH * T)
            ncm = ar(A_NCM, SH * T)
            tk_nbm = sc.dma(
                nbm.rearrange("p (s t) -> p s t", t=T),
                s_bm[half * SH:(half + 1) * SH, :].partition_broadcast(128),
                waits=[tk_bm_dma, cst["burst"], tdrl])
            tk_ncm = sc.dma(
                ncm.rearrange("p (s t) -> p s t", t=T),
                s_bm[16 + half * SH:16 + (half + 1) * SH, :]
                .partition_broadcast(128),
                waits=[tk_bm_dma, cst["y"]])
            for db in range(NEB):
                bank = db % 2
                sp1 = ar(A_SP1, T)
                sp2 = ar(A_SP2, T)
                tmm = None
                for p, (wa, ua) in enumerate((
                        (dtw_hi, dtrh_sb), (dtw_hi, dtrl_sb),
                        (dtw_lo, dtrh_sb))):
                    tmm = sc.emit("tensor", lambda e, _, wa=wa, ua=ua, db=db,
                                  p=p, bank=bank: e.matmul(
                        Pb[4 + bank][:], wa[:, db * 128:(db + 1) * 128], ua,
                        start=(p == 0), stop=(p == 2)),
                        waits=([tdrl, tk_dtp_evac[bank]] if p == 0 else []),
                        inc=(p == 2))
                t1 = act(0, sp1, Pb[4 + bank][:], AF.Tanh,
                         [tmm, tk_dt_prev,
                          tk_half if (c == 0 and half == 0 and db == 0)
                          else None],
                         scale=halfc[:, 1:2], bias=dtb2col[:, db:db + 1])
                tk_dtp_evac[bank] = t1
                tdt = act(0, DT, sp1, AF.Ln, [t1, tk_dtu_prev],
                          scale=halfc[:, 0:1], bias=halfc[:, 0:1])
                tk_dt_prev = tdt
                tdtu = sc.emit("vector", lambda e, _, db=db:
                    e.scalar_tensor_tensor(
                        DTU, DT, -1.0,
                        ar(A_XC, NEB * T)[:, db * T:(db + 1) * T],
                        ALU.mult, ALU.mult),
                    waits=[tdt, tk_xc[db]], inc=True)
                tk_dtu_prev = tdtu
                for g in range(SH // SG):
                    s0 = half * SH + g * SG
                    gs = g * SG
                    slot = gi % 2
                    dbuh = ar(A_DBUH + slot * (SG * T), SG * T)
                    flush_pend()
                    tdbu = sc.emit("vector", lambda e, _, gs=gs, dbuh=dbuh:
                        e.tensor_tensor(
                            dbuh.rearrange("p (s t) -> p s t", t=T),
                            DTU.unsqueeze(1).broadcast_to([128, SG, T]),
                            ar(A_NBM, SH * T)[:, gs * T:(gs + SG) * T]
                            .rearrange("p (s t) -> p s t", t=T),
                            ALU.mult),
                        waits=[tdtu, tk_nbm, tk_slotfree[slot],
                               cst["burst"]], inc=True)
                    cst["pool"] = tdbu
                    tk_dbu_last = tdbu
                    tsc_ = None
                    for si in range(SG):
                        s = s0 + si
                        da_s = gi2 % 2
                        gi2 += 1
                        da = ar(A_DA if da_s == 0 else A_DA2, T)
                        tda = act(0, da, DT, AF.Exp,
                                  [tdt, tk_scan2[da_s]],
                                  scale=acol[:, db * S + s:db * S + s + 1])
                        hseg = dbuh[:, si * T:(si + 1) * T]
                        init = (0.0 if c == 0 else
                                carry[:, db * 16 + s:db * 16 + s + 1])
                        tsc_ = sc.emit("vector", lambda e, _, hseg=hseg,
                                       init=init, da=da: e.tensor_tensor_scan(
                            hseg, da, hseg, init, ALU.mult, ALU.add),
                            waits=[tda, cst["pool"] if si == 0 else None],
                            inc=True)
                        tk_scan2[da_s] = tsc_
                    cst["burst"] = tsc_
                    tcar = sc.emit("gpsimd", lambda e, _, db=db, s0=s0,
                                   dbuh=dbuh: e.tensor_copy(
                            carry[:, db * 16 + s0:db * 16 + s0 + SG],
                            dbuh.rearrange(
                                "p (s t) -> p s t", t=T)[:, :, T - 1]),
                        waits=[tsc_], inc=True)
                    cst["pend"] = (gs, slot, dbuh, tcar, False, db,
                                   half == 0 and g == 0, tk_ncm)
                    gi += 1
                if half < 2 and db % 2 == 0:
                    emit_z(half * 8 + db // 2)
        flush_pend()
        tk_scan_dv = cst["y"]

        if debug == "y" and c == 0:
            sc.dma(dbg_h[:, 0:NEB * T], ar(A_Y, NEB * T), waits=[tk_scan_dv])

        tk_evacD = {0: None, 1: None}
        tk_gate = []
        for eb in range(NEB):
            bank = eb % 2
            szb = ar(A_SZ + bank * 512, T)
            tsz = sc.dma(szb, s_sz[eb * 128:(eb + 1) * 128, :],
                         waits=[tk_evacD[bank]], ch="cp")
            ysl = ar(A_Y, NEB * T)[:, eb * T:(eb + 1) * T]
            xcs = ar(A_XC, NEB * T)[:, eb * T:(eb + 1) * T]
            tyt = sc.emit("vector", lambda e, _, ysl=ysl, xcs=xcs, eb=eb:
                e.scalar_tensor_tensor(
                    ysl, xcs, dcol[:, eb:eb + 1], ysl, ALU.mult, ALU.add),
                waits=[tsz, tk_scan_dv], inc=True)
            tyg = sc.emit("vector", lambda e, _, ysl=ysl, szb=szb:
                e.tensor_tensor(ysl, ysl, szb, ALU.mult),
                waits=[tyt], inc=True)
            tk_evacD[bank] = tyg
            tk_gate.append(tyg)

        if debug == "yg" and c == 0:
            sc.dma(dbg_h[:, 0:NEB * T], ar(A_Y, NEB * T), waits=[tk_gate[-1]])

        tk_op_mm = None
        for db in range(NEB):
            ysl = ar(A_Y, NEB * T)[:, db * T:(db + 1) * T]
            typh = act(0, yph_sb, ysl, AF.Copy,
                       [tk_gate[db], tk_gate[-1], tk_op_mm])
            typl = sc.emit("vector", lambda e, _, ysl=ysl: e.tensor_tensor(
                ypl_sb, ysl, yph_sb, ALU.subtract),
                waits=[typh], inc=True)
            bank = db % 2
            wsh, wsl = wshb[bank], wslb[bank]
            tw1 = sc.dma(wsh[:, 0:1024], s_outh[db * 128:(db + 1) * 128, :],
                         waits=[tk_wfree[bank]], ch="cw0")
            tw2 = sc.dma(wsl[:, 0:1024], s_outl[db * 128:(db + 1) * 128, :],
                         waits=[tk_wfree[bank]], ch="cw1")
            for p, (ya, wa) in enumerate((
                    (yph_sb, wshb[bank]), (ypl_sb, wshb[bank]),
                    (yph_sb, wslb[bank]))):
                for tt in range(NTT):
                    for eh in range(2):
                        tk_op_mm = sc.emit("tensor",
                            lambda e, _, ya=ya, wa=wa, tt=tt, eh=eh, p=p,
                            db=db: e.matmul(
                                Pb[tt * 2 + eh][:],
                                ya[:, tt * 128:(tt + 1) * 128],
                                wa[:, eh * 512:(eh + 1) * 512],
                                start=(db == 0 and p == 0),
                                stop=(db == NEB - 1 and p == 2)),
                            waits=([typl, tw1, tw2, tk_evacD[0], tk_evacD[1]]
                                   if (p == 0 and tt == 0 and eh == 0)
                                   else []),
                            inc=(p == 2 and tt == NTT - 1 and eh == 1))
            tk_wfree[bank] = tk_op_mm

        if debug == "m" and c == 0:
            for i in range(8):
                tcp = sc.emit("vector", lambda e, _, i=i: e.tensor_copy(
                    ar(A_NBM, 4096)[:, i * 512:(i + 1) * 512], Pb[i][:]),
                    waits=[tk_op_mm], inc=True)
            sc.dma(dbg_h[:, 0:4096], ar(A_NBM, 4096), waits=[tcp])

        tk_a = None
        for tt in range(NTT):
            tt2 = sc.emit("vector", lambda e, _, tt=tt: e.tensor_scalar(
                ar(A_T2, 1024), x_sb[:, tt * D:(tt + 1) * D],
                cols[:, 8 + tt:9 + tt], cols[:, 12 + tt:13 + tt],
                ALU.subtract, ALU.mult), waits=[tk_op_mm], inc=True)
            for eh in range(2):
                tk_a = sc.emit("vector", lambda e, _, tt=tt, eh=eh:
                    e.tensor_tensor(
                        ar(A_AB, NTT * D)[
                            :, tt * D + eh * 512:tt * D + (eh + 1) * 512],
                        ar(A_T2, 1024)[:, eh * 512:(eh + 1) * 512],
                        Pb[tt * 2 + eh][:], ALU.add), waits=[tt2], inc=True)
        if debug == "a" and c == 0:
            sc.dma(dbg_h[:, 0:NTT * D], ar(A_AB, NTT * D), waits=[tk_a])
        tsum = emit_ln_sums(
            lambda tt: ar(A_AB, NTT * D)[:, tt * D:(tt + 1) * D],
            lambda tt: [tk_a] if tt == 0 else [], trash=A_NBM)
        tRA = emit_ln_cols(tsum, 40, 44)
        tk_x1 = None
        for tt in range(NTT):
            tq = sc.emit("vector", lambda e, _, tt=tt: e.tensor_scalar(
                ar(A_NBM, D), ar(A_AB, NTT * D)[:, tt * D:(tt + 1) * D],
                cols[:, 40 + tt:41 + tt], cols[:, 44 + tt:45 + tt],
                ALU.subtract, ALU.mult), waits=[tRA], inc=True)
            tk_x1 = sc.emit("vector", lambda e, _, tt=tt: e.tensor_tensor(
                x_sb[:, tt * D:(tt + 1) * D], x_sb[:, tt * D:(tt + 1) * D],
                ar(A_NBM, D), ALU.add), waits=[tq], inc=True)

        if debug == "x1" and c == 0:
            tdmp = sc.dma(dbg_h[:, 0:NTT * D], x_sb, waits=[tk_x1])
            tk_x1 = sc.emit("vector", lambda e, _: e.engine_nop(),
                            waits=[tdmp], inc=True)

        tk_bfcL = sc.dma(ar(E_BFC, HID), s_bfcB, waits=[tk_bfcB, tk_x1])
        tk_bpjL = sc.dma(ar(E_BPJ, D), s_bpjB, waits=[tk_bpjB])
        tsum = emit_ln_sums(
            lambda tt: x_sb[:, tt * D:(tt + 1) * D],
            lambda tt: [tk_x1] if tt == 0 else [])
        tR3 = emit_ln_cols(tsum, 48, 52)

        tk_tpfree = None
        tk_fc_mm = {0: None, 1: None}
        tk_pj_mm = None
        tk_pjw = {0: None, 1: None}
        tk_f = None
        for tt in range(NTT):
            tq = sc.emit("vector", lambda e, _, tt=tt: e.tensor_scalar(
                ar(E_A, D), x_sb[:, tt * D:(tt + 1) * D],
                cols[:, 48 + tt:49 + tt], cols[:, 52 + tt:53 + tt],
                ALU.subtract, ALU.mult), waits=[tR3], inc=True)
            tq = emit_quant(
                ar(E_A, D), 32,
                ar(E_X2Q, NTT * 512, BF16)[:, tt * D:(tt + 1) * D],
                [tq], E_QA, E_QC)
            if debug == "ffn" and c == 0 and tt == 0:
                tq2_ = sc.dma(dbg_h.bitcast(BF16)[:, 16384:17408],
                              ar(E_X2Q, NTT * 512, BF16)[:, 0:D], waits=[tq])
                tq = sc.emit("vector", lambda e, _: e.engine_nop(),
                             waits=[tq2_], inc=True)
            tp = None
            for kd in range(KD):
                tp = sc.emit("tensor", lambda e, _, tt=tt, kd=kd: e.transpose(
                    Pb[3].bitcast(BF16)[:, kd * 128:(kd + 1) * 128],
                    ar(E_X2Q, NTT * 512, BF16)[
                        :, tt * D + kd * 128:tt * D + (kd + 1) * 128],
                    id_bf),
                    waits=[tq, tk_tpfree] if kd == 0 else [], inc=(kd == 7))
            tk_x2qT = sc.emit("vector", lambda e, _: e.tensor_copy(
                ar(E_X2QT, 512, BF16), Pb[3].bitcast(BF16)[:, 0:1024]),
                waits=[tp], inc=True)
            tk_tpfree = tk_x2qT

            tk_fcev = {0: None, 1: None}
            for nb in range(8):
                par = nb % 2
                bank = 4 + par
                wfcs = ar(E_WFC + par * 2048, 2048, BF16)
                twf = sc.dma(
                    wfcs.rearrange("p (kd n) -> p kd n", n=512),
                    s_fcq.rearrange("(kd p) n -> p kd n", p=128)[
                        :, :, nb * 512:(nb + 1) * 512],
                    waits=[tk_fc_mm[par], tk_pj_mm,
                           tk_x1 if (tt == 0 and nb <= 1) else None,
                           tk_prologue if (c == 0 and tt == 0 and nb == 0)
                           else None], ch="cw0")
                tmm = None
                for kd in range(KD):
                    tmm = sc.emit("tensor", lambda e, _, kd=kd, bank=bank,
                                  wfcs=wfcs: e.matmul(
                            Pb[bank][:],
                            ar(E_X2QT, 512, BF16)[:, kd * 128:(kd + 1) * 128],
                            wfcs[:, kd * 512:(kd + 1) * 512],
                            start=(kd == 0), stop=(kd == KD - 1)),
                        waits=([twf, tk_x2qT, tk_fcev[par]]
                               if kd == 0 else []),
                        inc=(kd == KD - 1))
                tk_fc_mm[par] = tmm
                te1 = sc.emit("vector", lambda e, _, nb=nb, bank=bank:
                    e.tensor_tensor(
                        ar(E_GB, T), Pb[bank][:],
                        ar(E_BFC, HID)[:, nb * 512:(nb + 1) * 512], ALU.add),
                    waits=[tmm, tk_bfcL], inc=True)
                if debug == "h0" and c == 0 and tt == 0 and nb == 0:
                    td_ = sc.dma(dbg_h[:, 0:T], ar(E_GB, T), waits=[te1])
                    td_ = sc.dma(dbg_h[:, T:T + HID], ar(E_BFC, HID),
                                 waits=[td_])
                    te1 = sc.emit("vector", lambda e, _: e.engine_nop(),
                                  waits=[td_], inc=True)
                te2 = act(0, ar(E_GB2, T), ar(E_GB, T), AF.Gelu, [te1])
                tq2 = emit_quant(
                    ar(E_GB2, T), 16,
                    ar(E_HQ, 2048, BF16)[:, nb * 512:(nb + 1) * 512],
                    [te2], E_QA, E_QC)
                tk_fcev[nb % 2] = tq2

            if debug == "hq0" and c == 0 and tt == 0:
                td_ = sc.dma(dbg_h.bitcast(BF16)[:, 0:HID],
                             ar(E_HQ, 2048, BF16),
                             waits=[tk_fcev[0], tk_fcev[1]])
                tk_fcev[1] = sc.emit("vector", lambda e, _: e.engine_nop(),
                                     waits=[td_], inc=True)
            tlast = tk_fcev[1]
            for rnd in range(4):
                tp = None
                for k in range(8):
                    hb = rnd * 8 + k
                    tp = sc.emit("tensor", lambda e, _, hb=hb, k=k:
                        e.transpose(
                            Pb[3].bitcast(BF16)[:, k * 128:(k + 1) * 128],
                            ar(E_HQ, 2048, BF16)[:, hb * 128:(hb + 1) * 128],
                            id_bf),
                        waits=[tk_fcev[0], tk_fcev[1], tlast]
                              if k == 0 else [], inc=(k == 7))
                tlast = sc.emit("vector", lambda e, _, rnd=rnd: e.tensor_copy(
                    ar(E_HQT, 2048, BF16)[:, rnd * 1024:(rnd + 1) * 1024],
                    Pb[3].bitcast(BF16)[:, 0:1024]), waits=[tp], inc=True)
            tk_tpfree = tlast

            for hbp in range(16):
                par = hbp % 2
                pslab = ar(E_WPJ + par * 1024, 1024, BF16)
                twp = sc.dma(
                    pslab.rearrange("p (two e) -> p two e", e=1024),
                    s_pjq[hbp * 256:(hbp + 1) * 256, :].rearrange(
                        "(two p) e -> p two e", p=128),
                    waits=[tk_pjw[par],
                           tk_fc_mm[0] if hbp == 0 else None,
                           tk_fc_mm[1] if hbp == 0 else None], ch="cw1")
                for two in range(2):
                    hb = hbp * 2 + two
                    for eh in range(2):
                        tk_pj_mm = sc.emit("tensor",
                            lambda e, _, hb=hb, eh=eh, two=two, pslab=pslab,
                            tlast=tlast: e.matmul(
                                Pb[6 + eh][:],
                                ar(E_HQT, 2048, BF16)[
                                    :, hb * 128:(hb + 1) * 128],
                                pslab[:, two * 1024 + eh * 512:
                                      two * 1024 + (eh + 1) * 512],
                                start=(hb == 0), stop=(hb == 31)),
                            waits=([twp, tlast] if (two == 0 and eh == 0)
                                   else []),
                            inc=(two == 1 and eh == 1))
                tk_pjw[par] = tk_pj_mm
            for eh in range(2):
                tk_f = sc.emit("vector", lambda e, _, tt=tt, eh=eh:
                    e.tensor_tensor(
                        ar(E_F, NTT * D)[
                            :, tt * D + eh * 512:tt * D + (eh + 1) * 512],
                        Pb[6 + eh][:],
                        ar(E_BPJ, D)[:, eh * 512:(eh + 1) * 512], ALU.add),
                    waits=[tk_pj_mm, tk_bpjL], inc=True)

        if debug == "wq" and c == 0:
            sc.dma(dbg_h.bitcast(BF16)[:, 0:HID], s_fcq[0:128, :],
                   waits=[tk_prologue])
            sc.dma(dbg_h.bitcast(BF16)[:, HID:HID + D], s_pjq[0:128, :],
                   waits=[tk_prologue])
        if debug == "ffn" and c == 0:
            sc.dma(dbg_h[:, 0:NTT * D], ar(E_F, NTT * D), waits=[tk_f])
            sc.dma(dbg_h.bitcast(BF16)[:, 2 * NTT * D:3 * NTT * D],
                   ar(E_X2Q, NTT * 512, BF16), waits=[tk_f])
            sc.dma(dbg_h.bitcast(BF16)[:, 3 * NTT * D:3 * NTT * D + 4096],
                   ar(E_HQ, 2048, BF16), waits=[tk_f])
        tsum = emit_ln_sums(
            lambda tt: ar(E_F, NTT * D)[:, tt * D:(tt + 1) * D],
            lambda tt: [tk_f] if tt == 0 else [])
        tR4 = emit_ln_cols(tsum, 56, 60)
        tk_x2 = None
        for tt in range(NTT):
            tq = sc.emit("vector", lambda e, _, tt=tt: e.tensor_scalar(
                ar(E_A, D), ar(E_F, NTT * D)[:, tt * D:(tt + 1) * D],
                cols[:, 56 + tt:57 + tt], cols[:, 60 + tt:61 + tt],
                ALU.subtract, ALU.mult), waits=[tR4], inc=True)
            tk_x2 = sc.emit("vector", lambda e, _, tt=tt: e.tensor_tensor(
                x_sb[:, tt * D:(tt + 1) * D], x_sb[:, tt * D:(tt + 1) * D],
                ar(E_A, D), ALU.add), waits=[tq], inc=True)

        tk_out_dma = sc.dma(
            out_h[r0:r0 + T, :].rearrange("(tt p) d -> p tt d", p=128),
            x_sb.rearrange("p (tt d) -> p tt d", d=D),
            waits=[tk_x2])

    finals = {ch: sc.cnt[ch] for ch in DMA_CH}

    with contextlib.ExitStack() as es:
        e = es.enter_context
        sems = {eng: e(nc.semaphore(f"sem_{eng}")) for eng in ENGS}
        dma_sems = {ch: e(nc.semaphore(f"sem_{ch}")) for ch in DMA_CH}
        block = e(nc.Block())

        @block.tensor
        def _(h):
            sc.replay("tensor", h, sems, dma_sems)

        @block.vector
        def _(h):
            sc.replay("vector", h, sems, dma_sems)

        @block.scalar
        def _(h):
            sc.replay("scalar", h, sems, dma_sems)

        @block.gpsimd
        def _(h):
            sc.replay("gpsimd", h, sems, dma_sems)

        @block.sync
        def _(h):
            sc.replay("sync", h, sems, dma_sems)
            for ch in DMA_CH:
                if finals[ch]:
                    h.wait_ge(dma_sems[ch], finals[ch])

    return nc


_CACHE = {}


def _get_nc():
    if "nc" not in _CACHE:
        _CACHE["nc"] = build_program()
    return _CACHE["nc"]


def make_inputs(L0_attn_x, L0_attn_ln_gamma, L0_attn_ln_beta, L0_attn_in_proj_w,
                L0_attn_conv_w, L0_attn_conv_b, L0_attn_x_proj_w,
                L0_attn_dt_proj_w, L0_attn_dt_proj_b, L0_attn_A_log, L0_attn_D,
                L0_attn_out_proj_w, L0_ffn_w_fc, L0_ffn_b_fc, L0_ffn_w_proj,
                L0_ffn_b_proj):
    import ml_dtypes

    def np32(a):
        return np.ascontiguousarray(np.asarray(a), dtype=np.float32)

    x = np32(L0_attn_x)

    def colsify(v, nb):
        return np.ascontiguousarray(np32(v).reshape(nb, 128).T)

    common = {
        "gammaC": colsify(L0_attn_ln_gamma, KD),
        "betaC": colsify(L0_attn_ln_beta, KD),
        "winT": np.ascontiguousarray(np32(L0_attn_in_proj_w).T),
        "cwC": np.ascontiguousarray(
            np32(L0_attn_conv_w).reshape(NEB, 128, 4).transpose(1, 0, 2)
            .reshape(128, NEB * 4)),
        "cbC": colsify(L0_attn_conv_b, NEB),
        "xpwT": np.ascontiguousarray(np32(L0_attn_x_proj_w).T),
        "dtwT": np.ascontiguousarray(np32(L0_attn_dt_proj_w).T),
        "dtbC": colsify(L0_attn_dt_proj_b, NEB),
        "alogC": np.ascontiguousarray(
            np32(L0_attn_A_log).reshape(NEB, 128, S).transpose(1, 0, 2)
            .reshape(128, NEB * S)),
        "dC": colsify(L0_attn_D, NEB),
        "woutT": np.ascontiguousarray(np32(L0_attn_out_proj_w).T),
        "wfc": np32(L0_ffn_w_fc),
        "bfc": np.ascontiguousarray(np32(L0_ffn_b_fc).reshape(1, HID)),
        "wproj": np32(L0_ffn_w_proj),
        "bproj": np.ascontiguousarray(np32(L0_ffn_b_proj).reshape(1, D)),
        "idf32": np.eye(128, dtype=np.float32),
        "idbf": np.eye(128, dtype=ml_dtypes.bfloat16),
        "onesf": np.ones((1, 128), dtype=np.float32),
    }
    return [dict(common, x=np.ascontiguousarray(x[b])) for b in range(8)]


def kernel(**inputs):
    in_maps = make_inputs(**inputs)
    nc = _get_nc()
    res = run_bass_kernel_spmd(nc, in_maps, core_ids=list(range(8)))
    return np.stack([np.asarray(r["out"], dtype=np.float32)
                     for r in res.results], axis=0)


# version 21 (final)
# speedup vs baseline: 1.1705x; 1.0021x over previous
import contextlib
import numpy as np
import concourse.bass as bass
import concourse.mybir as mybir
from concourse.bass_utils import run_bass_kernel_spmd

F32 = mybir.dt.float32
F32R = mybir.dt.float32r
BF16 = mybir.dt.bfloat16
I32 = mybir.dt.int32
AF = mybir.ActivationFunctionType
ALU = mybir.AluOpType
AX = mybir.AxisListType

L, D, DI, S, RK, HID = 4096, 1024, 2048, 16, 64, 4096
T = 512
NCH = L // T
NTT = T // 128
KD = D // 128
NEB = DI // 128
SG = 2
SH = 4
EPS = 1e-5
CMUL = 1.5 * (2.0 ** 20)
TINY = 1.17549435e-38

ENGS = ("tensor", "vector", "scalar", "gpsimd", "sync")


DMA_CH = ("cio", "cw0", "cw1", "cp")


class Sched:

    def __init__(self):
        self.q = {e: [] for e in ENGS}
        self.cnt = {e: 0 for e in ENGS}
        for ch in DMA_CH:
            self.cnt[ch] = 0

    def _res(self, waits):
        out = []
        for w in waits:
            if w is None:
                continue
            owner, c = w
            if owner in DMA_CH:
                c = self.cnt[owner]
            out.append((owner, c))
        return tuple(out)

    def emit(self, eng, fn, waits=(), inc=False):
        if inc:
            self.cnt[eng] += 1
        self.q[eng].append((self._res(waits), fn, bool(inc)))
        return (eng, self.cnt[eng]) if inc else None

    def dma(self, out, in_, waits=(), ch="cio"):
        rw = self._res(waits)
        self.cnt[ch] += 16

        def fn(e, _):
            return e.dma_start(out=out, in_=in_)

        self.q["sync"].append((rw, fn, ch))
        return (ch, self.cnt[ch])

    def tok(self, eng):
        return (eng, self.cnt[eng])

    def replay(self, eng, handle, sems, dma_sems):
        waited = {}
        for waits, fn, inc in self.q[eng]:
            for owner, c in waits:
                if c <= 0 or waited.get(owner, 0) >= c:
                    continue
                waited[owner] = c
                handle.wait_ge(dma_sems[owner] if owner in DMA_CH
                               else sems[owner], c)
            inst = fn(handle, inc)
            if inc in DMA_CH:
                inst.then_inc(dma_sems[inc], 16)
            elif inc:
                inst.then_inc(sems[eng], 1)


def build_program(debug=None, nch=NCH, pstop=8, psel=None):
    if psel is None:
        psel = set(range(1, pstop + 1))
    nc = bass.Bass("TRN2", target_bir_lowering=False)
    sc = Sched()

    x_h = nc.dram_tensor("x", [L, D], F32, kind="ExternalInput").ap()
    out_h = nc.dram_tensor("out", [L, D], F32, kind="ExternalOutput").ap()
    gammaC_h = nc.dram_tensor("gammaC", [128, KD], F32, kind="ExternalInput").ap()
    betaC_h = nc.dram_tensor("betaC", [128, KD], F32, kind="ExternalInput").ap()
    winT_h = nc.dram_tensor("winT", [D, 2 * DI], F32, kind="ExternalInput").ap()
    cwC_h = nc.dram_tensor("cwC", [128, NEB * 4], F32, kind="ExternalInput").ap()
    cbC_h = nc.dram_tensor("cbC", [128, NEB], F32, kind="ExternalInput").ap()
    xpwT_h = nc.dram_tensor("xpwT", [DI, 96], F32, kind="ExternalInput").ap()
    dtwT_h = nc.dram_tensor("dtwT", [RK, DI], F32, kind="ExternalInput").ap()
    dtbC_h = nc.dram_tensor("dtbC", [128, NEB], F32, kind="ExternalInput").ap()
    alogC_h = nc.dram_tensor("alogC", [128, NEB * S], F32, kind="ExternalInput").ap()
    dC_h = nc.dram_tensor("dC", [128, NEB], F32, kind="ExternalInput").ap()
    woutT_h = nc.dram_tensor("woutT", [DI, D], F32, kind="ExternalInput").ap()
    wfc_h = nc.dram_tensor("wfc", [HID, D], F32, kind="ExternalInput").ap()
    bfc_h = nc.dram_tensor("bfc", [1, HID], F32, kind="ExternalInput").ap()
    wproj_h = nc.dram_tensor("wproj", [D, HID], F32, kind="ExternalInput").ap()
    bproj_h = nc.dram_tensor("bproj", [1, D], F32, kind="ExternalInput").ap()
    idf32_h = nc.dram_tensor("idf32", [128, 128], F32, kind="ExternalInput").ap()
    idbf_h = nc.dram_tensor("idbf", [128, 128], BF16, kind="ExternalInput").ap()
    onesf_h = nc.dram_tensor("onesf", [1, 128], F32, kind="ExternalInput").ap()

    dbg_h = None
    if debug is not None:
        dbg_h = nc.dram_tensor("dbg", [128, 16384], F32,
                               kind="ExternalOutput").ap()

    s_inh = nc.dram_tensor("s_inh", [D, 2 * DI], F32R, kind="Internal").ap()
    s_inl = nc.dram_tensor("s_inl", [D, 2 * DI], F32R, kind="Internal").ap()
    s_outh = nc.dram_tensor("s_outh", [DI, D], F32R, kind="Internal").ap()
    s_outl = nc.dram_tensor("s_outl", [DI, D], F32R, kind="Internal").ap()
    s_fcq = nc.dram_tensor("s_fcq", [D, HID], BF16, kind="Internal").ap()
    s_pjq = nc.dram_tensor("s_pjq", [HID, D], BF16, kind="Internal").ap()
    s_xpwh = nc.dram_tensor("s_xpwh", [128, NEB * 96], F32R, kind="Internal").ap()
    s_xpwl = nc.dram_tensor("s_xpwl", [128, NEB * 96], F32R, kind="Internal").ap()
    s_bfcB = nc.dram_tensor("s_bfcB", [128, HID], F32, kind="Internal").ap()
    s_bpjB = nc.dram_tensor("s_bpjB", [128, D], F32, kind="Internal").ap()
    s_bm = nc.dram_tensor("s_bm", [32, T], F32, kind="Internal").ap()
    s_sz = nc.dram_tensor("s_sz", [NEB * 128, T], F32, kind="Internal").ap()

    uTh = nc.alloc_sbuf_tensor("uTh", [128, KD * T], F32R).ap()
    uTl = nc.alloc_sbuf_tensor("uTl", [128, KD * T], F32R).ap()
    x_sb = nc.alloc_sbuf_tensor("x_sb", [128, NTT * D], F32).ap()
    wshb = [nc.alloc_sbuf_tensor(f"wsh{i}", [128, 1024], F32R).ap()
            for i in range(2)]
    wslb = [nc.alloc_sbuf_tensor(f"wsl{i}", [128, 1024], F32R).ap()
            for i in range(2)]
    dtw_hi = nc.alloc_sbuf_tensor("dtw_hi", [64, DI], F32R).ap()
    dtw_lo = nc.alloc_sbuf_tensor("dtw_lo", [64, DI], F32R).ap()
    id_f32 = nc.alloc_sbuf_tensor("id_f32", [128, 128], F32).ap()
    id_bf = nc.alloc_sbuf_tensor("id_bf", [128, 128], BF16).ap()
    onesf = nc.alloc_sbuf_tensor("onesf_sb", [1, 128], F32).ap()
    acol = nc.alloc_sbuf_tensor("acol", [128, NEB * S], F32).ap()
    dcol = nc.alloc_sbuf_tensor("dcol", [128, NEB], F32).ap()
    cwcol = nc.alloc_sbuf_tensor("cwcol", [128, NEB * 4], F32).ap()
    cbcol = nc.alloc_sbuf_tensor("cbcol", [128, NEB], F32).ap()
    dtbcol = nc.alloc_sbuf_tensor("dtbcol", [128, NEB], F32).ap()
    dtb2col = nc.alloc_sbuf_tensor("dtb2col", [128, NEB], F32).ap()
    halfc = nc.alloc_sbuf_tensor("halfc", [128, 2], F32).ap()
    gcol = nc.alloc_sbuf_tensor("gcol", [128, KD], F32).ap()
    bcol = nc.alloc_sbuf_tensor("bcol", [128, KD], F32).ap()
    wbeta = nc.alloc_sbuf_tensor("wbeta", [128, 32], F32).ap()
    carry = nc.alloc_sbuf_tensor("carry", [128, 256], F32).ap()
    tails = nc.alloc_sbuf_tensor("tails", [128, NEB * 3], F32).ap()
    cols = nc.alloc_sbuf_tensor("cols", [128, 64], F32).ap()
    xch_sb = nc.alloc_sbuf_tensor("xch_sb", [128, T], F32R).ap()
    xcl_sb = nc.alloc_sbuf_tensor("xcl_sb", [128, T], F32R).ap()
    dtrh_sb = nc.alloc_sbuf_tensor("dtrh_sb", [64, T], F32R).ap()
    dtrl_sb = nc.alloc_sbuf_tensor("dtrl_sb", [64, T], F32R).ap()
    yph_sb, ypl_sb = xch_sb, xcl_sb
    xpwh_sb = nc.alloc_sbuf_tensor("xpwh_sb", [128, 768], F32R).ap()
    xpwl_sb = nc.alloc_sbuf_tensor("xpwl_sb", [128, 768], F32R).ap()

    AR_N = 26624
    arena = nc.alloc_sbuf_tensor("arena", [128, AR_N], F32).ap()

    def ar(off, n, dt=F32):
        a = arena[:, off:off + n]
        if dt is not F32:
            a = a.bitcast(dt)
        return a

    A_U32, A_U32T, A_TRASH = 0, 4096, 8192
    A_XC, A_Y = 0, 8192
    A_NBM, A_NCM = 16384, 18432
    A_DBUH = 20480
    A_DT, A_DTU, A_DA = 22528, 23040, 23552
    A_SP1, A_SP2 = 24064, 24576
    A_DA2 = 26112
    A_DTR32 = 16384
    A_XIN = 16896
    A_AB = 8192
    A_T2 = 24064
    A_SZ = 25088
    A_YPH, A_YPL = 20480, 20992
    E_A, E_F = 0, 4096
    E_X2Q, E_X2QT = 8192, 10240
    E_HQ = 10752
    E_HQT = 12800
    E_GB, E_GB2 = 14848, 15360
    E_BFC, E_BPJ = 16384, 20480
    E_QA, E_QC = 21504, 21760
    E_WFC = 22016
    E_WPJ = 22016
    W0, W1, W2 = 0, 4096, 8192
    P_QA, P_QC = 12288, 12544

    bmcm = ar(A_Y, T)

    Pb = [nc.alloc_psum_tensor(f"pb{i}", [128, 512], F32).ap() for i in range(8)]

    def act(eng_ignore, out, in_, func, waits, inc=True, **kw):
        if func is AF.Copy and any(
                not isinstance(kw.get(k, 0.0), (int, float))
                for k in ("bias", "scale")):
            func = AF.Identity
        return sc.emit("scalar", lambda e, _: e.activation(out, in_, func, **kw),
                       waits=waits, inc=inc)

    def emit_ln_cols(sum_tok, mud, rd, gd=None):
        S0, S1, S2, S3 = 20, 24, 28, 32
        t = sc.emit("vector", lambda e, _: e.tensor_scalar_mul(
            cols[:, mud:mud + 4], cols[:, 0:4], 1.0 / D), [sum_tok], inc=True)
        t = sc.emit("vector", lambda e, _: e.tensor_scalar_mul(
            cols[:, S0:S0 + 4], cols[:, 4:8], 1.0 / D), [t], inc=True)
        t = sc.emit("vector", lambda e, _: e.tensor_tensor(
            cols[:, S1:S1 + 4], cols[:, mud:mud + 4], cols[:, mud:mud + 4],
            ALU.mult), [t], inc=True)
        t = sc.emit("vector", lambda e, _: e.tensor_tensor(
            cols[:, S0:S0 + 4], cols[:, S0:S0 + 4], cols[:, S1:S1 + 4],
            ALU.subtract), [t], inc=True)
        tva = sc.emit("vector", lambda e, _: e.tensor_scalar_add(
            cols[:, S1:S1 + 4], cols[:, S0:S0 + 4], EPS), [t], inc=True)
        t = act(0, cols[:, S2:S2 + 4], cols[:, S1:S1 + 4], AF.Ln, [tva])
        tr = act(0, cols[:, rd:rd + 4], cols[:, S2:S2 + 4], AF.Exp, [t],
                 scale=-0.5)
        if gd is None:
            return tr
        t = sc.emit("vector", lambda e, _: e.reciprocal(
            cols[:, S2:S2 + 4], cols[:, S1:S1 + 4]), [tva, tr], inc=True)
        t = sc.emit("vector", lambda e, _: e.tensor_tensor(
            cols[:, S2:S2 + 4], cols[:, S0:S0 + 4], cols[:, S2:S2 + 4],
            ALU.mult), [t], inc=True)
        t = sc.emit("vector", lambda e, _: e.tensor_scalar_add(
            cols[:, S2:S2 + 4], cols[:, S2:S2 + 4], EPS), [t], inc=True)
        t = act(0, cols[:, S3:S3 + 4], cols[:, S2:S2 + 4], AF.Ln, [t])
        t = act(0, cols[:, S3:S3 + 4], cols[:, S3:S3 + 4], AF.Exp, [t],
                scale=-0.5)
        tg = sc.emit("vector", lambda e, _: e.tensor_tensor(
            cols[:, gd:gd + 4], cols[:, rd:rd + 4], cols[:, S3:S3 + 4],
            ALU.mult), [t, tr], inc=True)
        return tg

    def emit_ln_sums(src_fn, waits_fn, trash=None):
        toks = []
        to = A_TRASH if trash is None else trash
        for tt in range(NTT):
            t1 = act(0, ar(to, D), src_fn(tt), AF.Copy,
                     waits_fn(tt), accum_out=cols[:, tt:tt + 1])
            t2 = act(0, ar(to, D), src_fn(tt), AF.Square, [t1],
                     accum_out=cols[:, 4 + tt:5 + tt])
            toks.append(t2)
        return toks[-1]

    def emit_quant(src_ap, nblk, out_ap, waits, qa, qc):
        t1 = sc.emit("vector", lambda e, _: e.tensor_reduce(
            ar(qa, nblk)[:, :, None],
            src_ap.rearrange("p (n b) -> p n b", b=32),
            AX.X, ALU.max, apply_absolute_value=True), waits=waits, inc=True)
        t2 = sc.emit("vector", lambda e, _: e.tensor_scalar_max(
            ar(qa, nblk), ar(qa, nblk), TINY), waits=[t1], inc=True)
        t3 = sc.emit("vector", lambda e, _: e.tensor_scalar(
            ar(qc, nblk).bitcast(I32), ar(qa, nblk).bitcast(I32),
            int(0x7F800000), None, ALU.bitwise_and), waits=[t2], inc=True)
        t4 = sc.emit("vector", lambda e, _: e.tensor_scalar_mul(
            ar(qc, nblk), ar(qc, nblk), CMUL), waits=[t3], inc=True)
        cb = ar(qc, nblk)[:, :, None].broadcast_to([128, nblk, 32])
        t5 = sc.emit("vector", lambda e, _: e.tensor_tensor(
            src_ap.rearrange("p (n b) -> p n b", b=32),
            src_ap.rearrange("p (n b) -> p n b", b=32), cb, ALU.add),
            waits=[t4], inc=True)
        t6 = sc.emit("vector", lambda e, _: e.tensor_tensor(
            out_ap.rearrange("p (n b) -> p n b", b=32),
            src_ap.rearrange("p (n b) -> p n b", b=32), cb, ALU.subtract),
            waits=[t5], inc=True)
        return t6

    tk_c = tk_alog = tk_acol = tk_dtw = tk_xpw = None
    tk_bfcB = tk_bpjB = tk_wbeta = None
    tk_pdma = tk_ppe = tk_pact = tk_ev = None
    if psel is None or 11 in psel or len(psel & {2,3,4,5,6,7,8}) > 0 and 11 not in psel and 12 not in psel:
        for dst, src in ((gcol, gammaC_h), (bcol, betaC_h), (cwcol, cwC_h),
                         (cbcol, cbC_h), (dtbcol, dtbC_h), (dcol, dC_h),
                         (id_f32, idf32_h), (id_bf, idbf_h), (onesf, onesf_h)):
            tk_c = sc.dma(dst, src)
    if psel is None or 12 in psel or len(psel & {2,3,4,5,6,7,8}) > 0 and 11 not in psel and 12 not in psel:
        tk_alog = sc.dma(ar(A_TRASH, NEB * S), alogC_h)
        t = act(0, acol, ar(A_TRASH, NEB * S), AF.Exp, [tk_alog])
        tk_acol = sc.emit("vector", lambda e, _: e.tensor_scalar_mul(
            dtb2col, dtbcol, -0.5), waits=[t, tk_c], inc=True)
        th_ = sc.emit("vector", lambda e, _: e.memset(
            halfc[:, 0:1], 0.5), waits=[tk_acol], inc=True)
        tk_half = sc.emit("vector", lambda e, _: e.memset(
            halfc[:, 1:2], -0.5), waits=[th_], inc=True)

    if 2 in psel:
        tkw = sc.dma(ar(A_U32, DI)[0:64, :], dtwT_h)
        t = act(0, dtw_hi, ar(A_U32, DI)[0:64, :], AF.Copy, [tkw])
        tk_dtw = sc.emit("vector", lambda e, _: e.tensor_tensor(
            dtw_lo, ar(A_U32, DI)[0:64, :], dtw_hi, ALU.subtract),
            waits=[t], inc=True)

    if 3 in psel or 31 in psel:
        tkw = sc.dma(ar(A_U32, NEB * 96),
                     xpwT_h.rearrange("(e p) j -> p e j", p=128), waits=[tk_dtw])
        tk_xpw = tkw
    if 3 in psel or 32 in psel:
        t = act(0, ar(A_U32T, NEB * 96, F32R), ar(A_U32, NEB * 96), AF.Copy, [tkw])
        tk_xpw = t
    if 3 in psel or 33 in psel:
        t2 = sc.emit("vector", lambda e, _: e.tensor_tensor(
            ar(A_TRASH, NEB * 96, F32R), ar(A_U32, NEB * 96),
            ar(A_U32T, NEB * 96, F32R), ALU.subtract), waits=[t], inc=True)
        tk_xpw = t2
    if 3 in psel or 34 in psel:
        sc.dma(s_xpwh, ar(A_U32T, NEB * 96, F32R), waits=[t])
        tk_xpw = sc.dma(s_xpwl, ar(A_TRASH, NEB * 96, F32R), waits=[t2])

    if 4 in psel:
        tk_bfc = sc.dma(ar(A_U32, HID)[0:1, :], bfc_h, waits=[tk_xpw])
        tk_bpj = sc.dma(ar(A_U32T, D)[0:1, :], bproj_h)
        te = None
        for j in range(8):
            tm = sc.emit("tensor", lambda e, _, j=j: e.matmul(
                Pb[0][:], onesf, ar(A_U32, HID)[0:1, j * 512:(j + 1) * 512],
                start=True, stop=True), waits=[tk_bfc, tk_c, te], inc=True)
            te = act(0, ar(A_TRASH, HID)[:, j * 512:(j + 1) * 512], Pb[0][:],
                     AF.Copy, [tm])
        tk_bfcB = sc.dma(s_bfcB, ar(A_TRASH, HID), waits=[te])
        te = None
        for j in range(2):
            tm = sc.emit("tensor", lambda e, _, j=j: e.matmul(
                Pb[1][:], onesf, ar(A_U32T, D)[0:1, j * 512:(j + 1) * 512],
                start=True, stop=True), waits=[tk_bpj, te], inc=True)
            te = act(0, ar(A_TRASH + HID, D)[:, j * 512:(j + 1) * 512], Pb[1][:],
                     AF.Copy, [tm])
        tk_bpjB = sc.dma(s_bpjB, ar(A_TRASH + HID, D), waits=[te])

    if 5 in psel:
        tk_pdma = tk_bpjB
        tk_ppe = None
        tk_pact = None
        for kd in range(KD):
            tkw = sc.dma(ar(W0, 2 * DI), winT_h[kd * 128:(kd + 1) * 128, :],
                         waits=[tk_pdma, tk_ppe, tk_pact, tk_bfcB])
            for eb in range(32):
                tk_ppe = sc.emit("tensor", lambda e, _, kd=kd, eb=eb: e.matmul(
                    Pb[2][:, eb:eb + 1], ar(W0, 2 * DI)[:, eb * 128:(eb + 1) * 128],
                    bcol[:, kd:kd + 1], start=(kd == 0 and eb == 0),
                    stop=(kd == KD - 1)),
                    waits=[tkw], inc=(eb == 31))
            tg = act(0, ar(W1, 2 * DI), ar(W0, 2 * DI), AF.Copy, [tkw],
                     scale=gcol[:, kd:kd + 1])
            th = act(0, ar(W2, 2 * DI, F32R), ar(W1, 2 * DI), AF.Copy, [tg])
            tk_pact = th
            tl = sc.emit("vector", lambda e, _: e.tensor_tensor(
                ar(W1, 2 * DI, F32R), ar(W1, 2 * DI), ar(W2, 2 * DI, F32R),
                ALU.subtract), waits=[th], inc=True)
            sc.dma(s_inh[kd * 128:(kd + 1) * 128, :], ar(W2, 2 * DI, F32R),
                   waits=[th])
            tk_pdma = sc.dma(s_inl[kd * 128:(kd + 1) * 128, :],
                             ar(W1, 2 * DI, F32R), waits=[tl])
        tk_wbeta = act(0, wbeta, Pb[2][:, 0:32], AF.Copy, [tk_ppe])

    if 6 in psel:
        for ed in range(NEB):
            tkw = sc.dma(ar(W0, D), woutT_h[ed * 128:(ed + 1) * 128, :],
                         waits=[tk_pdma, tk_pact])
            th = act(0, ar(W1, D, F32R), ar(W0, D), AF.Copy, [tkw])
            tk_pact = th
            tl = sc.emit("vector", lambda e, _: e.tensor_tensor(
                ar(W2, D, F32R), ar(W0, D), ar(W1, D, F32R), ALU.subtract),
                waits=[th], inc=True)
            sc.dma(s_outh[ed * 128:(ed + 1) * 128, :], ar(W1, D, F32R), waits=[th])
            tk_pdma = sc.dma(s_outl[ed * 128:(ed + 1) * 128, :],
                             ar(W2, D, F32R), waits=[tl])

    if 7 in psel:
        tk_ev = None
        for nb in range(32):
            tkw = sc.dma(ar(W0, D), wfc_h[nb * 128:(nb + 1) * 128, :],
                         waits=[tk_pdma, tk_pact])
            tk_pact = None
            tq = emit_quant(ar(W0, D), 32, ar(W1, 512, BF16), [tkw], P_QA, P_QC)
            tp = None
            for kb in range(8):
                tp = sc.emit("tensor", lambda e, _, kb=kb: e.transpose(
                    Pb[3].bitcast(BF16)[:, kb * 128:(kb + 1) * 128],
                    ar(W1, 512, BF16)[:, kb * 128:(kb + 1) * 128], id_bf),
                    waits=[tq, tk_ev] if kb == 0 else [], inc=(kb == 7))
            tk_ev = sc.emit("vector", lambda e, _: e.tensor_copy(
                ar(W2, 512, BF16), Pb[3].bitcast(BF16)[:, 0:1024]),
                waits=[tp], inc=True)
            tk_pdma = sc.dma(
                s_fcq.rearrange("(kb p) n -> p kb n", p=128)[
                    :, :, nb * 128:(nb + 1) * 128],
                ar(W2, 512, BF16).rearrange("p (kb n) -> p kb n", n=128),
                waits=[tk_ev])

    if 8 in psel:
        for eb in range(8):
            tkw = sc.dma(ar(W0, HID), wproj_h[eb * 128:(eb + 1) * 128, :],
                         waits=[tk_pdma])
            tq = emit_quant(ar(W0, HID), 128, ar(W1, 2048, BF16), [tkw], P_QA, P_QC)
            tlast = tq
            for rnd in range(4):
                tp = None
                for k in range(8):
                    hb = rnd * 8 + k
                    tp = sc.emit("tensor", lambda e, _, hb=hb, k=k: e.transpose(
                        Pb[3].bitcast(BF16)[:, k * 128:(k + 1) * 128],
                        ar(W1, 2048, BF16)[:, hb * 128:(hb + 1) * 128], id_bf),
                        waits=[tlast, tk_ev] if k == 0 else [], inc=(k == 7))
                tk_ev = sc.emit("vector", lambda e, _, rnd=rnd: e.tensor_copy(
                    ar(W2, 2048, BF16)[:, rnd * 1024:(rnd + 1) * 1024],
                    Pb[3].bitcast(BF16)[:, 0:1024]), waits=[tp], inc=True)
                tlast = tk_ev
            tk_pdma = sc.dma(
                s_pjq.rearrange("(hb p) e -> p hb e", p=128)[
                    :, :, eb * 128:(eb + 1) * 128],
                ar(W2, 2048, BF16).rearrange("p (hb e) -> p hb e", e=128),
                waits=[tlast])

    if 5 not in psel:
        tk_pdma = None
        tk_ev = None
    tk_prologue = tk_pdma
    tk_prologue_ev = tk_ev if 7 in psel else None

    tk_out_dma = tk_prologue

    for c in range(nch):
        r0 = c * T

        tk_x = sc.dma(
            x_sb.rearrange("p (tt d) -> p tt d", d=D),
            x_h[r0:r0 + T, :].rearrange("(tt p) d -> p tt d", p=128),
            waits=[tk_out_dma])

        tsum = emit_ln_sums(
            lambda tt: x_sb[:, tt * D:(tt + 1) * D],
            lambda tt: [tk_x, tk_wbeta, tk_acol] if tt == 0 else [])
        tg1 = emit_ln_cols(tsum, 8, 12, gd=16)

        tk_pe_prev = tk_wbeta if c == 0 else None
        tk_u = None
        for tt in range(NTT):
            tu = sc.emit("vector", lambda e, _, tt=tt: e.tensor_scalar(
                ar(A_U32, NTT * D)[:, tt * D:(tt + 1) * D],
                x_sb[:, tt * D:(tt + 1) * D],
                cols[:, 8 + tt:9 + tt], cols[:, 16 + tt:17 + tt],
                ALU.subtract, ALU.mult), waits=[tg1], inc=True)
            for grp in range(2):
                tp = None
                for k in range(4):
                    kd = grp * 4 + k
                    tp = sc.emit("tensor", lambda e, _, tt=tt, kd=kd, k=k:
                        e.transpose(
                            Pb[2][:, k * 128:(k + 1) * 128],
                            ar(A_U32, NTT * D)[
                                :, tt * D + kd * 128:tt * D + (kd + 1) * 128],
                            id_f32),
                        waits=[tu, tk_pe_prev] if k == 0 else [], inc=(k == 3))
                uh_dst = uTh.rearrange("p (kd t) -> p kd t", t=T)[
                    :, grp * 4:(grp + 1) * 4, tt * 128:(tt + 1) * 128]
                u32_dst = ar(A_U32T, NTT * D).rearrange(
                    "p (kd t) -> p kd t", t=T)[
                    :, grp * 4:(grp + 1) * 4, tt * 128:(tt + 1) * 128]
                psrc = Pb[2].rearrange("p (k t) -> p k t", t=128)
                th = act(0, uh_dst, psrc, AF.Copy, [tp])
                t32 = act(0, u32_dst, psrc, AF.Copy, [th])
                tk_pe_prev = t32
                tk_u = t32
        tk_ulo = sc.emit("vector", lambda e, _: e.tensor_tensor(
            uTl, ar(A_U32T, KD * T), uTh, ALU.subtract), waits=[tk_u], inc=True)

        if debug == "u" and c == 0:
            sc.dma(dbg_h[:, 0:NTT * D], ar(A_U32, NTT * D), waits=[tk_ulo])
            sc.dma(dbg_h[:, NTT * D:2 * NTT * D], ar(A_U32T, NTT * D),
                   waits=[tk_ulo])
            sc.dma(dbg_h[:, 2 * NTT * D:2 * NTT * D + 64], cols,
                   waits=[tk_ulo])

        tk_xpwl = None

        if c == 0:
            tk_tails = sc.emit("vector", lambda e, _: e.memset(tails, 0.0),
                               waits=[], inc=True)
        tk_wfree = {0: None, 1: None}
        tk_evac = {0: None, 1: None}
        tk_xp_last = None
        tk_xp_prev = None
        tk_xc = []
        xin = ar(A_XIN, 530)
        for eb in range(NEB):
            bank = eb % 2
            if eb % 8 == 0:
                hw0 = (eb // 8) * 768
                tk_xpwl = sc.dma(xpwh_sb, s_xpwh[:, hw0:hw0 + 768],
                                 waits=[tk_xpw, tk_xp_prev])
                tk_xpwl = sc.dma(xpwl_sb, s_xpwl[:, hw0:hw0 + 768],
                                 waits=[tk_xpwl])
            wsh, wsl = wshb[bank], wslb[bank]
            tw1 = sc.dma(
                wsh.rearrange("p (kd e) -> p kd e", e=128),
                s_inh.rearrange("(kd p) e -> p kd e", p=128)[
                    :, :, eb * 128:(eb + 1) * 128],
                waits=[tk_wfree[bank],
                       tk_prologue if (c == 0 and eb == 0) else None],
                ch="cw0")
            tw2 = sc.dma(
                wsl.rearrange("p (kd e) -> p kd e", e=128),
                s_inl.rearrange("(kd p) e -> p kd e", p=128)[
                    :, :, eb * 128:(eb + 1) * 128],
                waits=[tk_wfree[bank]], ch="cw1")
            tmm = None
            for p, (wa, ua) in enumerate(((wsh, uTh), (wsh, uTl), (wsl, uTh))):
                for kd in range(KD):
                    tmm = sc.emit("tensor", lambda e, _, wa=wa, ua=ua, kd=kd,
                                  p=p, bank=bank: e.matmul(
                        Pb[bank][:], wa[:, kd * 128:(kd + 1) * 128],
                        ua[:, kd * T:(kd + 1) * T],
                        start=(p == 0 and kd == 0),
                        stop=(p == 2 and kd == KD - 1)),
                        waits=([tw1, tw2, tk_ulo, tk_evac[bank]]
                               if (p == 0 and kd == 0) else []),
                        inc=(p == 2 and kd == KD - 1))
            tk_wfree[bank] = tmm
            if debug == "xin" and c == 0 and eb == 0:
                trec = sc.emit("vector", lambda e, _: e.tensor_tensor(
                    ar(A_Y, 1024), wshb[0], wslb[0], ALU.add), waits=[tmm], inc=True)
                sc.dma(dbg_h[:, 2048:3072], ar(A_Y, 1024), waits=[trec])
            thal = sc.emit("vector", lambda e, _, eb=eb: e.tensor_copy(
                xin[:, 0:3], tails[:, eb * 3:(eb + 1) * 3]),
                waits=[tk_tails if c == 0 else None], inc=True)
            tev = act(0, xin[:, 3:3 + T], Pb[bank][:], AF.Copy, [tmm, thal],
                      bias=wbeta[:, eb:eb + 1])
            tk_evac[bank] = tev
            sc.emit("vector", lambda e, _, eb=eb: e.tensor_copy(
                tails[:, eb * 3:(eb + 1) * 3], xin[:, T:T + 3]),
                waits=[tev], inc=True)
            tcv = sc.emit("vector", lambda e, _, eb=eb: e.tensor_scalar_mul(
                ar(A_SP1, T), xin[:, 0:T], cwcol[:, eb * 4:eb * 4 + 1]),
                waits=[tev], inc=True)
            for k in range(1, 4):
                tcv = sc.emit("vector", lambda e, _, eb=eb, k=k:
                    e.scalar_tensor_tensor(
                        ar(A_SP1, T), xin[:, k:k + T],
                        cwcol[:, eb * 4 + k:eb * 4 + k + 1], ar(A_SP1, T),
                        ALU.mult, ALU.add), waits=[], inc=(k == 3))
            if debug == "xin" and c == 0 and eb == 0:
                sc.dma(dbg_h[:, 0:530], xin, waits=[tcv])
                sc.dma(dbg_h[:, 530:530 + 512], ar(A_SP1, T), waits=[tcv])
            xcs = ar(A_XC, NEB * T)[:, eb * T:(eb + 1) * T]
            txc = act(0, xcs, ar(A_SP1, T), AF.Silu, [tcv],
                      bias=cbcol[:, eb:eb + 1])
            tk_xc.append(txc)
            tph = act(0, xch_sb, xcs, AF.Copy, [txc, tk_xp_prev])
            tpl = sc.emit("vector", lambda e, _, xcs=xcs: e.tensor_tensor(
                xcl_sb, xcs, xch_sb, ALU.subtract),
                waits=[tph], inc=True)
            for p, wa, ua in ((0, xpwh_sb, xch_sb),
                              (1, xpwh_sb, xcl_sb),
                              (2, xpwl_sb, xch_sb)):
                tk_xp_last = sc.emit("tensor",
                    lambda e, _, eb=eb, p=p, wa=wa, ua=ua: e.matmul(
                        Pb[6][0:96, :],
                        wa[:, (eb % 8) * 96:(eb % 8 + 1) * 96], ua,
                        start=(eb == 0 and p == 0),
                        stop=(eb == NEB - 1 and p == 2)),
                    waits=([tpl, tk_xpwl] if p == 0 else []),
                    inc=(p == 2))
            tk_xp_prev = tk_xp_last

        tdr = act(0, ar(A_DTR32, T)[0:64, :], Pb[6][0:64, :], AF.Copy,
                  [tk_xp_last])
        tbc = act(0, bmcm[64:96, :], Pb[6][64:96, :], AF.Copy, [tk_xp_last])
        tdrh = act(0, dtrh_sb, ar(A_DTR32, T)[0:64, :], AF.Copy, [tdr])
        tdrl = sc.emit("vector", lambda e, _: e.tensor_tensor(
            dtrl_sb, ar(A_DTR32, T)[0:64, :], dtrh_sb,
            ALU.subtract), waits=[tdrh], inc=True)
        tk_bm_dma = sc.dma(s_bm, bmcm[64:96, :], waits=[tbc])

        if debug == "xc" and c == 0:
            sc.dma(dbg_h[:, 0:NEB * T], ar(A_XC, NEB * T), waits=[tk_xc[-1]])
            sc.dma(dbg_h[:, NEB * T:NEB * T + T], bmcm, waits=[tbc])
            sc.dma(dbg_h[:, NEB * T + T:NEB * T + 2 * T], ar(A_DTR32, T),
                   waits=[tdr])

        DT, DTU = ar(A_DT, T), ar(A_DTU, T)
        tk_scan_dv = None
        tk_dt_prev = None
        tk_dtu_prev = None
        tk_dtp_evac = {0: None, 1: None}
        tk_slotfree = {0: None, 1: None}
        tk_scan2 = {0: None, 1: None}
        tk_zevac = dict(tk_evac)
        tk_zout = {0: None, 1: None}
        tk_dbu_last = None
        gi = 0
        gi2 = 0

        def emit_z(eb):
            bank = eb % 2
            tw1 = sc.dma(
                wshb[bank].rearrange("p (kd e) -> p kd e", e=128),
                s_inh.rearrange("(kd p) e -> p kd e", p=128)[
                    :, :, (16 + eb) * 128:(17 + eb) * 128],
                waits=[tk_wfree[bank]], ch="cw0")
            tw2 = sc.dma(
                wslb[bank].rearrange("p (kd e) -> p kd e", e=128),
                s_inl.rearrange("(kd p) e -> p kd e", p=128)[
                    :, :, (16 + eb) * 128:(17 + eb) * 128],
                waits=[tk_wfree[bank]], ch="cw1")
            tmm = None
            for p, (wa, ua) in enumerate(((wshb[bank], uTh), (wshb[bank], uTl),
                                          (wslb[bank], uTh))):
                for kd in range(KD):
                    tmm = sc.emit("tensor", lambda e, _, wa=wa, ua=ua, kd=kd,
                                  p=p, bank=bank: e.matmul(
                        Pb[bank][:], wa[:, kd * 128:(kd + 1) * 128],
                        ua[:, kd * T:(kd + 1) * T],
                        start=(p == 0 and kd == 0),
                        stop=(p == 2 and kd == KD - 1)),
                        waits=([tw1, tw2, tk_zevac[bank]]
                               if (p == 0 and kd == 0) else []),
                        inc=(p == 2 and kd == KD - 1))
            tk_wfree[bank] = tmm
            szb = ar(A_SZ + bank * 512, T)
            tsz = act(0, szb, Pb[bank][:], AF.Silu, [tmm, tk_zout[bank]],
                      bias=wbeta[:, 16 + eb:17 + eb])
            tk_zevac[bank] = tsz
            tk_zout[bank] = sc.dma(s_sz[eb * 128:(eb + 1) * 128, :], szb,
                                   waits=[tsz], ch="cp")
        cst = {"pend": None, "y": None, "pool": None, "burst": None}

        def flush_pend():
            p = cst["pend"]
            if p is None:
                return
            cst["pend"] = None
            gs, slot, dbuh, tcar, hc_pool, db_p, first, tkncm = p
            eng = "gpsimd" if hc_pool else "vector"
            thc = sc.emit(eng, lambda e, _, gs=gs, dbuh=dbuh:
                e.tensor_tensor(
                    dbuh, dbuh,
                    ar(A_NCM, SH * T)[:, gs * T:(gs + SG) * T], ALU.mult),
                waits=[tcar, tkncm], inc=True)
            if hc_pool:
                cst["pool"] = thc
            ysl = ar(A_Y, NEB * T)[:, db_p * T:(db_p + 1) * T]
            ty0 = sc.emit("vector", lambda e, _, ysl=ysl, dbuh=dbuh,
                          first=first: (e.tensor_copy(
                              ysl, dbuh[:, 0:T]) if first else
                              e.tensor_tensor(ysl, ysl, dbuh[:, 0:T],
                                              ALU.add)),
                waits=[thc], inc=True)
            ty1 = sc.emit("vector", lambda e, _, ysl=ysl,
                          dbuh=dbuh: e.tensor_tensor(
                    ysl, ysl, dbuh[:, T:2 * T], ALU.add),
                waits=[ty0], inc=True)
            tk_slotfree[slot] = ty1
            cst["y"] = ty1

        for half in range(S // SH):
            flush_pend()
            nbm = ar(A_NBM, SH * T)
            ncm = ar(A_NCM, SH * T)
            tk_nbm = sc.dma(
                nbm.rearrange("p (s t) -> p s t", t=T),
                s_bm[half * SH:(half + 1) * SH, :].partition_broadcast(128),
                waits=[tk_bm_dma, cst["burst"], tdrl])
            tk_ncm = sc.dma(
                ncm.rearrange("p (s t) -> p s t", t=T),
                s_bm[16 + half * SH:16 + (half + 1) * SH, :]
                .partition_broadcast(128),
                waits=[tk_bm_dma, cst["y"]], ch="cp")
            for db in range(NEB):
                bank = db % 2
                sp1 = ar(A_SP1, T)
                sp2 = ar(A_SP2, T)
                tmm = None
                for p, (wa, ua) in enumerate((
                        (dtw_hi, dtrh_sb), (dtw_hi, dtrl_sb),
                        (dtw_lo, dtrh_sb))):
                    tmm = sc.emit("tensor", lambda e, _, wa=wa, ua=ua, db=db,
                                  p=p, bank=bank: e.matmul(
                        Pb[4 + bank][:], wa[:, db * 128:(db + 1) * 128], ua,
                        start=(p == 0), stop=(p == 2)),
                        waits=([tdrl, tk_dtp_evac[bank]] if p == 0 else []),
                        inc=(p == 2))
                t1 = act(0, sp1, Pb[4 + bank][:], AF.Tanh,
                         [tmm, tk_dt_prev,
                          tk_half if (c == 0 and half == 0 and db == 0)
                          else None],
                         scale=halfc[:, 1:2], bias=dtb2col[:, db:db + 1])
                tk_dtp_evac[bank] = t1
                tdt = act(0, DT, sp1, AF.Ln, [t1, tk_dtu_prev],
                          scale=halfc[:, 0:1], bias=halfc[:, 0:1])
                tk_dt_prev = tdt
                tdtu = sc.emit("vector", lambda e, _, db=db:
                    e.scalar_tensor_tensor(
                        DTU, DT, -1.0,
                        ar(A_XC, NEB * T)[:, db * T:(db + 1) * T],
                        ALU.mult, ALU.mult),
                    waits=[tdt, tk_xc[db]], inc=True)
                tk_dtu_prev = tdtu
                for g in range(SH // SG):
                    s0 = half * SH + g * SG
                    gs = g * SG
                    slot = gi % 2
                    dbuh = ar(A_DBUH + slot * (SG * T), SG * T)
                    flush_pend()
                    tdbu = sc.emit("vector", lambda e, _, gs=gs, dbuh=dbuh:
                        e.tensor_tensor(
                            dbuh.rearrange("p (s t) -> p s t", t=T),
                            DTU.unsqueeze(1).broadcast_to([128, SG, T]),
                            ar(A_NBM, SH * T)[:, gs * T:(gs + SG) * T]
                            .rearrange("p (s t) -> p s t", t=T),
                            ALU.mult),
                        waits=[tdtu, tk_nbm, tk_slotfree[slot],
                               cst["burst"]], inc=True)
                    cst["pool"] = tdbu
                    tk_dbu_last = tdbu
                    tsc_ = None
                    for si in range(SG):
                        s = s0 + si
                        da_s = gi2 % 2
                        gi2 += 1
                        da = ar(A_DA if da_s == 0 else A_DA2, T)
                        tda = act(0, da, DT, AF.Exp,
                                  [tdt, tk_scan2[da_s]],
                                  scale=acol[:, db * S + s:db * S + s + 1])
                        hseg = dbuh[:, si * T:(si + 1) * T]
                        init = (0.0 if c == 0 else
                                carry[:, db * 16 + s:db * 16 + s + 1])
                        tsc_ = sc.emit("vector", lambda e, _, hseg=hseg,
                                       init=init, da=da: e.tensor_tensor_scan(
                            hseg, da, hseg, init, ALU.mult, ALU.add),
                            waits=[tda, cst["pool"] if si == 0 else None],
                            inc=True)
                        tk_scan2[da_s] = tsc_
                    cst["burst"] = tsc_
                    tcar = sc.emit("gpsimd", lambda e, _, db=db, s0=s0,
                                   dbuh=dbuh: e.tensor_copy(
                            carry[:, db * 16 + s0:db * 16 + s0 + SG],
                            dbuh.rearrange(
                                "p (s t) -> p s t", t=T)[:, :, T - 1]),
                        waits=[tsc_], inc=True)
                    cst["pend"] = (gs, slot, dbuh, tcar, False, db,
                                   half == 0 and g == 0, tk_ncm)
                    gi += 1
                if half < 2 and db % 2 == 0:
                    emit_z(half * 8 + db // 2)
        flush_pend()
        tk_scan_dv = cst["y"]

        if debug == "y" and c == 0:
            sc.dma(dbg_h[:, 0:NEB * T], ar(A_Y, NEB * T), waits=[tk_scan_dv])

        tk_evacD = {0: None, 1: None}
        tk_gate = []
        for eb in range(NEB):
            bank = eb % 2
            szb = ar(A_SZ + bank * 512, T)
            tsz = sc.dma(szb, s_sz[eb * 128:(eb + 1) * 128, :],
                         waits=[tk_evacD[bank]], ch="cp")
            ysl = ar(A_Y, NEB * T)[:, eb * T:(eb + 1) * T]
            xcs = ar(A_XC, NEB * T)[:, eb * T:(eb + 1) * T]
            tyt = sc.emit("vector", lambda e, _, ysl=ysl, xcs=xcs, eb=eb:
                e.scalar_tensor_tensor(
                    ysl, xcs, dcol[:, eb:eb + 1], ysl, ALU.mult, ALU.add),
                waits=[tsz, tk_scan_dv], inc=True)
            tyg = sc.emit("vector", lambda e, _, ysl=ysl, szb=szb:
                e.tensor_tensor(ysl, ysl, szb, ALU.mult),
                waits=[tyt], inc=True)
            tk_evacD[bank] = tyg
            tk_gate.append(tyg)

        if debug == "yg" and c == 0:
            sc.dma(dbg_h[:, 0:NEB * T], ar(A_Y, NEB * T), waits=[tk_gate[-1]])

        tk_op_mm = None
        for db in range(NEB):
            ysl = ar(A_Y, NEB * T)[:, db * T:(db + 1) * T]
            typh = act(0, yph_sb, ysl, AF.Copy,
                       [tk_gate[db], tk_gate[-1], tk_op_mm])
            typl = sc.emit("vector", lambda e, _, ysl=ysl: e.tensor_tensor(
                ypl_sb, ysl, yph_sb, ALU.subtract),
                waits=[typh], inc=True)
            bank = db % 2
            wsh, wsl = wshb[bank], wslb[bank]
            tw1 = sc.dma(wsh[:, 0:1024], s_outh[db * 128:(db + 1) * 128, :],
                         waits=[tk_wfree[bank]], ch="cw0")
            tw2 = sc.dma(wsl[:, 0:1024], s_outl[db * 128:(db + 1) * 128, :],
                         waits=[tk_wfree[bank]], ch="cw1")
            for p, (ya, wa) in enumerate((
                    (yph_sb, wshb[bank]), (ypl_sb, wshb[bank]),
                    (yph_sb, wslb[bank]))):
                for tt in range(NTT):
                    for eh in range(2):
                        tk_op_mm = sc.emit("tensor",
                            lambda e, _, ya=ya, wa=wa, tt=tt, eh=eh, p=p,
                            db=db: e.matmul(
                                Pb[tt * 2 + eh][:],
                                ya[:, tt * 128:(tt + 1) * 128],
                                wa[:, eh * 512:(eh + 1) * 512],
                                start=(db == 0 and p == 0),
                                stop=(db == NEB - 1 and p == 2)),
                            waits=([typl, tw1, tw2, tk_evacD[0], tk_evacD[1]]
                                   if (p == 0 and tt == 0 and eh == 0)
                                   else []),
                            inc=(p == 2 and tt == NTT - 1 and eh == 1))
            tk_wfree[bank] = tk_op_mm

        if debug == "m" and c == 0:
            for i in range(8):
                tcp = sc.emit("vector", lambda e, _, i=i: e.tensor_copy(
                    ar(A_NBM, 4096)[:, i * 512:(i + 1) * 512], Pb[i][:]),
                    waits=[tk_op_mm], inc=True)
            sc.dma(dbg_h[:, 0:4096], ar(A_NBM, 4096), waits=[tcp])

        tk_a = None
        for tt in range(NTT):
            tt2 = sc.emit("vector", lambda e, _, tt=tt: e.tensor_scalar(
                ar(A_T2, 1024), x_sb[:, tt * D:(tt + 1) * D],
                cols[:, 8 + tt:9 + tt], cols[:, 12 + tt:13 + tt],
                ALU.subtract, ALU.mult), waits=[tk_op_mm], inc=True)
            for eh in range(2):
                tk_a = sc.emit("vector", lambda e, _, tt=tt, eh=eh:
                    e.tensor_tensor(
                        ar(A_AB, NTT * D)[
                            :, tt * D + eh * 512:tt * D + (eh + 1) * 512],
                        ar(A_T2, 1024)[:, eh * 512:(eh + 1) * 512],
                        Pb[tt * 2 + eh][:], ALU.add), waits=[tt2], inc=True)
        if debug == "a" and c == 0:
            sc.dma(dbg_h[:, 0:NTT * D], ar(A_AB, NTT * D), waits=[tk_a])
        tsum = emit_ln_sums(
            lambda tt: ar(A_AB, NTT * D)[:, tt * D:(tt + 1) * D],
            lambda tt: [tk_a] if tt == 0 else [], trash=A_NBM)
        tRA = emit_ln_cols(tsum, 40, 44)
        tk_x1 = None
        for tt in range(NTT):
            tq = sc.emit("vector", lambda e, _, tt=tt: e.tensor_scalar(
                ar(A_NBM, D), ar(A_AB, NTT * D)[:, tt * D:(tt + 1) * D],
                cols[:, 40 + tt:41 + tt], cols[:, 44 + tt:45 + tt],
                ALU.subtract, ALU.mult), waits=[tRA], inc=True)
            tk_x1 = sc.emit("vector", lambda e, _, tt=tt: e.tensor_tensor(
                x_sb[:, tt * D:(tt + 1) * D], x_sb[:, tt * D:(tt + 1) * D],
                ar(A_NBM, D), ALU.add), waits=[tq], inc=True)

        if debug == "x1" and c == 0:
            tdmp = sc.dma(dbg_h[:, 0:NTT * D], x_sb, waits=[tk_x1])
            tk_x1 = sc.emit("vector", lambda e, _: e.engine_nop(),
                            waits=[tdmp], inc=True)

        tk_bfcL = sc.dma(ar(E_BFC, HID), s_bfcB, waits=[tk_bfcB, tk_x1])
        tk_bpjL = sc.dma(ar(E_BPJ, D), s_bpjB, waits=[tk_bpjB])
        tsum = emit_ln_sums(
            lambda tt: x_sb[:, tt * D:(tt + 1) * D],
            lambda tt: [tk_x1] if tt == 0 else [])
        tR3 = emit_ln_cols(tsum, 48, 52)

        tk_tpfree = None
        tk_fc_mm = {0: None, 1: None}
        tk_pj_mm = None
        tk_pjw = {0: None, 1: None}
        tk_f = None
        for tt in range(NTT):
            tq = sc.emit("vector", lambda e, _, tt=tt: e.tensor_scalar(
                ar(E_A, D), x_sb[:, tt * D:(tt + 1) * D],
                cols[:, 48 + tt:49 + tt], cols[:, 52 + tt:53 + tt],
                ALU.subtract, ALU.mult), waits=[tR3], inc=True)
            tq = emit_quant(
                ar(E_A, D), 32,
                ar(E_X2Q, NTT * 512, BF16)[:, tt * D:(tt + 1) * D],
                [tq], E_QA, E_QC)
            if debug == "ffn" and c == 0 and tt == 0:
                tq2_ = sc.dma(dbg_h.bitcast(BF16)[:, 16384:17408],
                              ar(E_X2Q, NTT * 512, BF16)[:, 0:D], waits=[tq])
                tq = sc.emit("vector", lambda e, _: e.engine_nop(),
                             waits=[tq2_], inc=True)
            tp = None
            for kd in range(KD):
                tp = sc.emit("tensor", lambda e, _, tt=tt, kd=kd: e.transpose(
                    Pb[3].bitcast(BF16)[:, kd * 128:(kd + 1) * 128],
                    ar(E_X2Q, NTT * 512, BF16)[
                        :, tt * D + kd * 128:tt * D + (kd + 1) * 128],
                    id_bf),
                    waits=[tq, tk_tpfree] if kd == 0 else [], inc=(kd == 7))
            tk_x2qT = sc.emit("vector", lambda e, _: e.tensor_copy(
                ar(E_X2QT, 512, BF16), Pb[3].bitcast(BF16)[:, 0:1024]),
                waits=[tp], inc=True)
            tk_tpfree = tk_x2qT

            tk_fcev = {0: None, 1: None}
            for nb in range(8):
                par = nb % 2
                bank = 4 + par
                wfcs = ar(E_WFC + par * 2048, 2048, BF16)
                twf = sc.dma(
                    wfcs.rearrange("p (kd n) -> p kd n", n=512),
                    s_fcq.rearrange("(kd p) n -> p kd n", p=128)[
                        :, :, nb * 512:(nb + 1) * 512],
                    waits=[tk_fc_mm[par], tk_pj_mm,
                           tk_x1 if (tt == 0 and nb <= 1) else None,
                           tk_prologue if (c == 0 and tt == 0 and nb == 0)
                           else None], ch="cw0")
                tmm = None
                for kd in range(KD):
                    tmm = sc.emit("tensor", lambda e, _, kd=kd, bank=bank,
                                  wfcs=wfcs: e.matmul(
                            Pb[bank][:],
                            ar(E_X2QT, 512, BF16)[:, kd * 128:(kd + 1) * 128],
                            wfcs[:, kd * 512:(kd + 1) * 512],
                            start=(kd == 0), stop=(kd == KD - 1)),
                        waits=([twf, tk_x2qT, tk_fcev[par]]
                               if kd == 0 else []),
                        inc=(kd == KD - 1))
                tk_fc_mm[par] = tmm
                te1 = sc.emit("vector", lambda e, _, nb=nb, bank=bank:
                    e.tensor_tensor(
                        ar(E_GB, T), Pb[bank][:],
                        ar(E_BFC, HID)[:, nb * 512:(nb + 1) * 512], ALU.add),
                    waits=[tmm, tk_bfcL], inc=True)
                if debug == "h0" and c == 0 and tt == 0 and nb == 0:
                    td_ = sc.dma(dbg_h[:, 0:T], ar(E_GB, T), waits=[te1])
                    td_ = sc.dma(dbg_h[:, T:T + HID], ar(E_BFC, HID),
                                 waits=[td_])
                    te1 = sc.emit("vector", lambda e, _: e.engine_nop(),
                                  waits=[td_], inc=True)
                te2 = act(0, ar(E_GB2, T), ar(E_GB, T), AF.Gelu, [te1])
                tq2 = emit_quant(
                    ar(E_GB2, T), 16,
                    ar(E_HQ, 2048, BF16)[:, nb * 512:(nb + 1) * 512],
                    [te2], E_QA, E_QC)
                tk_fcev[nb % 2] = tq2

            if debug == "hq0" and c == 0 and tt == 0:
                td_ = sc.dma(dbg_h.bitcast(BF16)[:, 0:HID],
                             ar(E_HQ, 2048, BF16),
                             waits=[tk_fcev[0], tk_fcev[1]])
                tk_fcev[1] = sc.emit("vector", lambda e, _: e.engine_nop(),
                                     waits=[td_], inc=True)
            tlast = tk_fcev[1]
            for rnd in range(4):
                tp = None
                for k in range(8):
                    hb = rnd * 8 + k
                    tp = sc.emit("tensor", lambda e, _, hb=hb, k=k:
                        e.transpose(
                            Pb[3].bitcast(BF16)[:, k * 128:(k + 1) * 128],
                            ar(E_HQ, 2048, BF16)[:, hb * 128:(hb + 1) * 128],
                            id_bf),
                        waits=[tk_fcev[0], tk_fcev[1], tlast]
                              if k == 0 else [], inc=(k == 7))
                tlast = sc.emit("vector", lambda e, _, rnd=rnd: e.tensor_copy(
                    ar(E_HQT, 2048, BF16)[:, rnd * 1024:(rnd + 1) * 1024],
                    Pb[3].bitcast(BF16)[:, 0:1024]), waits=[tp], inc=True)
            tk_tpfree = tlast

            for hbp in range(16):
                par = hbp % 2
                pslab = ar(E_WPJ + par * 1024, 1024, BF16)
                twp = sc.dma(
                    pslab.rearrange("p (two e) -> p two e", e=1024),
                    s_pjq[hbp * 256:(hbp + 1) * 256, :].rearrange(
                        "(two p) e -> p two e", p=128),
                    waits=[tk_pjw[par],
                           tk_fc_mm[0] if hbp == 0 else None,
                           tk_fc_mm[1] if hbp == 0 else None], ch="cw1")
                for two in range(2):
                    hb = hbp * 2 + two
                    for eh in range(2):
                        tk_pj_mm = sc.emit("tensor",
                            lambda e, _, hb=hb, eh=eh, two=two, pslab=pslab,
                            tlast=tlast: e.matmul(
                                Pb[6 + eh][:],
                                ar(E_HQT, 2048, BF16)[
                                    :, hb * 128:(hb + 1) * 128],
                                pslab[:, two * 1024 + eh * 512:
                                      two * 1024 + (eh + 1) * 512],
                                start=(hb == 0), stop=(hb == 31)),
                            waits=([twp, tlast] if (two == 0 and eh == 0)
                                   else []),
                            inc=(two == 1 and eh == 1))
                tk_pjw[par] = tk_pj_mm
            for eh in range(2):
                tk_f = sc.emit("vector", lambda e, _, tt=tt, eh=eh:
                    e.tensor_tensor(
                        ar(E_F, NTT * D)[
                            :, tt * D + eh * 512:tt * D + (eh + 1) * 512],
                        Pb[6 + eh][:],
                        ar(E_BPJ, D)[:, eh * 512:(eh + 1) * 512], ALU.add),
                    waits=[tk_pj_mm, tk_bpjL], inc=True)

        if debug == "wq" and c == 0:
            sc.dma(dbg_h.bitcast(BF16)[:, 0:HID], s_fcq[0:128, :],
                   waits=[tk_prologue])
            sc.dma(dbg_h.bitcast(BF16)[:, HID:HID + D], s_pjq[0:128, :],
                   waits=[tk_prologue])
        if debug == "ffn" and c == 0:
            sc.dma(dbg_h[:, 0:NTT * D], ar(E_F, NTT * D), waits=[tk_f])
            sc.dma(dbg_h.bitcast(BF16)[:, 2 * NTT * D:3 * NTT * D],
                   ar(E_X2Q, NTT * 512, BF16), waits=[tk_f])
            sc.dma(dbg_h.bitcast(BF16)[:, 3 * NTT * D:3 * NTT * D + 4096],
                   ar(E_HQ, 2048, BF16), waits=[tk_f])
        tsum = emit_ln_sums(
            lambda tt: ar(E_F, NTT * D)[:, tt * D:(tt + 1) * D],
            lambda tt: [tk_f] if tt == 0 else [])
        tR4 = emit_ln_cols(tsum, 56, 60)
        tk_x2 = None
        for tt in range(NTT):
            tq = sc.emit("vector", lambda e, _, tt=tt: e.tensor_scalar(
                ar(E_A, D), ar(E_F, NTT * D)[:, tt * D:(tt + 1) * D],
                cols[:, 56 + tt:57 + tt], cols[:, 60 + tt:61 + tt],
                ALU.subtract, ALU.mult), waits=[tR4], inc=True)
            tk_x2 = sc.emit("vector", lambda e, _, tt=tt: e.tensor_tensor(
                x_sb[:, tt * D:(tt + 1) * D], x_sb[:, tt * D:(tt + 1) * D],
                ar(E_A, D), ALU.add), waits=[tq], inc=True)

        tk_out_dma = sc.dma(
            out_h[r0:r0 + T, :].rearrange("(tt p) d -> p tt d", p=128),
            x_sb.rearrange("p (tt d) -> p tt d", d=D),
            waits=[tk_x2])

    finals = {ch: sc.cnt[ch] for ch in DMA_CH}

    with contextlib.ExitStack() as es:
        e = es.enter_context
        sems = {eng: e(nc.semaphore(f"sem_{eng}")) for eng in ENGS}
        dma_sems = {ch: e(nc.semaphore(f"sem_{ch}")) for ch in DMA_CH}
        block = e(nc.Block())

        @block.tensor
        def _(h):
            sc.replay("tensor", h, sems, dma_sems)

        @block.vector
        def _(h):
            sc.replay("vector", h, sems, dma_sems)

        @block.scalar
        def _(h):
            sc.replay("scalar", h, sems, dma_sems)

        @block.gpsimd
        def _(h):
            sc.replay("gpsimd", h, sems, dma_sems)

        @block.sync
        def _(h):
            sc.replay("sync", h, sems, dma_sems)
            for ch in DMA_CH:
                if finals[ch]:
                    h.wait_ge(dma_sems[ch], finals[ch])

    return nc


_CACHE = {}


def _get_nc():
    if "nc" not in _CACHE:
        _CACHE["nc"] = build_program()
    return _CACHE["nc"]


def make_inputs(L0_attn_x, L0_attn_ln_gamma, L0_attn_ln_beta, L0_attn_in_proj_w,
                L0_attn_conv_w, L0_attn_conv_b, L0_attn_x_proj_w,
                L0_attn_dt_proj_w, L0_attn_dt_proj_b, L0_attn_A_log, L0_attn_D,
                L0_attn_out_proj_w, L0_ffn_w_fc, L0_ffn_b_fc, L0_ffn_w_proj,
                L0_ffn_b_proj):
    import ml_dtypes

    def np32(a):
        return np.ascontiguousarray(np.asarray(a), dtype=np.float32)

    x = np32(L0_attn_x)

    def colsify(v, nb):
        return np.ascontiguousarray(np32(v).reshape(nb, 128).T)

    common = {
        "gammaC": colsify(L0_attn_ln_gamma, KD),
        "betaC": colsify(L0_attn_ln_beta, KD),
        "winT": np.ascontiguousarray(np32(L0_attn_in_proj_w).T),
        "cwC": np.ascontiguousarray(
            np32(L0_attn_conv_w).reshape(NEB, 128, 4).transpose(1, 0, 2)
            .reshape(128, NEB * 4)),
        "cbC": colsify(L0_attn_conv_b, NEB),
        "xpwT": np.ascontiguousarray(np32(L0_attn_x_proj_w).T),
        "dtwT": np.ascontiguousarray(np32(L0_attn_dt_proj_w).T),
        "dtbC": colsify(L0_attn_dt_proj_b, NEB),
        "alogC": np.ascontiguousarray(
            np32(L0_attn_A_log).reshape(NEB, 128, S).transpose(1, 0, 2)
            .reshape(128, NEB * S)),
        "dC": colsify(L0_attn_D, NEB),
        "woutT": np.ascontiguousarray(np32(L0_attn_out_proj_w).T),
        "wfc": np32(L0_ffn_w_fc),
        "bfc": np.ascontiguousarray(np32(L0_ffn_b_fc).reshape(1, HID)),
        "wproj": np32(L0_ffn_w_proj),
        "bproj": np.ascontiguousarray(np32(L0_ffn_b_proj).reshape(1, D)),
        "idf32": np.eye(128, dtype=np.float32),
        "idbf": np.eye(128, dtype=ml_dtypes.bfloat16),
        "onesf": np.ones((1, 128), dtype=np.float32),
    }
    return [dict(common, x=np.ascontiguousarray(x[b])) for b in range(8)]


def kernel(**inputs):
    in_maps = make_inputs(**inputs)
    nc = _get_nc()
    res = run_bass_kernel_spmd(nc, in_maps, core_ids=list(range(8)))
    return np.stack([np.asarray(r["out"], dtype=np.float32)
                     for r in res.results], axis=0)
